# version 19
# baseline (speedup 1.0000x reference)
import numpy as np
import contextlib
import concourse.bass as bass
import concourse.mybir as mybir
from concourse.bass_utils import run_bass_kernel_spmd

F32 = mybir.dt.float32
BF16 = mybir.dt.bfloat16
U16 = mybir.dt.uint16
U32 = mybir.dt.uint32
I32 = mybir.dt.int32
AL = mybir.AluOpType
AF = mybir.ActivationFunctionType
AX = mybir.AxisListType.X

NCORE = 8
TPC = 2048
TB = 4096
D = 1024
E = 2048
EH = 1024
NBLK = 8
N = 16
R = 64
FF = 4096
CAP = 2560
NE = 8
QT = 512
NQ = TB // QT
NT = TPC // 128
EPS = 1e-5


class Prog:

    ENGINES = ("tensor", "vector", "scalar", "gpsimd", "sync")
    QPOOL = 10

    def __init__(self):
        self.stream = {e: [] for e in self.ENGINES}
        self.cnt = {}
        self.events = {}
        self.readers = {}
        self.seen = {e: {} for e in self.ENGINES}
        self.groups = {}
        self.rr = {}
        self.semcum = {}
        self.dma_sems = set()

    def _qsem(self, q, i):
        name = f"{q}_q{i}"
        self.dma_sems.add(name)
        return name

    def _open(self, q):
        g = self.groups.get(q)
        if g is not None and not g["closed"]:
            return g
        i = self.rr.get(q, 0)
        self.rr[q] = i + 1
        sem = self._qsem(q, i % self.QPOOL)
        base = self.semcum.get(sem, 0)
        if base > 0:
            self.gseq = getattr(self, "gseq", 0) + 1
            self.stream[q].append((None, [(sem, base)], None, self.gseq))
            self.seen[q][sem] = max(self.seen[q].get(sem, 0), base)
        g = {"sem": sem, "base": base, "n": 0, "closed": False}
        self.groups[q] = g
        return g

    def _close(self, g):
        if not g["closed"]:
            g["closed"] = True
            g["total"] = g["base"] + 16 * g["n"]
            self.semcum[g["sem"]] = g["total"]

    def emit(self, eng, fn, reads=(), writes=(), dma=False):
        import sys as _sys
        f = _sys._getframe(1)
        tag = f"{f.f_lineno}"
        if f.f_code.co_name in ("LD", "ST", "MM", "MMC", "SCOPY", "ACT", "COPY", "TT",
                                "TS", "STT", "RED", "MEMSET", "CC", "FENCE", "TRANS", "VOP"):
            f2 = f.f_back
            tag = f"{f.f_code.co_name}@{f2.f_lineno}"
        self.tags = getattr(self, "tags", [])
        waits = {}

        def need(ev):
            if ev is None:
                return
            if isinstance(ev, dict):
                self._close(ev)
                s, v = ev["sem"], ev["total"]
            else:
                s, v = ev
            if self.seen[eng].get(s, 0) >= v:
                return
            if waits.get(s, 0) < v:
                waits[s] = v

        for k in reads:
            need(self.events.get(k))
        for k in writes:
            need(self.events.get(k))
            for ev in self.readers.get(k, ()):
                need(ev)
        if dma:
            g = self._open(eng)
            g["n"] += 1
            ev = g
            inc = (g["sem"], 16)
        else:
            sem = eng
            self.cnt[sem] = self.cnt.get(sem, 0) + 1
            ev = (sem, self.cnt[sem])
            inc = (sem, 1)
        for k in reads:
            self.readers.setdefault(k, []).append(ev)
        for k in writes:
            self.events[k] = ev
            self.readers[k] = []
        for s, v in waits.items():
            self.seen[eng][s] = max(self.seen[eng].get(s, 0), v)
        self.gseq = getattr(self, "gseq", 0) + 1
        self.tags.append((self.gseq, eng, tag))
        self.stream[eng].append((fn, sorted(waits.items()), inc, self.gseq))

    def finalize_outputs(self):
        for q in ("gpsimd", "sync", "scalar"):
            g = self.groups.get(q)
            if g is not None:
                self._close(g)
            for i in range(self.QPOOL):
                sem = f"{q}_q{i}"
                tot = self.semcum.get(sem, 0)
                if tot > 0 and self.seen[q].get(sem, 0) < tot:
                    self.gseq = getattr(self, "gseq", 0) + 1
                    self.stream[q].append((None, [(sem, tot)], None, self.gseq))
                    self.seen[q][sem] = tot

    limit = None

    def replay(self, eng_name, e, sems):
        for fn, waits, inc, seq in self.stream[eng_name]:
            if self.limit is not None and seq > self.limit:
                continue
            for s, v in waits:
                e.wait_ge(sems[s], v)
            if fn is None:
                continue
            instr = fn(e)
            if inc is not None and instr is not None:
                instr.then_inc(sems[inc[0]], inc[1])


def build_nc(limit=None):
    nc = bass.Bass("TRN2", target_bir_lowering=False)

    def ein(name, shape, dt=F32):
        return nc.dram_tensor(name, shape, dt, kind="ExternalInput")

    xrows = ein("xrows", [TPC, D])
    win_my = ein("win_my", [D, 2048])
    bin_my = ein("bin_my", [16, 128])
    bdt_my = ein("bdt_my", [NBLK, 128])
    wdbc_my = ein("wdbc_my", [EH, 96])
    rbc_bias = ein("rbc_bias", [96, 1])
    wdt_my = ein("wdt_my", [R, EH])
    alog_my = ein("alog_my", [EH, N])
    d_my = ein("d_my", [NBLK, 128])
    wout_i = ein("wout_i", [E, D])
    bout_i = ein("bout_i", [NBLK, 128])
    lng_i = ein("lng_i", [NBLK, 128])
    lnb_i = ein("lnb_i", [NBLK, 128])
    gwT = ein("gwT", [D, NE])
    gb_i = ein("gb_i", [NE, 1])
    w1e = ein("w1e", [D, FF])
    w2e = ein("w2e", [FF, D])
    b1e = ein("b1e", [32, 128])
    b2e = ein("b2e", [NBLK, 128])
    prefmask = ein("prefmask", [NE, 1])
    myexp_i = ein("myexp_i", [128, 1])
    bases_i = ein("bases_i", [1, 8], I32)
    out_rows = nc.dram_tensor("out_rows", [TPC, D], F32, kind="ExternalOutput")

    dr = lambda name, shape, dt: nc.dram_tensor(name, shape, dt)
    ag1z = dr("ag1z", [D, TPC], BF16)
    ag1zout = dr("ag1zout", [NCORE * D, TPC], BF16)
    ag1x = dr("ag1x", [D, TPC], BF16)
    ag1xout = dr("ag1xout", [NCORE * D, TPC], BF16)
    cntin = dr("cntin", [1, NE], F32)
    cntout = dr("cntout", [NCORE, NE], F32)
    metain = dr("metain", [4, TPC], F32)
    metaout = dr("metaout", [NCORE * 4, TPC], F32)
    ag3in = dr("ag3in", [NQ * 96, QT], F32)
    ag3out = dr("ag3out", [NQ * NCORE * 96, QT], F32)
    ag4in = dr("ag4in", [D, CAP], BF16)
    ag4out = dr("ag4out", [NCORE * D, CAP], BF16)
    ag5in = dr("ag5in", [NQ * EH, QT], BF16)
    ag5out = dr("ag5out", [NQ * NCORE * EH, QT], BF16)
    u_sp = dr("u_sp", [EH, TB], BF16)
    gate_sp = dr("gate_sp", [EH, TB], BF16)
    lns_sp = dr("lns_sp", [EH, TB], BF16)
    winsp = dr("winsp", [E, D], BF16)
    w1bf = dr("w1bf", [FF, D], BF16)
    w2bf = dr("w2bf", [D, FF], BF16)
    woutbf = dr("woutbf", [D, E], BF16)
    xs_sp = dr("xs_sp", [D, CAP], BF16)
    myx2 = dr("myx2", [2 * EH, TPC], BF16)
    lv_hbm = dr("lv_hbm", [1, CAP], BF16)
    mrow_hbm = dr("mrow_hbm", [3, TPC], F32)

    P = Prog()

    import contextlib
    with contextlib.ExitStack() as _stack:
        _E = _stack.enter_context
        _stack.enter_context(nc.allow_non_contiguous_dma(
            reason="small transposed parameter loads / column stores"))
        onesf = _E(nc.sbuf_tensor("onesf", [128, 8], F32))
        ident_f = _E(nc.sbuf_tensor("ident_f", [128, 128], F32))
        tri_f = _E(nc.sbuf_tensor("tri_f", [128, 128], F32))
        ones1b = _E(nc.sbuf_tensor("ones1b", [1, 128], BF16))
        nones1b = _E(nc.sbuf_tensor("nones1b", [1, 128], BF16))
        ones1f = _E(nc.sbuf_tensor("ones1f", [1, 128], F32))
        iota8 = _E(nc.sbuf_tensor("iota8", [128, 8], F32))
        consts = _E(nc.sbuf_tensor("consts", [128, 64], F32))
        b1_s = _E(nc.sbuf_tensor("b1_s", [128, 32], F32))
        b2_s = _E(nc.sbuf_tensor("b2_s", [128, 8], F32))
        gwT_f = _E(nc.sbuf_tensor("gwT_f", [128, 64], F32))
        gb_s = _E(nc.sbuf_tensor("gb_s", [8, 1], F32))
        rbcb_s = _E(nc.sbuf_tensor("rbcb_s", [96, 1], F32))
        prefm_s = _E(nc.sbuf_tensor("prefm_s", [8, 1], F32))
        myexp_s = _E(nc.sbuf_tensor("myexp_s", [128, 1], F32))
        iotar = _E(nc.sbuf_tensor("iotar", [128, 128], F32))
        onescb = _E(nc.sbuf_tensor("onescb", [128, 2], BF16))
        bases_s = _E(nc.sbuf_tensor("bases_s", [1, 8], I32))
        nega = _E(nc.sbuf_tensor("nega", [128, NBLK * N], F32))
        fencebuf = _E(nc.sbuf_tensor("fencebuf", [1, 4], F32))
        cvals = _E(nc.sbuf_tensor("cvals", [128, 8], F32))
        ARBIG = _E(nc.sbuf_tensor("ARBIG", [128, 10240], F32))
        ARB2 = _E(nc.sbuf_tensor("ARB2", [128, 16384], BF16))
        AR2 = _E(nc.sbuf_tensor("AR2", [128, 4096], F32))
        ARX = _E(nc.sbuf_tensor("ARX", [128, 4096], BF16))
        ARW = _E(nc.sbuf_tensor("ARW", [128, 8192], BF16))
        ARE = _E(nc.sbuf_tensor("ARE", [128, 2048], F32))
        bmcm_b = _E(nc.sbuf_tensor("bmcm_b", [32, TB], BF16))
        castA = _E(nc.sbuf_tensor("castA", [128, 1024], F32))
        castB = _E(nc.sbuf_tensor("castB", [128, 1024], F32))
        moepk = _E(nc.sbuf_tensor("moepk", [16, 4096], F32))
        tokid = _E(nc.sbuf_tensor("tokid", [16, 1024], F32))
        comp16 = _E(nc.sbuf_tensor("comp16", [16, CAP // 16], U16))
        wsc = _E(nc.sbuf_tensor("wsc", [64, 2048], BF16))
        cidx = _E(nc.sbuf_tensor("cidx", [128, CAP // 16], U16))
        gidx16 = _E(nc.sbuf_tensor("gidx16", [128, 384], U16))
        selmat = _E(nc.sbuf_tensor("selmat", [32, 4096], BF16))
        w1t = _E(nc.sbuf_tensor("w1t", [128, 2048], BF16))
        wot = _E(nc.sbuf_tensor("wot", [128, 2048], BF16))
        lns_t = _E(nc.sbuf_tensor("lns_t", [128, 2 * QT], BF16))
        u_t = _E(nc.sbuf_tensor("u_t", [128, 2 * QT], BF16))
        gate_t = _E(nc.sbuf_tensor("gate_t", [128, 2 * QT], BF16))
        ndu = _E(nc.sbuf_tensor("ndu", [128, QT], BF16))
        b_t = _E(nc.sbuf_tensor("b_t", [128, QT], BF16))
        ygt = _E(nc.sbuf_tensor("ygt", [128, 2 * QT], BF16))
        rstat = _E(nc.sbuf_tensor("rstat", [128, 128], F32))
        carryc = _E(nc.sbuf_tensor("carryc", [128, NBLK * N], F32))
        lg_tm = _E(nc.sbuf_tensor("lg_tm", [128, NT * 8], F32))
        oh_tm = _E(nc.sbuf_tensor("oh_tm", [128, NT * 8], F32))
        pos_tm = _E(nc.sbuf_tensor("pos_tm", [128, NT * 8], F32))
        rcols = _E(nc.sbuf_tensor("rcols", [128, NT * 10], F32))
        base8 = _E(nc.sbuf_tensor("base8", [128, 8], F32))
        cnt_s = _E(nc.sbuf_tensor("cnt_s", [8, 8], F32))
        rtmp8 = _E(nc.sbuf_tensor("rtmp8", [128, 8], F32))
        stile = _E(nc.sbuf_tensor("stile", [128, 512], F32))
        resid = _E(nc.sbuf_tensor("resid", [128, 512], F32))
        sgt = _E(nc.sbuf_tensor("sgt", [128, 512], F32))
        x2stage = _E(nc.sbuf_tensor("x2stage", [128, 1024], BF16))
        xnstage = _E(nc.sbuf_tensor("xnstage", [128, 1024], BF16))
        psA = _E(nc.psum_tensor("psA", [128, 512], F32))
        psB = _E(nc.psum_tensor("psB", [128, 512], F32))
        psC = _E(nc.psum_tensor("psC", [128, 512], F32))
        ps96 = _E(nc.psum_tensor("ps96", [96, 512], F32))
        psT = _E(nc.psum_tensor("psT", [128, 128], F32))
        psT2 = _E(nc.psum_tensor("psT2", [128, 128], F32))
        psY = _E(nc.psum_tensor("psY", [128, 512], F32))
        ps8 = _E(nc.psum_tensor("ps8", [128, 8], F32))
        s_tensor = _E(nc.semaphore("s_tensor"))
        s_vector = _E(nc.semaphore("s_vector"))
        s_scalar = _E(nc.semaphore("s_scalar"))
        s_gpsimd = _E(nc.semaphore("s_gpsimd"))
        s_sync = _E(nc.semaphore("s_sync"))
        qsems = {}
        for q in ("sync", "gpsimd", "scalar"):
            for i in range(Prog.QPOOL):
                qsems[f"{q}_q{i}"] = _E(nc.semaphore(f"{q}_q{i}"))
        block = _E(nc.Block())
        sems = {
            "tensor": s_tensor, "vector": s_vector, "scalar": s_scalar,
            "gpsimd": s_gpsimd, "sync": s_sync,
        }
        sems.update(qsems)

        xtA = ARBIG[:, 0:1024]
        xtB = ARBIG[:, 1024:2048]
        sqA = ARBIG[:, 2048:3072]
        sqB = ARBIG[:, 5120:6144]
        lg_fm = ARBIG[0:8, 3072:5120]
        rbct_v = ARBIG[0:96, 0:4096]
        slab_v = ARBIG[:, 0:10240].bitcast(BF16)
        bbc_v = ARBIG[:, 0:4096].bitcast(BF16)
        cbc_v = ARBIG[:, 4096:8192].bitcast(BF16)
        wpar_v = ARBIG.bitcast(BF16)[:, 16384:18432]
        diagD_v = ARBIG.bitcast(BF16)[:, 18432:19456]
        identb_v = ARBIG.bitcast(BF16)[:, 19456:19584]
        slabx_v = ARB2
        h1_v = ARB2
        gfb_v = ARB2
        a_v = AR2[:, 0:1024]
        art = [AR2[:, 1024:1536], AR2[:, 1536:2048],
               AR2[:, 3584:4096]]
        yacc_v = AR2[:, 3072:3584]
        h_v = AR2.bitcast(BF16)[:, 4096:5120]
        ymul_v = AR2.bitcast(BF16)[:, 5120:6144]
        gvm_bc = AR2.bitcast(BF16)
        ysl_v = AR2[:, 0:4096].bitcast(BF16)
        xsf_v = ARX[:, 0:CAP]
        xssl_v = ARX
        gacc_v = ARX[:, 0:2048]
        gtmp_v = ARX[:, 2048:4096]
        xrow_v = ARX.bitcast(F32)[:, 0:1024]
        x2sl_v = ARW[:, 0:4096]
        w2t_v = ARW[:, 4096:8192]
        ottm_v = ARW.bitcast(F32)
        uev_v = ARE.bitcast(BF16)
        rbcev_v = castB[0:96, 0:1024]
        ffev_v = sgt.bitcast(BF16)[:, 0:512]
        rbf2_v = wsc[0:64, 1024:1536]
        gvrow = moepk.bitcast(BF16)[0:1, 4096:6144]
        dsgb = [stile.bitcast(BF16)[:, 0:512],
                resid.bitcast(BF16)[:, 0:512]]
        midx_v = moepk[:, 0:1024]
        mslot_v = moepk[:, 1024:2048]
        mkeep_v = moepk[:, 2048:3072]
        mwork_v = moepk[:, 3072:4096]

        def LD(dst, src, key, reads=(), eng="sync"):
            P.emit(eng, lambda e, d=dst, s=src: e.dma_start(out=d, in_=s),
                   reads=tuple(reads), writes=(key,), dma=True)

        def ST(dst, src, key, reads, eng="gpsimd"):
            P.emit(eng, lambda e, d=dst, s=src: e.dma_start(out=d, in_=s),
                   reads=tuple(reads), writes=(key,), dma=True)

        def MM(outp, lhsT, rhs, reads, writes, start, stop):
            P.emit("tensor",
                   lambda e, o=outp, l=lhsT, r=rhs, s1=start, s2=stop:
                   e.matmul(o, l, r, start=s1, stop=s2),
                   reads=tuple(reads), writes=tuple(writes))

        def MMC(outp, pairs, reads, writes, first=True, last=True):
            def fn(e, o=outp, ps=tuple(pairs), first=first, last=last):
                n = len(ps)
                for i, (l, r) in enumerate(ps):
                    mm = e.matmul(o, l, r, start=(first and i == 0),
                                  stop=(last and i == n - 1))
                return mm
            P.emit("tensor", fn, reads=tuple(reads), writes=tuple(writes))

        def SCOPY(outp, inp, reads, writes):
            P.emit("scalar", lambda e, o=outp, i=inp: e.copy(out=o, in_=i),
                   reads=tuple(reads), writes=tuple(writes))

        def TRANS(outp, inp, reads, writes):
            P.emit("tensor",
                   lambda e, o=outp, i=inp: e.transpose(o, i, ident_f[:, :]),
                   reads=tuple(reads) + ("ident_f",), writes=tuple(writes))

        def ACT(outp, inp, func, reads, writes, bias=0.0, scale=1.0):
            P.emit("scalar",
                   lambda e, o=outp, i=inp, f=func, b=bias, s=scale:
                   e.activation(out=o, in_=i, func=f, bias=b, scale=s),
                   reads=tuple(reads), writes=tuple(writes))

        def COPY(outp, inp, reads, writes, eng="vector"):
            P.emit(eng, lambda e, o=outp, i=inp: e.tensor_copy(out=o, in_=i),
                   reads=tuple(reads), writes=tuple(writes))

        def TT(outp, in0, in1, op, reads, writes, eng="vector"):
            P.emit(eng, lambda e, o=outp, a=in0, b=in1, p=op:
                   e.tensor_tensor(out=o, in0=a, in1=b, op=p),
                   reads=tuple(reads), writes=tuple(writes))

        def TS(outp, in0, s1, s2, op0, op1, reads, writes, eng="vector"):
            if s2 is None:
                op1 = AL.bypass
            P.emit(eng, lambda e, o=outp, a=in0, x=s1, y=s2, p=op0, q=op1:
                   e.tensor_scalar(out=o, in0=a, scalar1=x, scalar2=y, op0=p, op1=q),
                   reads=tuple(reads), writes=tuple(writes))

        def STT(outp, in0, scal, in1, op0, op1, reads, writes, eng="vector"):
            P.emit(eng, lambda e, o=outp, a=in0, s=scal, b=in1, p=op0, q=op1:
                   e.scalar_tensor_tensor(out=o, in0=a, scalar=s, in1=b, op0=p, op1=q),
                   reads=tuple(reads), writes=tuple(writes))

        def RED(outp, inp, op, reads, writes, negate=False):
            P.emit("vector", lambda e, o=outp, i=inp, p=op, n=negate:
                   e.tensor_reduce(o, i, axis=AX, op=p, negate=n),
                   reads=tuple(reads), writes=tuple(writes))

        def MEMSET(ap, val, writes, eng="vector"):
            P.emit(eng, lambda e, a=ap, v=val: e.memset(a, v), reads=(), writes=tuple(writes))

        def CC(kind, ins_ap, outs_ap, reads, writes):
            P.emit("gpsimd", lambda e, i=ins_ap, u=outs_ap, k=kind:
                   e.collective_compute(k, AL.bypass, replica_groups=[list(range(NCORE))],
                                        ins=[i], outs=[u]),
                   reads=tuple(reads), writes=tuple(writes))
            v = P.cnt["gpsimd"]
            P.gseq = getattr(P, "gseq", 0) + 1
            P.stream["gpsimd"].append((None, [("gpsimd", v)], None, P.gseq))
            P.seen["gpsimd"]["gpsimd"] = max(P.seen["gpsimd"].get("gpsimd", 0), v)

        def FENCE(new_keys, old_keys):
            if isinstance(new_keys, str):
                new_keys = (new_keys,)
            P.emit("vector", lambda e: e.memset(fencebuf[:, :], 0.0),
                   reads=(), writes=tuple(new_keys) + tuple(old_keys) + ("fencebuf",))

        _reg_cache = {}

        def dyn_ld(dst, handle, base_idx, const_off, apdims, key, reads=()):
            def fn(e, d=dst, h=handle, bi=base_idx, co=const_off, ap=apdims):
                rk = id(e)
                if rk not in _reg_cache:
                    _reg_cache[rk] = e.alloc_register("dynoff")
                r = _reg_cache[rk]
                e.reg_load(r, bases_s[0:1, bi:bi + 1])
                e.reg_add(r, r, co)
                try:
                    return e.dma_start(out=d, in_=bass.AP(h, r, ap))
                except Exception as ex:
                    print("DYN FAIL bi=", bi, "co=", co, ap, d.offset, ex)
                    raise
            P.emit("sync", fn, reads=tuple(reads) + ("bases_s",), writes=(key,), dma=True)

        MEMSET(onesf[:, :], 1.0, ("onesf",))
        MEMSET(cvals[:, 0:1], EPS, ("cvals",))
        MEMSET(cvals[:, 1:2], -0.5, ("cvals",))
        MEMSET(cvals[:, 2:3], -1.0, ("cvals",))
        MEMSET(cvals[:, 3:4], 1.0 / D, ("cvals",))
        MEMSET(cvals[:, 4:5], float(CAP), ("cvals",))
        MEMSET(ones1b[:, :], 1.0, ("ones1b",))
        MEMSET(nones1b[:, :], -1.0, ("nones1b",))
        MEMSET(ones1f[:, :], 1.0, ("ones1f",))
        MEMSET(carryc[:, :], 0.0, ("carryc",))
        P.emit("gpsimd", lambda e: e.affine_select(
            ident_f[:, :], onesf[:, 0:1].broadcast_to((128, 128)), pattern=[[1, 128]],
            base=0, channel_multiplier=-1, compare_op=AL.is_equal, fill=0.0),
            reads=("onesf",), writes=("ident_f",))
        P.emit("gpsimd", lambda e: e.affine_select(
            tri_f[:, :], onesf[:, 0:1].broadcast_to((128, 128)), pattern=[[1, 128]],
            base=0, channel_multiplier=-1, compare_op=AL.is_ge, fill=0.0),
            reads=("onesf",), writes=("tri_f",))
        P.emit("gpsimd", lambda e: e.iota(
            iota8[:, :], pattern=[[1, 8]], base=0, channel_multiplier=0,
            allow_small_or_imprecise_dtypes=True), reads=(), writes=("iota8",))
        P.emit("gpsimd", lambda e: e.iota(
            tokid[:, :], pattern=[[16, 1024]], base=0, channel_multiplier=1,
            allow_small_or_imprecise_dtypes=True), reads=(), writes=("tokid",))
        MEMSET(ARB2[0:32, 0:4096], 1.0, ("h1",))
        MEMSET(onescb[:, :], 1.0, ("onescb",))
        P.emit("gpsimd", lambda e: e.iota(
            iotar[:, :], pattern=[[1, 128]], base=0, channel_multiplier=0,
            allow_small_or_imprecise_dtypes=True), reads=(), writes=("iotar",))
        P.emit("gpsimd", lambda e: e.affine_select(
            selmat[:, :], ARB2[0:32, 0:4096], pattern=[[1, 32], [0, 128]], base=0,
            channel_multiplier=-1, compare_op=AL.is_equal, fill=0.0),
            reads=("h1",), writes=("selmat",))

        LD(consts[:, 0:16], bin_my.ap().transpose([1, 0]), "consts", eng="gpsimd")
        LD(consts[:, 56:64], bdt_my.ap().transpose([1, 0]), "consts", eng="gpsimd")
        LD(consts[:, 24:32], bout_i.ap().transpose([1, 0]), "consts", eng="gpsimd")
        LD(consts[:, 32:40], d_my.ap().transpose([1, 0]), "consts", eng="gpsimd")
        LD(consts[:, 40:48], lng_i.ap().transpose([1, 0]), "consts", eng="gpsimd")
        LD(consts[:, 48:56], lnb_i.ap().transpose([1, 0]), "consts", eng="gpsimd")
        TS(consts[:, 16:24], consts[:, 56:64], -1.0, None, AL.mult, AL.add,
           ("consts",), ("consts",))
        LD(b1_s[:, :], b1e.ap().transpose([1, 0]), "b1_s", eng="gpsimd")
        LD(b2_s[:, :], b2e.ap().transpose([1, 0]), "b2_s", eng="gpsimd")
        LD(gb_s[:, :], gb_i.ap(), "gb_s")
        LD(rbcb_s[:, :], rbc_bias.ap(), "rbcb_s")
        LD(prefm_s[:, :], prefmask.ap(), "prefm_s")
        LD(myexp_s[:, :], myexp_i.ap(), "myexp_s")
        LD(bases_s[:, :], bases_i.ap(), "bases_s")
        for k in range(8):
            LD(gwT_f[:, k * 8:(k + 1) * 8], gwT[k * 128:(k + 1) * 128, :], "gwT_f")
        for i in range(NBLK):
            LD(castA[:, i * N:(i + 1) * N], alog_my[i * 128:(i + 1) * 128, :], "castA")
        ACT(nega[:, :], castA[:, 0:NBLK * N], AF.Exp, ("castA",), ("nega",))

        for it in range(NT):
            xt = xtA if it % 2 == 0 else xtB
            kx = "xtA" if it % 2 == 0 else "xtB"
            sq = sqA if it % 2 == 0 else sqB
            ks = f"sq{it % 2}"
            st = lambda j: rstat[:, it * 8 + j: it * 8 + j + 1]
            kt = f"st{it}"
            LD(xt, xrows[it * 128:(it + 1) * 128, :], kx)
            for ln2 in range(2):
                RED(st(0), xt, AL.add, (kx,), (kt,))
                TT(sq, xt, xt, AL.mult, (kx, ks), (ks,), eng="gpsimd")
                RED(st(1), sq, AL.add, (ks,), (kt,))
                TS(st(2), st(0), 1.0 / D, None, AL.mult, AL.add, (kt,), (kt,))
                TT(st(3), st(2), st(2), AL.mult, (kt,), (kt,))
                STT(st(4), st(1), cvals[:, 3:4], st(3), AL.mult,
                    AL.subtract, (kt, "cvals"), (kt,))
                ACT(st(5), st(4), AF.Ln, (kt, "cvals"), (kt,), bias=cvals[:, 0:1])
                ACT(st(6), st(5), AF.Exp, (kt, "cvals"), (kt,), scale=cvals[:, 1:2])
                if ln2 == 0:
                    TS(xt, xt, st(2), st(6), AL.subtract, AL.mult, (kx, kt), (kx,))
                else:
                    TS(sq, xt, st(2), st(6), AL.subtract, AL.mult, (kx, kt), (ks,))
            for fb in range(8):
                cbs = castB[:, fb * 128:(fb + 1) * 128]
                kcb, kxn, kx2 = f"cB{fb}", f"xns{fb}", f"x2s{fb}"
                pcr = psC[:, (fb % 4) * 128:(fb % 4 + 1) * 128]
                kpc = f"psC{fb % 4}"
                pt2 = psT if fb % 2 == 0 else psT2
                kpt = "psT" if fb % 2 == 0 else "psT2"
                P.emit("tensor", lambda e, pcr=pcr, xt=xt, fb=fb: e.transpose(
                    pcr, xt[:, fb * 128:(fb + 1) * 128], ident_f[:, :]),
                    reads=(kx, "ident_f", kpc), writes=(kpc,))
                COPY(cbs, pcr, (kpc, kcb), (kcb,))
                SCOPY(xnstage[:, fb * 128:(fb + 1) * 128], cbs, (kcb, kxn), (kxn,))
                MM(psA[0:8, 0:128], gwT_f[:, fb * 8:(fb + 1) * 8], cbs,
                   ("gwT_f", kcb), ("psA",), start=(fb == 0), stop=(fb == 7))
                P.emit("tensor", lambda e, pt2=pt2, sq=sq, fb=fb: e.transpose(
                    pt2[:, :], sq[:, fb * 128:(fb + 1) * 128], ident_f[:, :]),
                    reads=(ks, "ident_f", kpt), writes=(kpt,))
                P.emit("vector", lambda e, fb=fb, pt2=pt2: e.tensor_scalar(
                    out=x2stage[:, fb * 128:(fb + 1) * 128], in0=pt2[:, :],
                    scalar1=consts[:, 40 + fb:41 + fb], scalar2=consts[:, 48 + fb:49 + fb],
                    op0=AL.mult, op1=AL.add),
                    reads=(kpt, "consts", kx2), writes=(kx2,))
            ST(bass.AP(ag1x, it * 128, [[TPC, 128], [128 * TPC, 8], [1, 128]]),
               xnstage[:, :], "ag1x", tuple(f"xns{fb}" for fb in range(8)))
            ST(bass.AP(ag1z, it * 128, [[TPC, 128], [128 * TPC, 8], [1, 128]]),
               x2stage[:, :], "ag1z", tuple(f"x2s{fb}" for fb in range(8)))
            TS(lg_fm[:, it * 128:(it + 1) * 128], psA[0:8, 0:128], gb_s[:, 0:1], None,
               AL.add, AL.add, ("psA", "gb_s"), ("lg_fm",))

        for it in range(NT):
            Ccol = lambda j, it=it: rcols[:, it * 10 + j: it * 10 + j + 1]
            ohs = oh_tm[:, it * 8:(it + 1) * 8]
            lgs = lg_tm[:, it * 8:(it + 1) * 8]
            P.emit("tensor", lambda e, it=it: e.transpose(
                psT[:, 0:8], lg_fm[:, it * 128:(it + 1) * 128], ident_f[0:8, 0:8]),
                reads=("lg_fm", "ident_f"), writes=("psT",))
            COPY(lgs, psT[:, 0:8], ("psT",), ("lg_tm",))
            RED(Ccol(0), lgs, AL.max, ("lg_tm",), (f"r{it}",), negate=True)
            ACT(ohs, lgs, AF.Exp, ("lg_tm", f"r{it}"), ("oh_tm",), bias=Ccol(0))
            RED(Ccol(1), ohs, AL.add, ("oh_tm",), (f"r{it}",))
            P.emit("vector", lambda e, it=it: e.reciprocal(
                rcols[:, it * 10 + 2: it * 10 + 3], rcols[:, it * 10 + 1: it * 10 + 2]),
                reads=(f"r{it}",), writes=(f"r{it}",))
            TS(Ccol(3), Ccol(0), -1.0, None, AL.mult, AL.add, (f"r{it}",), (f"r{it}",))
            TS(ohs, lgs, Ccol(3), None, AL.is_equal, AL.add,
               ("lg_tm", f"r{it}"), ("oh_tm",))
            TT(pos_tm[:, it * 8:(it + 1) * 8], ohs, iota8[:, :], AL.mult,
               ("oh_tm", "iota8"), ("pos_tm",))
            RED(Ccol(4), pos_tm[:, it * 8:(it + 1) * 8], AL.add, ("pos_tm",), (f"r{it}",))
            MM(psT2[0:8, it:it + 1], ohs, onesf[:, 0:1], ("oh_tm", "onesf"), ("psT2",),
               start=True, stop=True)
        cnts_c = moepk[0:8, 3968:3984]
        cnts16 = moepk[0:16, 3840:3848]
        mb16 = moepk[0:16, 3848:3856]
        mrow16 = moepk[0:1, 3856:3984]
        COPY(cnts_c, psT2[0:8, 0:16], ("psT2", "cnts16"), ("cnts16",))
        P.emit("tensor", lambda e: e.transpose(
            psT[0:16, 0:8], cnts_c, ident_f[0:8, 0:8]),
            reads=("cnts16", "ident_f"), writes=("psT",))
        COPY(cnts16, psT[0:16, 0:8], ("psT", "cnts16"), ("cnts16",))
        MM(ps8[0:1, 0:8], onesf[0:16, 0:1], cnts16, ("onesf", "cnts16"), ("ps8",),
           start=True, stop=True)
        COPY(rtmp8[0:1, 0:8], ps8[0:1, 0:8], ("ps8",), ("cntcol",))
        ST(cntin.ap(), rtmp8[0:1, 0:8], "cntin", ("cntcol",))

        CC("AllGather", cntin.ap().opt(), cntout.ap().opt(), ("cntin",), ("cntout",))
        LD(cnt_s[:, :], cntout.ap(), "cnt_s", reads=("cntout",))
        MM(ps8[0:1, 0:8], prefm_s[:, :], cnt_s[:, :], ("prefm_s", "cnt_s"), ("ps8",),
           start=True, stop=True)
        COPY(rtmp8[0:1, 0:8], ps8[0:1, 0:8], ("ps8",), ("baserow",))
        MM(ps8[0:128, 0:8], ones1f[0:1, 0:128], rtmp8[0:1, 0:8], ("ones1f", "baserow"),
           ("ps8",), start=True, stop=True)
        COPY(base8[:, :], ps8[0:128, 0:8], ("ps8",), ("base8",))
        MM(psT2[0:16, 0:8], tri_f[0:16, 0:16], cnts16, ("tri_f", "cnts16"), ("psT2",),
           start=True, stop=True)
        TT(mb16, psT2[0:16, 0:8], cnts16, AL.subtract, ("psT2", "cnts16", "mb16"),
           ("mb16",))
        TT(mb16, mb16, base8[0:16, 0:8], AL.add, ("mb16", "base8"), ("mb16",))
        for it in range(NT):
            LD(mrow16[0:1, it * 8:(it + 1) * 8], mb16[it:it + 1, 0:8], "mrow16",
               reads=("mb16", "mrow16"), eng="sync")

        for it in range(NT):
            Ccol = lambda j, it=it: rcols[:, it * 10 + j: it * 10 + j + 1]
            ohs = oh_tm[:, it * 8:(it + 1) * 8]
            MM(psT2[:, 0:8], tri_f[:, :], ohs, ("tri_f", "oh_tm"), ("psT2",),
               start=True, stop=True)
            COPY(pos_tm[:, it * 8:(it + 1) * 8], psT2[:, 0:8], ("psT2",), ("pos_tm",))
            MM(ps8[0:128, 0:8], ones1f[0:1, 0:128], mrow16[0:1, it * 8:(it + 1) * 8],
               ("ones1f", "mrow16"), ("ps8",), start=True, stop=True)
            TT(rtmp8[:, :], pos_tm[:, it * 8:(it + 1) * 8], ps8[0:128, 0:8], AL.add,
               ("pos_tm", "ps8"), ("rtmp8",))
            TS(rtmp8[:, :], rtmp8[:, :], 1.0, None, AL.subtract, AL.add,
               ("rtmp8",), ("rtmp8",))
            TT(rtmp8[:, :], rtmp8[:, :], ohs, AL.mult, ("rtmp8", "oh_tm"), ("rtmp8",))
            RED(Ccol(9), rtmp8[:, :], AL.add, ("rtmp8",), (f"r{it}",))
            TS(Ccol(6), Ccol(9), float(CAP), None, AL.is_lt, AL.add, (f"r{it}",), (f"r{it}",))
            TT(Ccol(5), Ccol(9), Ccol(6), AL.mult, (f"r{it}",), (f"r{it}",))
            TT(Ccol(7), Ccol(2), Ccol(6), AL.mult, (f"r{it}",), (f"r{it}",))
            P.emit("tensor", lambda e, it=it: e.transpose(
                psT[0:4, 0:128], rcols[:, it * 10 + 4: it * 10 + 8], ident_f[:, :]),
                reads=(f"r{it}", "ident_f"), writes=("psT",))
            mst = moepk[0:4, 3584 + (it % 2) * 128: 3712 + (it % 2) * 128]
            SCOPY(mst, psT[0:4, 0:128], ("psT", f"mst{it % 2}"), (f"mst{it % 2}",))
            ST(metain.ap()[0:4, it * 128:(it + 1) * 128], mst, "metain", (f"mst{it % 2}",))
        CC("AllGather", ag1z.ap().opt(), ag1zout.ap().opt(), ("ag1z",), ("ag1zout",))
        CC("AllGather", metain.ap().opt(), metaout.ap().opt(), ("metain",), ("metaout",))
        CC("AllGather", ag1x.ap().opt(), ag1xout.ap().opt(), ("ag1x",), ("ag1xout",))
        for half in range(2):
            dyn_ld(myx2.ap()[half * EH:(half + 1) * EH, :], ag1zout,
                   half, 0, [[TPC, EH], [1, TPC]], "myx2", reads=("ag1zout",))

        PRI = []
        BULK = []
        _ps = [0]

        def take_ps():
            i = _ps[0] % 2
            _ps[0] += 1
            return (psA, "psA") if i == 0 else (psC, "psC")

        def pop_fill():
            if BULK:
                BULK.pop(0)()

        def inproj_pieces(ck):
            half = ck // 4
            tcl = ck % 4
            ps_list = []

            def p_x2sl(ck=ck, half=half, tcl=tcl):
                for k in range(8):
                    LD(x2sl_v[:, k * 512:(k + 1) * 512],
                       myx2.ap()[half * EH + k * 128: half * EH + (k + 1) * 128,
                                 tcl * 512:(tcl + 1) * 512],
                       "x2sl", reads=("myx2", "x2sl"))
            ps_list.append(p_x2sl)
            for m in range(16):
                def p_m(ck=ck, m=m):
                    par = m % 2
                    LD(wpar_v[:, par * 1024:(par + 1) * 1024],
                       winsp.ap()[m * 128:(m + 1) * 128, :], f"wpar{par}",
                       reads=("winsp", f"wpar{par}"))
                    ps, pk = take_ps()
                    MMC(ps[:, :],
                        [(wpar_v[:, par * 1024 + k * 128: par * 1024 + (k + 1) * 128],
                          x2sl_v[:, k * 512:(k + 1) * 512]) for k in range(8)],
                        (f"wpar{par}", "x2sl"), (pk,))
                    ev = uev_v[:, (m % 8) * 512:(m % 8 + 1) * 512]
                    evk = f"uev{m % 8}"
                    ACT(ev, ps[:, :], AF.Silu, (pk, "consts", evk), (evk,),
                        bias=consts[:, m:m + 1])
                    if m == 7:
                        ST(bass.AP(u_sp, ck * 512, [[TB, 128], [128 * TB, 8], [1, 512]]),
                           uev_v[:, 0:4096], "u_sp", tuple(f"uev{j}" for j in range(8)))
                        MMC(ps96[:, :],
                            [(wot[:, j * 96:(j + 1) * 96], uev_v[:, j * 512:(j + 1) * 512])
                             for j in range(8)],
                            ("wot",) + tuple(f"uev{j}" for j in range(8)), ("ps96",))
                        ev96 = rbcev_v[:, (ck % 2) * 512:(ck % 2 + 1) * 512]
                        COPY(ev96, ps96[:, :], ("ps96", f"rbcev{ck % 2}"),
                             (f"rbcev{ck % 2}",))
                        ST(ag3in[ck * 96:(ck + 1) * 96, :], ev96, "ag3in",
                           (f"rbcev{ck % 2}",))
                    if m == 15:
                        ST(bass.AP(gate_sp, ck * 512, [[TB, 128], [128 * TB, 8], [1, 512]]),
                           uev_v[:, 0:4096], "gate_sp", tuple(f"uev{j}" for j in range(8)))
                ps_list.append(p_m)
            return ps_list

        def ag3cc(ck):
            CC("AllGather", ag3in.ap()[ck * 96:(ck + 1) * 96, :].opt(),
               ag3out.ap()[ck * 768:(ck + 1) * 768, :].opt(),
               ("ag3in",), (f"ag3o{ck}",))

        def rbc(ck):
            dyn_ld(AR2[0:96, 1024:2048], ag3out, 2, ck * 8 * 96 * 512,
                   [[512, 96], [96 * 512, 2], [1, 512]], "art0",
                   reads=(f"ag3o{ck}", "art0", "art1"))
            TT(art[0][0:96, :], art[0][0:96, :], art[1][0:96, :], AL.add,
               ("art0",), ("art0", "art1"))
            TS(art[0][0:96, :], art[0][0:96, :], rbcb_s[:, :], None, AL.add, AL.add,
               ("art0", "rbcb_s"), ("art0",))
            COPY(rbf2_v, art[0][0:64, :], ("art0", "rbf2"), ("rbf2",))
            COPY(bmcm_b[:, ck * 512:(ck + 1) * 512], art[0][64:96, :],
                 ("art0", "bmcm"), ("bmcm",))
            TS(bmcm_b[0:16, ck * 512:(ck + 1) * 512],
               bmcm_b[0:16, ck * 512:(ck + 1) * 512], -1.0, None, AL.mult, AL.add,
               ("bmcm",), ("bmcm",))

        def delta(ck):
            groups = [(0, 3), (3, 3), (6, 2)]
            for g0, gn in groups:
                for j in range(gn):
                    blk = g0 + j
                    ps, pk = take_ps()
                    MM(ps[:, :], wsc[0:64, blk * 128:(blk + 1) * 128], rbf2_v,
                       ("wsc", "rbf2"), (pk,), start=True, stop=True)
                    ACT(art[j], ps[:, :], AF.Sigmoid,
                        (pk, "consts", "cvals", f"art{j}"), (f"art{j}",),
                        bias=consts[:, 16 + blk:17 + blk], scale=cvals[:, 2:3])
                for j in range(gn):
                    blk = g0 + j
                    db = dsgb[j % 2]
                    kdb = f"dsgb{j % 2}"
                    ACT(db, art[j], AF.Ln, (f"art{j}", kdb), (kdb,))
                    ST(lns_sp[blk * 128:(blk + 1) * 128, ck * 512:(ck + 1) * 512], db,
                       "lns_sp", (kdb,), eng="sync")

        def gather_pieces():
            out = []

            def p_f():
                FENCE(("slabx",), ("llistbc", "xsf"))
                FENCE(("xsfo",), ())
            out.append(p_f)
            for fb in range(8):
                def p_fb(fb=fb):
                    for r8 in range(NCORE):
                        LD(slabx_v[:, r8 * TPC:(r8 + 1) * TPC],
                           ag1xout.ap()[r8 * D + fb * 128: r8 * D + (fb + 1) * 128, :],
                           f"slx{r8}", reads=("ag1xout", f"slx{r8}"))
                    for part in range(3):
                        w = 1024 if part < 2 else 512
                        P.emit("gpsimd", lambda e, part=part, w=w: e.indirect_copy(
                            xsf_v[:, part * 1024: part * 1024 + w],
                            slabx_v[:, 0:NCORE * TPC],
                            cidx[:, part * 64: part * 64 + w // 16], True),
                            reads=tuple(f"slx{r}" for r in range(NCORE)) + ("cidx", "xsfo"),
                            writes=("xsfo",))
                    ST(xs_sp[fb * 128:(fb + 1) * 128, :], xsf_v, "xs_sp", ("xsfo",))
                out.append(p_fb)
            return out

        ag4state = [False, False]

        def ffn_pieces():
            out = []

            def p_fence():
                FENCE(("xssl", "h1"), ("xsfo", "slabx"))
            out.append(p_fence)
            for tc5 in range(5):
                def p_xl(tc5=tc5):
                    for k in range(8):
                        LD(xssl_v[:, k * 512:(k + 1) * 512],
                           xs_sp.ap()[k * 128:(k + 1) * 128, tc5 * 512:(tc5 + 1) * 512],
                           "xssl", reads=("xs_sp", "xssl"))
                out.append(p_xl)
                for f in range(32):
                    def p_f(tc5=tc5, f=f):
                        par = f % 2
                        LD(w1t[:, par * 1024:(par + 1) * 1024],
                           w1bf.ap()[f * 128:(f + 1) * 128, :], f"w1t{par}",
                           reads=("w1bf", f"w1t{par}"))
                        ps, pk = take_ps()
                        MMC(ps[:, :],
                            [(w1t[:, par * 1024 + k * 128: par * 1024 + (k + 1) * 128],
                              xssl_v[:, k * 512:(k + 1) * 512]) for k in range(8)],
                            (f"w1t{par}", "xssl"), (pk,))
                        ACT(h1_v[:, f * 512:(f + 1) * 512], ps[:, :], AF.Relu,
                            (pk, "b1_s", "h1"), ("h1",), bias=b1_s[:, f:f + 1])
                    out.append(p_f)
                for d8 in range(8):
                    def p_w2ld(d8=d8):
                        LD(w2t_v[:, :], w2bf.ap()[d8 * 128:(d8 + 1) * 128, :], "w2t",
                           reads=("w2bf", "w2t"))
                    out.append(p_w2ld)
                    psd = (psB, "psB")
                    for g in range(4):
                        def p_sub(tc5=tc5, d8=d8, g=g, psd=psd):
                            ps, pk = psd
                            MMC(ps[:, :],
                                [(w2t_v[:, (g * 8 + f) * 128:(g * 8 + f + 1) * 128],
                                  h1_v[:, (g * 8 + f) * 512:(g * 8 + f + 1) * 512])
                                 for f in range(8)],
                                ("w2t", "h1"), (pk,), first=(g == 0), last=(g == 3))
                        out.append(p_sub)

                    def p_ev(tc5=tc5, d8=d8, psd=psd):
                        ps, pk = psd
                        fe = ffev_v[:, 0:512]
                        TS(fe, ps[:, :], b2_s[:, d8:d8 + 1], None, AL.add, AL.add,
                           (pk, "b2_s", "ffev0"), ("ffev0",))
                        ST(ag4in[d8 * 128:(d8 + 1) * 128, tc5 * 512:(tc5 + 1) * 512], fe,
                           "ag4in", ("ffev0",))
                    out.append(p_ev)
            return out

        def scan_q(q, mid=None):
            while PRI:
                PRI.pop(0)()
            if q == 0:
                FENCE(("bbc", "cbc"), ("lbctab",))
                FENCE(("a0", "a1", "ym0", "ym1", "h0", "h1p", "yac"), ())
                FENCE(("ygt0", "ygt1"), ())
            for n in range(N):
                MM(psC[:, :], selmat[:, n * 128:(n + 1) * 128],
                   bmcm_b[:, q * QT:(q + 1) * QT],
                   ("bmcm", "selmat"), ("psC",), start=True, stop=True)
                COPY(bbc_v[:, n * QT:(n + 1) * QT], psC[:, :], ("psC", "bbc"), ("bbc",))
                MM(psC[:, :], selmat[:, (16 + n) * 128:(17 + n) * 128],
                   bmcm_b[:, q * QT:(q + 1) * QT],
                   ("bmcm", "selmat"), ("psC",), start=True, stop=True)
                COPY(cbc_v[:, n * QT:(n + 1) * QT], psC[:, :], ("psC", "cbc"), ("cbc",))
            for blk in range(NBLK):
                if blk == 2 and mid is not None:
                    mid()
                if blk in (3, 6):
                    npop = (len(PRI) + 1) // 2 if blk == 3 else len(PRI)
                    for _ in range(npop):
                        if PRI:
                            PRI.pop(0)()
                pb = blk % 2
                lk, uk, gk, ygk = f"lns{pb}", f"ut{pb}", f"gt{pb}", f"ygt{pb}"
                lns = lns_t[:, pb * QT:(pb + 1) * QT]
                ut = u_t[:, pb * QT:(pb + 1) * QT]
                gt = gate_t[:, pb * QT:(pb + 1) * QT]
                ygs = ygt[:, pb * QT:(pb + 1) * QT]
                LD(lns, lns_sp.ap()[blk * 128:(blk + 1) * 128, q * QT:(q + 1) * QT],
                   lk, reads=("lns_sp", lk))
                LD(ut, u_sp.ap()[blk * 128:(blk + 1) * 128, q * QT:(q + 1) * QT],
                   uk, reads=("u_sp", uk))
                LD(gt, gate_sp.ap()[blk * 128:(blk + 1) * 128, q * QT:(q + 1) * QT],
                   gk, reads=("gate_sp", gk))
                TT(ndu[:, :], lns, ut, AL.mult, (lk, uk, "ndu"), ("ndu",))
                for n in range(N):
                    pn = n % 2
                    ak, hk, mk = f"a{pn}", (f"h{pn}" if pn == 0 else "h1p"), f"ym{pn}"
                    av = a_v[:, pn * 512:(pn + 1) * 512]
                    hv = h_v[:, pn * 512:(pn + 1) * 512]
                    mv = ymul_v[:, pn * 512:(pn + 1) * 512]
                    ACT(av, lns, AF.Exp, (lk, "nega", ak), (ak,),
                        scale=nega[:, blk * N + n: blk * N + n + 1])
                    TT(b_t[:, :], ndu[:, :], bbc_v[:, n * QT:(n + 1) * QT], AL.mult,
                       ("ndu", "bbc", "b_t"), ("b_t",))
                    P.emit("vector", lambda e, q=q, blk=blk, n=n, hv=hv, av=av:
                           e.tensor_tensor_scan(
                        out=hv, data0=av, data1=b_t[:, :],
                        initial=(0.0 if q == 0 else carryc[:, blk * N + n: blk * N + n + 1]),
                        op0=AL.mult, op1=AL.add),
                        reads=(ak, "b_t", hk) + (() if q == 0 else ("carryc",)),
                        writes=(hk,))
                    COPY(carryc[:, blk * N + n: blk * N + n + 1], hv[:, QT - 1:QT],
                         (hk, "carryc"), ("carryc",))
                    TT(mv, hv, cbc_v[:, n * QT:(n + 1) * QT], AL.mult,
                       (hk, "cbc", mk), (mk,))
                    if n == 0:
                        COPY(yacc_v, mv, (mk, "yac"), ("yac",))
                    else:
                        TT(yacc_v, yacc_v, mv, AL.add, ("yac", mk), ("yac",))
                    if n % 2 == 1:
                        pop_fill()
                STT(yacc_v, ut, consts[:, 32 + blk:33 + blk], yacc_v,
                    AL.mult, AL.add, (uk, "consts", "yac"), ("yac",))
                TT(ygs, yacc_v, gt, AL.mult, ("yac", gk, ygk), (ygk,))
                ST(ag5in[q * EH + blk * 128: q * EH + (blk + 1) * 128, :], ygs,
                   "ag5in", (ygk,), eng="sync")

        def ag5cc(q):
            CC("AllGather", ag5in.ap()[q * EH:(q + 1) * EH, :].opt(),
               ag5out.ap()[q * 8 * EH:(q + 1) * 8 * EH, :].opt(),
               ("ag5in",), (f"ag5o{q}",))

        FENCE(("x2sl",), ())
        FENCE(("psC",), ("psC0", "psC1", "psC2", "psC3"))
        for k in range(8):
            LD(castB[:, 0:96], wdbc_my[k * 128:(k + 1) * 128, :], "castB")
            COPY(wot[:, k * 96:(k + 1) * 96],
                 castB[:, 0:96], ("castB", "wot"), ("wot",))
        LD(castA[0:64, :], wdt_my.ap(), "castA")
        COPY(wsc[0:64, 0:1024], castA[0:64, :], ("castA", "wsc"), ("wsc",))
        FENCE(("rbcev0", "rbcev1"), ("castB",))
        FENCE(("spb0", "spb1"), ("castA",))
        FENCE(("spo0", "spo1"), ())

        _spn = [0]

        def spill_tile(src, rr, cc, dst, dstw, key):
            j = _spn[0] % 2
            _spn[0] += 1
            bsl = castA[:, j * 512:(j + 1) * 512]
            osl = (stile if j == 0 else resid).bitcast(BF16)[:, 512:1024]
            out_ap = bass.AP(dst, cc * 4 * 128 * dstw + rr * 128,
                             [[dstw, 128], [128 * dstw, 4], [1, 128]])
            kb, ko = f"spb{j}", f"spo{j}"
            LD(bsl, src[rr * 128:(rr + 1) * 128, cc * 512:(cc + 1) * 512], kb,
               reads=(kb,), eng="scalar")
            SCOPY(osl, bsl, (kb, ko), (ko,))
            ST(out_ap, osl, key, (ko,), eng="scalar")

        for rr in range(8):
            for cc in range(4):
                spill_tile(win_my.ap(), rr, cc, winsp, D, "winsp")

        def spill_pieces():
            out = []
            for rr in range(8):
                for cc in range(8):
                    out.append(lambda rr=rr, cc=cc: spill_tile(
                        w1e.ap(), rr, cc, w1bf, D, "w1bf"))
            for rr in range(32):
                for cc in range(2):
                    out.append(lambda rr=rr, cc=cc: spill_tile(
                        w2e.ap(), rr, cc, w2bf, FF, "w2bf"))
            for rr in range(16):
                for cc in range(2):
                    out.append(lambda rr=rr, cc=cc: spill_tile(
                        wout_i.ap(), rr, cc, woutbf, E, "woutbf"))
            return out

        FENCE(("diagD",), ())
        COPY(identb_v[:, 0:128], ident_f[:, :], ("ident_f", "diagD"), ("diagD",))
        for blk in range(NBLK):
            TS(diagD_v[:, blk * 128:(blk + 1) * 128], identb_v[:, 0:128],
               consts[:, 32 + blk:33 + blk], None, AL.mult, AL.add,
               ("diagD", "consts"), ("diagD",))

        for go in inproj_pieces(0):
            go()
        ag3cc(0)
        rbc(0)
        delta(0)
        for go in inproj_pieces(1):
            go()
        ag3cc(1)
        FENCE(("lbctab",), ("xtA", "xtB", "sq0", "sq1", "lg_fm"))
        FENCE(("castA",), ("spb0", "spb1"))
        FENCE(("xsf",), ("llistbc",))
        for r8 in range(NCORE):
            LD(castA[:, r8 * 16:(r8 + 1) * 16],
               bass.AP(metaout, (r8 * 4 + 0) * TPC, [[1, 128], [128, 16]]),
               "castA", reads=("metaout",))
            LD(castA[:, 128 + r8 * 16: 128 + (r8 + 1) * 16],
               bass.AP(metaout, (r8 * 4 + 2) * TPC, [[1, 128], [128, 16]]),
               "castA", reads=("metaout",))
        TS(castB[:, 0:128], castA[:, 0:128], myexp_s[:, 0:1], None, AL.is_equal, AL.add,
           ("castA", "myexp_s", "castB"), ("castB",))
        TT(castA[:, 256:384], castB[:, 0:128], castA[:, 128:256], AL.mult,
           ("castB", "castA"), ("castA",))
        MM(psT[:, :], tri_f[:, :], castA[:, 256:384], ("tri_f", "castA"), ("psT",),
           start=True, stop=True)
        COPY(castA[:, 384:512], psT[:, :], ("psT", "castA"), ("castA",))
        LD(castB[0:1, 128:256], castA[127:128, 384:512], "castB", reads=("castA",),
           eng="sync")
        P.emit("tensor", lambda e: e.transpose(
            psT2[:, 0:1], castB[0:1, 128:256], ident_f[0:1, 0:1]),
            reads=("castB", "ident_f"), writes=("psT2",))
        COPY(castB[:, 256:257], psT2[:, 0:1], ("psT2", "castB"), ("castB",))
        MM(psT2[:, 0:1], tri_f[:, :], castB[:, 256:257], ("tri_f", "castB"), ("psT2",),
           start=True, stop=True)
        COPY(castB[:, 257:258], psT2[:, 0:1], ("psT2", "castB"), ("castB",))
        TT(castB[:, 258:259], castB[:, 257:258], castB[:, 256:257], AL.subtract,
           ("castB",), ("castB",))
        P.emit("tensor", lambda e: e.transpose(
            psT2[0:1, 0:128], castB[:, 258:259], ident_f[:, :]),
            reads=("castB", "ident_f"), writes=("psT2",))
        COPY(castB[0:1, 384:512], psT2[0:1, 0:128], ("psT2", "castB"), ("castB",))
        for pp in range(16):
            LD(moepk[pp:pp + 1, 0:128], castB[0:1, 384:512], "moepk",
               reads=("castB",), eng="sync")
        acc_i = moepk[:, 1024:1184]
        acc_p = moepk[:, 1184:1344]
        cmp_t = moepk[:, 1344:1504]
        tmp_t = moepk[:, 1504:1664]
        MEMSET(acc_i, 0.0, ("moepk",))
        MEMSET(acc_p, 0.0, ("moepk",))
        for i in range(128):
            TS(cmp_t, tokid[:, 0:160], moepk[:, i:i + 1], None, AL.is_ge, AL.add,
               ("tokid", "moepk"), ("moepk",))
            TT(acc_i, acc_i, cmp_t, AL.add, ("moepk",), ("moepk",))
            TS(tmp_t, cmp_t, moepk[:, i:i + 1], None, AL.mult, AL.add,
               ("moepk",), ("moepk",))
            TT(acc_p, acc_p, tmp_t, AL.max, ("moepk",), ("moepk",))
        TS(acc_i, acc_i, 1.0, None, AL.subtract, AL.add, ("moepk",), ("moepk",))
        TT(cmp_t, tokid[:, 0:160], acc_p, AL.subtract, ("tokid", "moepk"), ("moepk",))
        TS(tmp_t, acc_i, 128.0, None, AL.mult, AL.add, ("moepk",), ("moepk",))
        TT(tmp_t, tmp_t, cmp_t, AL.add, ("moepk",), ("moepk",))
        P.emit("vector", lambda e: e.tensor_scalar_min(tmp_t, tmp_t, 16383.0),
               reads=("moepk",), writes=("moepk",))
        COPY(comp16[:, 0:160], tmp_t, ("moepk", "comp16"), ("comp16",))
        for g in range(8):
            P.emit("sync", lambda e, g=g: e.dma_start(
                out=cidx[g * 16:(g + 1) * 16, :], in_=comp16[:, :]),
                reads=("comp16",), writes=("cidx",), dma=True)
        FENCE(("llistbc",), ("h1",))
        for i in range(128):
            TS(ARB2[:, i * 128:(i + 1) * 128], iotar[:, :], castA[:, 384 + i:385 + i],
               None, AL.is_ge, AL.add, ("iotar", "castA", "llistbc"), ("llistbc",))
        for ch in range(32):
            MM(psC[0:1, 0:512], onescb[:, 0:1], ARB2[:, ch * 512:(ch + 1) * 512],
               ("onescb", "llistbc"), ("psC",), start=True, stop=True)
            COPY(castB.bitcast(BF16)[0:1, 1024:1536], psC[0:1, 0:512],
                 ("psC", "castB"), ("castB",))
            MM(psA[:, :], ones1b[0:1, :], castB.bitcast(BF16)[0:1, 1024:1536],
               ("ones1b", "castB"), ("psA",), start=True, stop=True)
            COPY(ARBIG.bitcast(BF16)[:, ch * 512:(ch + 1) * 512], psA[:, :],
                 ("psA", "lbctab"), ("lbctab",))
        mr0 = moepk[:, 128:288]
        mr1 = moepk[:, 288:448]
        idr = moepk[:, 448:608]
        for r2 in range(2):
            mr = mr0 if r2 == 0 else mr1
            TS(mr, tmp_t, float(8192 * r2), None, AL.is_ge, AL.add, ("moepk",), ("moepk",))
            if r2 == 0:
                TS(idr, tmp_t, 8192.0, None, AL.is_lt, AL.add, ("moepk",), ("moepk",))
                TT(mr, mr, idr, AL.mult, ("moepk",), ("moepk",))
            TS(idr, tmp_t, float(8192 * r2), None, AL.subtract, AL.add,
               ("moepk",), ("moepk",))
            TT(idr, idr, mr, AL.mult, ("moepk",), ("moepk",))
            COPY(comp16[:, 0:160], idr, ("moepk", "comp16"), ("comp16",))
            for g in range(8):
                P.emit("sync", lambda e, g=g: e.dma_start(
                    out=cidx[g * 16:(g + 1) * 16, :], in_=comp16[:, :]),
                    reads=("comp16",), writes=("cidx",), dma=True)
            for part in range(3):
                w = 1024 if part < 2 else 512
                P.emit("gpsimd", lambda e, part=part, w=w, r2=r2: e.indirect_copy(
                    ARB2[:, r2 * 2560 + part * 1024:
                         r2 * 2560 + part * 1024 + w],
                    ARBIG.bitcast(BF16)[:, 8192 * r2: 8192 * (r2 + 1)],
                    cidx[:, part * 64: part * 64 + w // 16], True),
                    reads=("lbctab", "cidx", "xsf"), writes=("xsf",))
            ST(lv_hbm.ap()[0:1, :], ARB2[0:1, r2 * 2560: r2 * 2560 + 2560],
               f"lv_hbm{r2}", ("xsf",), eng="gpsimd")
            LD(moepk.bitcast(BF16)[:, 7600 + r2 * 160: 7760 + r2 * 160],
               bass.AP(lv_hbm, 0, [[1, 16], [16, 160]]), f"lvint{r2}",
               reads=(f"lv_hbm{r2}", "moepk"))
        COPY(cmp_t, moepk.bitcast(BF16)[:, 7600:7760], ("moepk",), ("moepk",))
        TT(cmp_t, cmp_t, mr0, AL.mult, ("moepk",), ("moepk",))
        COPY(idr, moepk.bitcast(BF16)[:, 7760:7920], ("moepk",), ("moepk",))
        TT(idr, idr, mr1, AL.mult, ("moepk",), ("moepk",))
        TT(cmp_t, cmp_t, idr, AL.add, ("moepk",), ("moepk",))
        TS(tmp_t, acc_i, 128.0, None, AL.mult, AL.add, ("moepk",), ("moepk",))
        TT(tmp_t, tmp_t, cmp_t, AL.add, ("moepk",), ("moepk",))
        P.emit("vector", lambda e: e.tensor_scalar_min(tmp_t, tmp_t, 16383.0),
               reads=("moepk",), writes=("moepk",))
        COPY(comp16[:, 0:160], tmp_t, ("moepk", "comp16"), ("comp16",))
        for g in range(8):
            P.emit("sync", lambda e, g=g: e.dma_start(
                out=cidx[g * 16:(g + 1) * 16, :], in_=comp16[:, :]),
                reads=("comp16",), writes=("cidx",), dma=True)

        LD(midx_v[:, 0:128], bass.AP(metain, 0, [[1, 16], [16, 128]]), "moepk",
           reads=("metain",))
        LD(mkeep_v[:, 0:128], bass.AP(metain, TPC, [[1, 16], [16, 128]]), "moepk",
           reads=("metain",))
        STT(mwork_v[:, 0:128], midx_v[:, 0:128], cvals[0:16, 4:5], mkeep_v[:, 0:128],
            AL.mult, AL.add, ("moepk", "cvals"), ("moepk",))
        for r3 in range(3):
            mr = moepk[:, 608 + r3 * 128: 736 + r3 * 128]
            TS(mr, mwork_v[:, 0:128], float(8192 * r3), None, AL.is_ge, AL.add,
               ("moepk",), ("moepk",))
            if r3 < 2:
                TS(moepk[:, 992:1120] if False else midx_v[:, 128:256],
                   mwork_v[:, 0:128], float(8192 * (r3 + 1)), None, AL.is_lt, AL.add,
                   ("moepk",), ("moepk",))
                TT(mr, mr, midx_v[:, 128:256], AL.mult, ("moepk",), ("moepk",))
            TS(midx_v[:, 128:256], mwork_v[:, 0:128], float(8192 * r3), None,
               AL.subtract, AL.add, ("moepk",), ("moepk",))
            TT(midx_v[:, 128:256], midx_v[:, 128:256], mr, AL.mult,
               ("moepk",), ("moepk",))
            COPY(gidx16[0:16, r3 * 128:(r3 + 1) * 128], midx_v[:, 128:256],
                 ("moepk", "gidx16"), ("gidx16",))
            for g in range(1, 8):
                P.emit("sync", lambda e, g=g, r3=r3: e.dma_start(
                    out=gidx16[g * 16:(g + 1) * 16, r3 * 128:(r3 + 1) * 128],
                    in_=gidx16[0:16, r3 * 128:(r3 + 1) * 128]),
                    reads=("gidx16",), writes=("gidx16",), dma=True)
        gxr = rstat[:, 0:16]
        FENCE(("gxr",), tuple(f"st{it}" for it in range(NT)))
        for r3 in range(3):
            mr = moepk[:, 608 + r3 * 128: 736 + r3 * 128]
            P.emit("tensor", lambda e, mr=mr: e.transpose(
                psT[0:128, 0:16], mr, ident_f[0:16, 0:16]),
                reads=("moepk", "ident_f"), writes=("psT",))
            SCOPY(gxr, psT[0:128, 0:16], ("psT", "gxr"), ("gxr",))
            ST(bass.AP(mrow_hbm, r3 * TPC, [[16, 128], [1, 16]]), gxr,
               f"mrowh{r3}", ("gxr",), eng="gpsimd")

        FENCE(("spb0", "spb1"), ("castA",))
        BULK.extend(gather_pieces())
        BULK.extend(spill_pieces())
        BULK.extend(ffn_pieces())
        for ck in range(NQ):
            if ck + 2 < NQ:
                PRI.extend(inproj_pieces(ck + 2))

            def mid(ck=ck):
                if ck + 1 < NQ:
                    rbc(ck + 1)
                    delta(ck + 1)
            scan_q(ck, mid=mid)
            while PRI:
                PRI.pop(0)()
            if ck + 2 < NQ:
                ag3cc(ck + 2)
            if ck >= NQ - 3 and not BULK and not ag4state[0]:
                CC("AllGather", ag4in.ap().opt(), ag4out.ap().opt(),
                   ("ag4in",), ("ag4out",))
                ag4state[0] = True
            ag5cc(ck)
        while BULK:
            BULK.pop(0)()
        if not ag4state[0]:
            CC("AllGather", ag4in.ap().opt(), ag4out.ap().opt(), ("ag4in",), ("ag4out",))

        FENCE(("gvmbc",), ("a0", "a1", "ym0", "ym1", "h0", "h1p", "yac",
                           "art0", "art1", "art2"))
        FENCE(("gvbc", "graw"), ("xssl", "xsfo"))
        FENCE(("castA", "castB"), ("spb0", "spb1", "rbcev0", "rbcev1"))
        FENCE(("gvrow",), ("moepk",))
        for r3 in range(3):
            for hh in range(2):
                LD(castA[0:1, :], mrow_hbm.ap()[r3:r3 + 1, hh * 1024:(hh + 1) * 1024],
                   "castA", reads=(f"mrowh{r3}", "castA"))
                LD(castB[0:1, :], metain.ap()[3:4, hh * 1024:(hh + 1) * 1024],
                   "castB", reads=("metain", "castB"))
                TT(castA[0:1, :], castA[0:1, :], castB[0:1, :], AL.mult,
                   ("castA", "castB"), ("castA",))
                COPY(gvrow[:, hh * 1024:(hh + 1) * 1024], castA[0:1, :],
                     ("castA", "gvrow"), ("gvrow",))
            for tc in range(4):
                MM(psC[:, :], ones1b[0:1, :], gvrow[0:1, tc * 512:(tc + 1) * 512],
                   ("ones1b", "gvrow"), ("psC",), start=True, stop=True)
                COPY(gvm_bc[:, r3 * 2048 + tc * 512: r3 * 2048 + (tc + 1) * 512],
                     psC[:, :], ("psC", "gvmbc"), ("gvmbc",))
        FENCE(("gfb",), ("h1", "slabx"))
        FENCE(("gacck",), ("yac", "art2"))
        FENCE(tuple(f"sl{r8}" for r8 in range(NCORE)),
              ("bbc", "cbc", "wpar0", "wpar1", "diagD"))
        gacc2 = AR2.bitcast(BF16)[:, 6144:8192]
        for fb in range(8):
            for r8 in range(NCORE):
                LD(slab_v[:, r8 * CAP:(r8 + 1) * CAP],
                   ag4out.ap()[r8 * D + fb * 128: r8 * D + (fb + 1) * 128, :],
                   f"sl{r8}", reads=("ag4out", f"sl{r8}"))
            t2 = gfb_v[:, fb * TPC:(fb + 1) * TPC]

            for r3 in range(3):
                dst = (gacc_v, gtmp_v, t2)[r3]

                def icpair(e, dst=dst, r3=r3):
                    last = None
                    for part in range(2):
                        last = e.indirect_copy(
                            dst[:, part * 1024:(part + 1) * 1024],
                            slab_v[:, 8192 * r3: min(8192 * (r3 + 1), NCORE * CAP)],
                            gidx16[:, r3 * 128 + part * 64: r3 * 128 + (part + 1) * 64],
                            True)
                    return last
                P.emit("gpsimd", icpair,
                       reads=tuple(f"sl{r8}" for r8 in range(NCORE)) +
                             ("gidx16", f"gr{r3}") + (("gfb",) if r3 == 2 else ()),
                       writes=(f"gr{r3}",))
            TT(gacc2, gacc_v, gvm_bc[:, 0:2048], AL.mult,
               ("gr0", "gvmbc", "gacck"), ("gacck",))
            TT(gtmp_v, gtmp_v, gvm_bc[:, 2048:4096], AL.mult,
               ("gr1", "gvmbc"), ("gr1",))
            TT(gacc2, gacc2, gtmp_v, AL.add, ("gacck", "gr1"), ("gacck",))
            TT(t2, t2, gvm_bc[:, 4096:6144], AL.mult, ("gr2", "gvmbc"), ("gr2",))
            TT(gacc2, gacc2, t2, AL.add, ("gacck", "gr2"), ("gacck",))
            ACT(gfb_v[:, fb * TPC:(fb + 1) * TPC], gacc2, AF.Silu,
                ("gacck", "gfb", "gr2"), ("gfb",))

        FENCE(("ysl",), ("gvmbc", "gacck"))
        FENCE(("ottm",), ("x2sl", "w2t"))
        FENCE(("wot0", "wot1"), ("w1t0", "w1t1", "wot", "wsc"))
        FENCE(("stile", "resid", "sgt"), ("dsgb0", "dsgb1", "spo0", "spo1", "ffev0"))
        FENCE(("xrow_v",), ("gvbc", "graw"))
        for tc in range(4):
            for half in range(2):
                dyn_ld(ysl_v[:, half * 4096:(half + 1) * 4096], ag5out, 4 + half,
                       tc * 8 * EH * 512,
                       [[512, 128], [128 * 512, 8], [1, 512]], "ysl",
                       reads=(f"ag5o{tc}", f"ag5o{4 + tc}", "ysl"))
            for d8 in range(8):
                par = d8 % 2
                wbuf = wot if par == 0 else w1t
                LD(wbuf[:, 0:2048], woutbf.ap()[d8 * 128:(d8 + 1) * 128, :],
                   f"wot{par}", reads=("woutbf", f"wot{par}"))
                ps, pk = take_ps()
                MMC(ps[:, :],
                    [(wbuf[:, eb * 128:(eb + 1) * 128],
                      ysl_v[:, eb * 512:(eb + 1) * 512]) for eb in range(16)],
                    (f"wot{par}", "ysl"), (pk,))
                TS(stile[:, :], ps[:, :], consts[:, 24 + d8:25 + d8], None, AL.add, AL.add,
                   (pk, "consts", "stile"), ("stile",))
                LD(resid.bitcast(BF16)[:, 512:1024],
                   ag1x.ap()[d8 * 128:(d8 + 1) * 128, tc * 512:(tc + 1) * 512],
                   "resid", reads=("ag1x", "resid"))
                TT(stile[:, :], stile[:, :], resid.bitcast(BF16)[:, 512:1024], AL.add,
                   ("stile", "resid"), ("stile",))
                TT(sgt[:, :], stile[:, :],
                   gfb_v[:, d8 * TPC + tc * 512: d8 * TPC + (tc + 1) * 512],
                   AL.mult, ("stile", "gfb", "sgt"), ("sgt",))
                for sub in range(4):
                    pst = psT if sub % 2 == 0 else psT2
                    pstk = "psT" if sub % 2 == 0 else "psT2"
                    TRANS(pst[:, :], sgt[:, sub * 128:(sub + 1) * 128], ("sgt",), (pstk,))
                    SCOPY(ottm_v[:, sub * 1024 + d8 * 128: sub * 1024 + (d8 + 1) * 128],
                          pst[:, :], (pstk, "ottm"), ("ottm",))
            for sub in range(4):
                it = tc * 4 + sub
                LD(xrow_v, xrows[it * 128:(it + 1) * 128, :], "xrow_v",
                   reads=("xrow_v",))
                TT(ottm_v[:, sub * 1024:(sub + 1) * 1024],
                   ottm_v[:, sub * 1024:(sub + 1) * 1024], xrow_v, AL.add,
                   ("xrow_v", "ottm"), ("ottm",))
                ST(out_rows[it * 128:(it + 1) * 128, :], ottm_v[:, sub * 1024:(sub + 1) * 1024],
                   "out", ("ottm",))

        if limit is None:
            P.finalize_outputs()
        else:
            P.limit = limit

        @block.tensor
        def _(e):
            P.replay("tensor", e, sems)

        @block.vector
        def _(e):
            P.replay("vector", e, sems)

        @block.scalar
        def _(e):
            P.replay("scalar", e, sems)

        @block.gpsimd
        def _(e):
            P.replay("gpsimd", e, sems)

        @block.sync
        def _(e):
            P.replay("sync", e, sems)

    return nc


def _in_maps(inputs):
    f32 = lambda a: np.ascontiguousarray(a, dtype=np.float32)
    x = f32(inputs["L0_attn_x"]).reshape(NCORE * TPC, D)
    W_in = f32(inputs["L0_attn_W_in"])
    b_in = f32(inputs["L0_attn_b_in"])
    W_delta = f32(inputs["L0_attn_W_delta"])
    b_delta = f32(inputs["L0_attn_b_delta"])
    W_dt = f32(inputs["L0_attn_W_dt"])
    b_dt = f32(inputs["L0_attn_b_dt"])
    W_B = f32(inputs["L0_attn_W_B"])
    b_B = f32(inputs["L0_attn_b_B"])
    W_C = f32(inputs["L0_attn_W_C"])
    b_C = f32(inputs["L0_attn_b_C"])
    A_log = f32(inputs["L0_attn_A_log"])
    Dv = f32(inputs["L0_attn_D"])
    W_out = f32(inputs["L0_attn_W_out"])
    b_out = f32(inputs["L0_attn_b_out"])
    ln_g = f32(inputs["L0_attn_ln_g"])
    ln_b = f32(inputs["L0_attn_ln_b"])
    gate_w = f32(inputs["L0_ffn_gate_w"])
    gate_b = f32(inputs["L0_ffn_gate_b"])
    w1 = f32(inputs["L0_ffn_w1"])
    b1 = f32(inputs["L0_ffn_b1"])
    w2 = f32(inputs["L0_ffn_w2"])
    b2 = f32(inputs["L0_ffn_b2"])

    maps = []
    for c in range(NCORE):
        b, h = c // 2, c % 2
        sl = slice(h * EH, (h + 1) * EH)
        win = np.concatenate([W_in[:, h * EH:(h + 1) * EH],
                              W_in[:, E + h * EH: E + (h + 1) * EH]], axis=1)
        binv = np.concatenate([b_in[h * EH:(h + 1) * EH],
                               b_in[E + h * EH: E + (h + 1) * EH]]).reshape(16, 128)
        wdbc = np.concatenate([W_delta[sl], W_B[sl], W_C[sl]], axis=1)
        rbcb = np.concatenate([b_delta, b_B, b_C]).reshape(96, 1)
        bases = np.array([[
            (2 * b) * D * TPC,
            (2 * b + 1) * D * TPC,
            (2 * b) * 96 * 512,
            (2 * b + 1) * 96 * 512,
            4 * h * 8 * EH * 512 + (2 * b) * EH * 512,
            4 * h * 8 * EH * 512 + (2 * b + 1) * EH * 512,
            0, 0]], dtype=np.int32)
        maps.append({
            "xrows": x[c * TPC:(c + 1) * TPC],
            "win_my": win,
            "bin_my": binv,
            "bdt_my": b_dt[sl].reshape(NBLK, 128),
            "wdbc_my": wdbc,
            "rbc_bias": rbcb,
            "wdt_my": np.ascontiguousarray(W_dt[:, sl]),
            "alog_my": A_log[sl],
            "d_my": Dv[sl].reshape(NBLK, 128),
            "wout_i": W_out,
            "bout_i": b_out.reshape(NBLK, 128),
            "lng_i": ln_g.reshape(NBLK, 128),
            "lnb_i": ln_b.reshape(NBLK, 128),
            "gwT": np.ascontiguousarray(gate_w.T),
            "gb_i": gate_b.reshape(NE, 1),
            "w1e": w1[c],
            "w2e": w2[c],
            "b1e": b1[c].reshape(32, 128),
            "b2e": b2[c].reshape(NBLK, 128),
            "prefmask": (np.arange(NE) < c).astype(np.float32).reshape(NE, 1),
            "myexp_i": np.full((128, 1), c, np.float32),
            "bases_i": bases,
        })
    return maps


_NC_CACHE = {}


def kernel(**inputs):
    if "nc" not in _NC_CACHE:
        _NC_CACHE["nc"] = build_nc()
    nc = _NC_CACHE["nc"]
    maps = _in_maps(inputs)
    res = run_bass_kernel_spmd(nc, maps, core_ids=list(range(NCORE)))
    out = np.concatenate([r["out_rows"] for r in res.results], axis=0)
    return out.reshape(4, 4096, 1024)


# version 20
# speedup vs baseline: 1.0031x; 1.0031x over previous
import numpy as np
import contextlib
import concourse.bass as bass
import concourse.mybir as mybir
from concourse.bass_utils import run_bass_kernel_spmd

F32 = mybir.dt.float32
BF16 = mybir.dt.bfloat16
U16 = mybir.dt.uint16
U32 = mybir.dt.uint32
I32 = mybir.dt.int32
AL = mybir.AluOpType
AF = mybir.ActivationFunctionType
AX = mybir.AxisListType.X

NCORE = 8
TPC = 2048
TB = 4096
D = 1024
E = 2048
EH = 1024
NBLK = 8
N = 16
R = 64
FF = 4096
CAP = 2560
NE = 8
QT = 512
NQ = TB // QT
NT = TPC // 128
EPS = 1e-5


class Prog:

    ENGINES = ("tensor", "vector", "scalar", "gpsimd", "sync")
    QPOOL = 10

    def __init__(self):
        self.stream = {e: [] for e in self.ENGINES}
        self.cnt = {}
        self.events = {}
        self.readers = {}
        self.seen = {e: {} for e in self.ENGINES}
        self.groups = {}
        self.rr = {}
        self.semcum = {}
        self.dma_sems = set()

    def _qsem(self, q, i):
        name = f"{q}_q{i}"
        self.dma_sems.add(name)
        return name

    def _open(self, q):
        g = self.groups.get(q)
        if g is not None and not g["closed"]:
            return g
        i = self.rr.get(q, 0)
        self.rr[q] = i + 1
        sem = self._qsem(q, i % self.QPOOL)
        base = self.semcum.get(sem, 0)
        if base > 0:
            self.gseq = getattr(self, "gseq", 0) + 1
            self.stream[q].append((None, [(sem, base)], None, self.gseq))
            self.seen[q][sem] = max(self.seen[q].get(sem, 0), base)
        g = {"sem": sem, "base": base, "n": 0, "closed": False}
        self.groups[q] = g
        return g

    def _close(self, g):
        if not g["closed"]:
            g["closed"] = True
            g["total"] = g["base"] + 16 * g["n"]
            self.semcum[g["sem"]] = g["total"]

    def emit(self, eng, fn, reads=(), writes=(), dma=False):
        import sys as _sys
        f = _sys._getframe(1)
        tag = f"{f.f_lineno}"
        if f.f_code.co_name in ("LD", "ST", "MM", "MMC", "SCOPY", "ACT", "COPY", "TT",
                                "TS", "STT", "RED", "MEMSET", "CC", "FENCE", "TRANS", "VOP"):
            f2 = f.f_back
            tag = f"{f.f_code.co_name}@{f2.f_lineno}"
        self.tags = getattr(self, "tags", [])
        waits = {}

        def need(ev):
            if ev is None:
                return
            if isinstance(ev, dict):
                self._close(ev)
                s, v = ev["sem"], ev["total"]
            else:
                s, v = ev
            if self.seen[eng].get(s, 0) >= v:
                return
            if waits.get(s, 0) < v:
                waits[s] = v

        for k in reads:
            need(self.events.get(k))
        for k in writes:
            need(self.events.get(k))
            for ev in self.readers.get(k, ()):
                need(ev)
        if dma:
            g = self._open(eng)
            g["n"] += 1
            ev = g
            inc = (g["sem"], 16)
        else:
            sem = eng
            self.cnt[sem] = self.cnt.get(sem, 0) + 1
            ev = (sem, self.cnt[sem])
            inc = (sem, 1)
        for k in reads:
            self.readers.setdefault(k, []).append(ev)
        for k in writes:
            self.events[k] = ev
            self.readers[k] = []
        for s, v in waits.items():
            self.seen[eng][s] = max(self.seen[eng].get(s, 0), v)
        self.gseq = getattr(self, "gseq", 0) + 1
        self.tags.append((self.gseq, eng, tag))
        self.stream[eng].append((fn, sorted(waits.items()), inc, self.gseq))

    def finalize_outputs(self):
        for q in ("gpsimd", "sync", "scalar"):
            g = self.groups.get(q)
            if g is not None:
                self._close(g)
            for i in range(self.QPOOL):
                sem = f"{q}_q{i}"
                tot = self.semcum.get(sem, 0)
                if tot > 0 and self.seen[q].get(sem, 0) < tot:
                    self.gseq = getattr(self, "gseq", 0) + 1
                    self.stream[q].append((None, [(sem, tot)], None, self.gseq))
                    self.seen[q][sem] = tot

    limit = None

    def replay(self, eng_name, e, sems):
        for fn, waits, inc, seq in self.stream[eng_name]:
            if self.limit is not None and seq > self.limit:
                continue
            for s, v in waits:
                e.wait_ge(sems[s], v)
            if fn is None:
                continue
            instr = fn(e)
            if inc is not None and instr is not None:
                instr.then_inc(sems[inc[0]], inc[1])


def build_nc(limit=None):
    nc = bass.Bass("TRN2", target_bir_lowering=False)

    def ein(name, shape, dt=F32):
        return nc.dram_tensor(name, shape, dt, kind="ExternalInput")

    xrows = ein("xrows", [TPC, D])
    win_my = ein("win_my", [D, 2048])
    bin_my = ein("bin_my", [16, 128])
    bdt_my = ein("bdt_my", [NBLK, 128])
    wdbc_my = ein("wdbc_my", [EH, 96])
    rbc_bias = ein("rbc_bias", [96, 1])
    wdt_my = ein("wdt_my", [R, EH])
    alog_my = ein("alog_my", [EH, N])
    d_my = ein("d_my", [NBLK, 128])
    wout_i = ein("wout_i", [E, D])
    bout_i = ein("bout_i", [NBLK, 128])
    lng_i = ein("lng_i", [NBLK, 128])
    lnb_i = ein("lnb_i", [NBLK, 128])
    gwT = ein("gwT", [D, NE])
    gb_i = ein("gb_i", [NE, 1])
    w1e = ein("w1e", [D, FF])
    w2e = ein("w2e", [FF, D])
    b1e = ein("b1e", [32, 128])
    b2e = ein("b2e", [NBLK, 128])
    prefmask = ein("prefmask", [NE, 1])
    myexp_i = ein("myexp_i", [128, 1])
    bases_i = ein("bases_i", [1, 8], I32)
    out_rows = nc.dram_tensor("out_rows", [TPC, D], F32, kind="ExternalOutput")

    dr = lambda name, shape, dt: nc.dram_tensor(name, shape, dt)
    ag1z = dr("ag1z", [D, TPC], BF16)
    ag1zout = dr("ag1zout", [NCORE * D, TPC], BF16)
    ag1x = dr("ag1x", [D, TPC], BF16)
    ag1xout = dr("ag1xout", [NCORE * D, TPC], BF16)
    cntin = dr("cntin", [1, NE], F32)
    cntout = dr("cntout", [NCORE, NE], F32)
    metain = dr("metain", [4, TPC], F32)
    metaout = dr("metaout", [NCORE * 4, TPC], F32)
    ag3in = dr("ag3in", [NQ * 96, QT], F32)
    ag3out = dr("ag3out", [NQ * NCORE * 96, QT], F32)
    ag4in = dr("ag4in", [D, CAP], BF16)
    ag4out = dr("ag4out", [NCORE * D, CAP], BF16)
    ag5in = dr("ag5in", [NQ * EH, QT], BF16)
    ag5out = dr("ag5out", [NQ * NCORE * EH, QT], BF16)
    u_sp = dr("u_sp", [EH, TB], BF16)
    gate_sp = dr("gate_sp", [EH, TB], BF16)
    lns_sp = dr("lns_sp", [EH, TB], BF16)
    winsp = dr("winsp", [E, D], BF16)
    w1bf = dr("w1bf", [FF, D], BF16)
    w2bf = dr("w2bf", [D, FF], BF16)
    woutbf = dr("woutbf", [D, E], BF16)
    xs_sp = dr("xs_sp", [D, CAP], BF16)
    myx2 = dr("myx2", [2 * EH, TPC], BF16)
    lv_hbm = dr("lv_hbm", [1, CAP], BF16)
    mrow_hbm = dr("mrow_hbm", [3, TPC], F32)

    P = Prog()

    import contextlib
    with contextlib.ExitStack() as _stack:
        _E = _stack.enter_context
        _stack.enter_context(nc.allow_non_contiguous_dma(
            reason="small transposed parameter loads / column stores"))
        onesf = _E(nc.sbuf_tensor("onesf", [128, 8], F32))
        ident_f = _E(nc.sbuf_tensor("ident_f", [128, 128], F32))
        tri_f = _E(nc.sbuf_tensor("tri_f", [128, 128], F32))
        ones1b = _E(nc.sbuf_tensor("ones1b", [1, 128], BF16))
        nones1b = _E(nc.sbuf_tensor("nones1b", [1, 128], BF16))
        ones1f = _E(nc.sbuf_tensor("ones1f", [1, 128], F32))
        iota8 = _E(nc.sbuf_tensor("iota8", [128, 8], F32))
        consts = _E(nc.sbuf_tensor("consts", [128, 64], F32))
        b1_s = _E(nc.sbuf_tensor("b1_s", [128, 32], F32))
        b2_s = _E(nc.sbuf_tensor("b2_s", [128, 8], F32))
        gwT_f = _E(nc.sbuf_tensor("gwT_f", [128, 64], F32))
        gb_s = _E(nc.sbuf_tensor("gb_s", [8, 1], F32))
        rbcb_s = _E(nc.sbuf_tensor("rbcb_s", [96, 1], F32))
        prefm_s = _E(nc.sbuf_tensor("prefm_s", [8, 1], F32))
        myexp_s = _E(nc.sbuf_tensor("myexp_s", [128, 1], F32))
        iotar = _E(nc.sbuf_tensor("iotar", [128, 128], F32))
        onescb = _E(nc.sbuf_tensor("onescb", [128, 2], BF16))
        bases_s = _E(nc.sbuf_tensor("bases_s", [1, 8], I32))
        nega = _E(nc.sbuf_tensor("nega", [128, NBLK * N], F32))
        fencebuf = _E(nc.sbuf_tensor("fencebuf", [1, 4], F32))
        cvals = _E(nc.sbuf_tensor("cvals", [128, 8], F32))
        ARBIG = _E(nc.sbuf_tensor("ARBIG", [128, 10240], F32))
        ARB2 = _E(nc.sbuf_tensor("ARB2", [128, 16384], BF16))
        AR2 = _E(nc.sbuf_tensor("AR2", [128, 4096], F32))
        ARX = _E(nc.sbuf_tensor("ARX", [128, 4096], BF16))
        ARW = _E(nc.sbuf_tensor("ARW", [128, 8192], BF16))
        ARE = _E(nc.sbuf_tensor("ARE", [128, 2048], F32))
        bmcm_b = _E(nc.sbuf_tensor("bmcm_b", [32, TB], BF16))
        castA = _E(nc.sbuf_tensor("castA", [128, 1024], F32))
        castB = _E(nc.sbuf_tensor("castB", [128, 1024], F32))
        moepk = _E(nc.sbuf_tensor("moepk", [16, 4096], F32))
        tokid = _E(nc.sbuf_tensor("tokid", [16, 1024], F32))
        comp16 = _E(nc.sbuf_tensor("comp16", [16, CAP // 16], U16))
        wsc = _E(nc.sbuf_tensor("wsc", [64, 2048], BF16))
        cidx = _E(nc.sbuf_tensor("cidx", [128, CAP // 16], U16))
        gidx16 = _E(nc.sbuf_tensor("gidx16", [128, 384], U16))
        selmat = _E(nc.sbuf_tensor("selmat", [32, 4096], BF16))
        w1t = _E(nc.sbuf_tensor("w1t", [128, 2048], BF16))
        wot = _E(nc.sbuf_tensor("wot", [128, 2048], BF16))
        lns_t = _E(nc.sbuf_tensor("lns_t", [128, 2 * QT], BF16))
        u_t = _E(nc.sbuf_tensor("u_t", [128, 2 * QT], BF16))
        gate_t = _E(nc.sbuf_tensor("gate_t", [128, 2 * QT], BF16))
        ndu = _E(nc.sbuf_tensor("ndu", [128, QT], BF16))
        b_t = _E(nc.sbuf_tensor("b_t", [128, QT], BF16))
        ygt = _E(nc.sbuf_tensor("ygt", [128, 2 * QT], BF16))
        rstat = _E(nc.sbuf_tensor("rstat", [128, 128], F32))
        carryc = _E(nc.sbuf_tensor("carryc", [128, NBLK * N], F32))
        lg_tm = _E(nc.sbuf_tensor("lg_tm", [128, NT * 8], F32))
        oh_tm = _E(nc.sbuf_tensor("oh_tm", [128, NT * 8], F32))
        pos_tm = _E(nc.sbuf_tensor("pos_tm", [128, NT * 8], F32))
        rcols = _E(nc.sbuf_tensor("rcols", [128, NT * 10], F32))
        base8 = _E(nc.sbuf_tensor("base8", [128, 8], F32))
        cnt_s = _E(nc.sbuf_tensor("cnt_s", [8, 8], F32))
        rtmp8 = _E(nc.sbuf_tensor("rtmp8", [128, 8], F32))
        stile = _E(nc.sbuf_tensor("stile", [128, 512], F32))
        resid = _E(nc.sbuf_tensor("resid", [128, 512], F32))
        sgt = _E(nc.sbuf_tensor("sgt", [128, 512], F32))
        x2stage = _E(nc.sbuf_tensor("x2stage", [128, 1024], BF16))
        xnstage = _E(nc.sbuf_tensor("xnstage", [128, 1024], BF16))
        psA = _E(nc.psum_tensor("psA", [128, 512], F32))
        psB = _E(nc.psum_tensor("psB", [128, 512], F32))
        psC = _E(nc.psum_tensor("psC", [128, 512], F32))
        ps96 = _E(nc.psum_tensor("ps96", [96, 512], F32))
        psT = _E(nc.psum_tensor("psT", [128, 128], F32))
        psT2 = _E(nc.psum_tensor("psT2", [128, 128], F32))
        psY = _E(nc.psum_tensor("psY", [128, 512], F32))
        ps8 = _E(nc.psum_tensor("ps8", [128, 8], F32))
        s_tensor = _E(nc.semaphore("s_tensor"))
        s_vector = _E(nc.semaphore("s_vector"))
        s_scalar = _E(nc.semaphore("s_scalar"))
        s_gpsimd = _E(nc.semaphore("s_gpsimd"))
        s_sync = _E(nc.semaphore("s_sync"))
        qsems = {}
        for q in ("sync", "gpsimd", "scalar"):
            for i in range(Prog.QPOOL):
                qsems[f"{q}_q{i}"] = _E(nc.semaphore(f"{q}_q{i}"))
        block = _E(nc.Block())
        sems = {
            "tensor": s_tensor, "vector": s_vector, "scalar": s_scalar,
            "gpsimd": s_gpsimd, "sync": s_sync,
        }
        sems.update(qsems)

        xtA = ARBIG[:, 0:1024]
        xtB = ARBIG[:, 1024:2048]
        sqA = ARBIG[:, 2048:3072]
        sqB = ARBIG[:, 5120:6144]
        lg_fm = ARBIG[0:8, 3072:5120]
        rbct_v = ARBIG[0:96, 0:4096]
        slab_v = ARBIG[:, 0:10240].bitcast(BF16)
        bbc_v = ARBIG[:, 0:4096].bitcast(BF16)
        cbc_v = ARBIG[:, 4096:8192].bitcast(BF16)
        wpar_v = ARBIG.bitcast(BF16)[:, 16384:18432]
        diagD_v = ARBIG.bitcast(BF16)[:, 18432:19456]
        identb_v = ARBIG.bitcast(BF16)[:, 19456:19584]
        slabx_v = ARB2
        h1_v = ARB2
        gfb_v = ARB2
        a_v = AR2[:, 0:1024]
        art = [AR2[:, 1024:1536], AR2[:, 1536:2048],
               AR2[:, 3584:4096]]
        yacc_v = AR2[:, 3072:3584]
        h_v = AR2.bitcast(BF16)[:, 4096:5120]
        ymul_v = AR2.bitcast(BF16)[:, 5120:6144]
        gvm_bc = AR2.bitcast(BF16)
        ysl_v = AR2[:, 0:4096].bitcast(BF16)
        xsf_v = ARX[:, 0:CAP]
        xssl_v = ARX
        gacc_v = ARX[:, 0:2048]
        gtmp_v = ARX[:, 2048:4096]
        xrow_v = ARX.bitcast(F32)[:, 0:1024]
        x2sl_v = ARW[:, 0:4096]
        w2t_v = ARW[:, 4096:8192]
        ottm_v = ARW.bitcast(F32)
        uev_v = ARE.bitcast(BF16)
        rbcev_v = castB[0:96, 0:1024]
        ffev_v = sgt.bitcast(BF16)[:, 0:512]
        rbf2_v = wsc[0:64, 1024:1536]
        gvrow = moepk.bitcast(BF16)[0:1, 4096:6144]
        dsgb = [stile.bitcast(BF16)[:, 0:512],
                resid.bitcast(BF16)[:, 0:512]]
        midx_v = moepk[:, 0:1024]
        mslot_v = moepk[:, 1024:2048]
        mkeep_v = moepk[:, 2048:3072]
        mwork_v = moepk[:, 3072:4096]

        def LD(dst, src, key, reads=(), eng="sync"):
            P.emit(eng, lambda e, d=dst, s=src: e.dma_start(out=d, in_=s),
                   reads=tuple(reads), writes=(key,), dma=True)

        def ST(dst, src, key, reads, eng="gpsimd"):
            P.emit(eng, lambda e, d=dst, s=src: e.dma_start(out=d, in_=s),
                   reads=tuple(reads), writes=(key,), dma=True)

        def MM(outp, lhsT, rhs, reads, writes, start, stop):
            P.emit("tensor",
                   lambda e, o=outp, l=lhsT, r=rhs, s1=start, s2=stop:
                   e.matmul(o, l, r, start=s1, stop=s2),
                   reads=tuple(reads), writes=tuple(writes))

        def MMC(outp, pairs, reads, writes, first=True, last=True):
            def fn(e, o=outp, ps=tuple(pairs), first=first, last=last):
                n = len(ps)
                for i, (l, r) in enumerate(ps):
                    mm = e.matmul(o, l, r, start=(first and i == 0),
                                  stop=(last and i == n - 1))
                return mm
            P.emit("tensor", fn, reads=tuple(reads), writes=tuple(writes))

        def SCOPY(outp, inp, reads, writes):
            P.emit("scalar", lambda e, o=outp, i=inp: e.copy(out=o, in_=i),
                   reads=tuple(reads), writes=tuple(writes))

        def TRANS(outp, inp, reads, writes):
            P.emit("tensor",
                   lambda e, o=outp, i=inp: e.transpose(o, i, ident_f[:, :]),
                   reads=tuple(reads) + ("ident_f",), writes=tuple(writes))

        def ACT(outp, inp, func, reads, writes, bias=0.0, scale=1.0):
            P.emit("scalar",
                   lambda e, o=outp, i=inp, f=func, b=bias, s=scale:
                   e.activation(out=o, in_=i, func=f, bias=b, scale=s),
                   reads=tuple(reads), writes=tuple(writes))

        def COPY(outp, inp, reads, writes, eng="vector"):
            P.emit(eng, lambda e, o=outp, i=inp: e.tensor_copy(out=o, in_=i),
                   reads=tuple(reads), writes=tuple(writes))

        def TT(outp, in0, in1, op, reads, writes, eng="vector"):
            P.emit(eng, lambda e, o=outp, a=in0, b=in1, p=op:
                   e.tensor_tensor(out=o, in0=a, in1=b, op=p),
                   reads=tuple(reads), writes=tuple(writes))

        def TS(outp, in0, s1, s2, op0, op1, reads, writes, eng="vector"):
            if s2 is None:
                op1 = AL.bypass
            P.emit(eng, lambda e, o=outp, a=in0, x=s1, y=s2, p=op0, q=op1:
                   e.tensor_scalar(out=o, in0=a, scalar1=x, scalar2=y, op0=p, op1=q),
                   reads=tuple(reads), writes=tuple(writes))

        def STT(outp, in0, scal, in1, op0, op1, reads, writes, eng="vector"):
            P.emit(eng, lambda e, o=outp, a=in0, s=scal, b=in1, p=op0, q=op1:
                   e.scalar_tensor_tensor(out=o, in0=a, scalar=s, in1=b, op0=p, op1=q),
                   reads=tuple(reads), writes=tuple(writes))

        def RED(outp, inp, op, reads, writes, negate=False):
            P.emit("vector", lambda e, o=outp, i=inp, p=op, n=negate:
                   e.tensor_reduce(o, i, axis=AX, op=p, negate=n),
                   reads=tuple(reads), writes=tuple(writes))

        def MEMSET(ap, val, writes, eng="vector"):
            P.emit(eng, lambda e, a=ap, v=val: e.memset(a, v), reads=(), writes=tuple(writes))

        def CC(kind, ins_ap, outs_ap, reads, writes):
            P.emit("gpsimd", lambda e, i=ins_ap, u=outs_ap, k=kind:
                   e.collective_compute(k, AL.bypass, replica_groups=[list(range(NCORE))],
                                        ins=[i], outs=[u]),
                   reads=tuple(reads), writes=tuple(writes))
            v = P.cnt["gpsimd"]
            P.gseq = getattr(P, "gseq", 0) + 1
            P.stream["gpsimd"].append((None, [("gpsimd", v)], None, P.gseq))
            P.seen["gpsimd"]["gpsimd"] = max(P.seen["gpsimd"].get("gpsimd", 0), v)

        def FENCE(new_keys, old_keys):
            if isinstance(new_keys, str):
                new_keys = (new_keys,)
            P.emit("vector", lambda e: e.memset(fencebuf[:, :], 0.0),
                   reads=(), writes=tuple(new_keys) + tuple(old_keys) + ("fencebuf",))

        _reg_cache = {}

        def dyn_ld(dst, handle, base_idx, const_off, apdims, key, reads=()):
            def fn(e, d=dst, h=handle, bi=base_idx, co=const_off, ap=apdims):
                rk = id(e)
                if rk not in _reg_cache:
                    _reg_cache[rk] = e.alloc_register("dynoff")
                r = _reg_cache[rk]
                e.reg_load(r, bases_s[0:1, bi:bi + 1])
                e.reg_add(r, r, co)
                try:
                    return e.dma_start(out=d, in_=bass.AP(h, r, ap))
                except Exception as ex:
                    print("DYN FAIL bi=", bi, "co=", co, ap, d.offset, ex)
                    raise
            P.emit("sync", fn, reads=tuple(reads) + ("bases_s",), writes=(key,), dma=True)

        MEMSET(onesf[:, :], 1.0, ("onesf",))
        MEMSET(cvals[:, 0:1], EPS, ("cvals",))
        MEMSET(cvals[:, 1:2], -0.5, ("cvals",))
        MEMSET(cvals[:, 2:3], -1.0, ("cvals",))
        MEMSET(cvals[:, 3:4], 1.0 / D, ("cvals",))
        MEMSET(cvals[:, 4:5], float(CAP), ("cvals",))
        MEMSET(ones1b[:, :], 1.0, ("ones1b",))
        MEMSET(nones1b[:, :], -1.0, ("nones1b",))
        MEMSET(ones1f[:, :], 1.0, ("ones1f",))
        MEMSET(carryc[:, :], 0.0, ("carryc",))
        P.emit("gpsimd", lambda e: e.affine_select(
            ident_f[:, :], onesf[:, 0:1].broadcast_to((128, 128)), pattern=[[1, 128]],
            base=0, channel_multiplier=-1, compare_op=AL.is_equal, fill=0.0),
            reads=("onesf",), writes=("ident_f",))
        P.emit("gpsimd", lambda e: e.affine_select(
            tri_f[:, :], onesf[:, 0:1].broadcast_to((128, 128)), pattern=[[1, 128]],
            base=0, channel_multiplier=-1, compare_op=AL.is_ge, fill=0.0),
            reads=("onesf",), writes=("tri_f",))
        P.emit("gpsimd", lambda e: e.iota(
            iota8[:, :], pattern=[[1, 8]], base=0, channel_multiplier=0,
            allow_small_or_imprecise_dtypes=True), reads=(), writes=("iota8",))
        P.emit("gpsimd", lambda e: e.iota(
            tokid[:, :], pattern=[[16, 1024]], base=0, channel_multiplier=1,
            allow_small_or_imprecise_dtypes=True), reads=(), writes=("tokid",))
        MEMSET(ARB2[0:32, 0:4096], 1.0, ("h1",))
        MEMSET(onescb[:, :], 1.0, ("onescb",))
        P.emit("gpsimd", lambda e: e.iota(
            iotar[:, :], pattern=[[1, 128]], base=0, channel_multiplier=0,
            allow_small_or_imprecise_dtypes=True), reads=(), writes=("iotar",))
        P.emit("gpsimd", lambda e: e.affine_select(
            selmat[:, :], ARB2[0:32, 0:4096], pattern=[[1, 32], [0, 128]], base=0,
            channel_multiplier=-1, compare_op=AL.is_equal, fill=0.0),
            reads=("h1",), writes=("selmat",))

        LD(consts[:, 0:16], bin_my.ap().transpose([1, 0]), "consts", eng="gpsimd")
        LD(consts[:, 56:64], bdt_my.ap().transpose([1, 0]), "consts", eng="gpsimd")
        LD(consts[:, 24:32], bout_i.ap().transpose([1, 0]), "consts", eng="gpsimd")
        LD(consts[:, 32:40], d_my.ap().transpose([1, 0]), "consts", eng="gpsimd")
        LD(consts[:, 40:48], lng_i.ap().transpose([1, 0]), "consts", eng="gpsimd")
        LD(consts[:, 48:56], lnb_i.ap().transpose([1, 0]), "consts", eng="gpsimd")
        TS(consts[:, 16:24], consts[:, 56:64], -1.0, None, AL.mult, AL.add,
           ("consts",), ("consts",))
        LD(b1_s[:, :], b1e.ap().transpose([1, 0]), "b1_s", eng="gpsimd")
        LD(b2_s[:, :], b2e.ap().transpose([1, 0]), "b2_s", eng="gpsimd")
        LD(gb_s[:, :], gb_i.ap(), "gb_s")
        LD(rbcb_s[:, :], rbc_bias.ap(), "rbcb_s")
        LD(prefm_s[:, :], prefmask.ap(), "prefm_s")
        LD(myexp_s[:, :], myexp_i.ap(), "myexp_s")
        LD(bases_s[:, :], bases_i.ap(), "bases_s")
        for k in range(8):
            LD(gwT_f[:, k * 8:(k + 1) * 8], gwT[k * 128:(k + 1) * 128, :], "gwT_f")
        for i in range(NBLK):
            LD(castA[:, i * N:(i + 1) * N], alog_my[i * 128:(i + 1) * 128, :], "castA")
        ACT(nega[:, :], castA[:, 0:NBLK * N], AF.Exp, ("castA",), ("nega",))

        for it in range(NT):
            xt = xtA if it % 2 == 0 else xtB
            kx = "xtA" if it % 2 == 0 else "xtB"
            sq = sqA if it % 2 == 0 else sqB
            ks = f"sq{it % 2}"
            st = lambda j: rstat[:, it * 8 + j: it * 8 + j + 1]
            kt = f"st{it}"
            LD(xt, xrows[it * 128:(it + 1) * 128, :], kx)
            for ln2 in range(2):
                RED(st(0), xt, AL.add, (kx,), (kt,))
                TT(sq, xt, xt, AL.mult, (kx, ks), (ks,), eng="gpsimd")
                RED(st(1), sq, AL.add, (ks,), (kt,))
                TS(st(2), st(0), 1.0 / D, None, AL.mult, AL.add, (kt,), (kt,))
                TT(st(3), st(2), st(2), AL.mult, (kt,), (kt,))
                STT(st(4), st(1), cvals[:, 3:4], st(3), AL.mult,
                    AL.subtract, (kt, "cvals"), (kt,))
                ACT(st(5), st(4), AF.Ln, (kt, "cvals"), (kt,), bias=cvals[:, 0:1])
                ACT(st(6), st(5), AF.Exp, (kt, "cvals"), (kt,), scale=cvals[:, 1:2])
                if ln2 == 0:
                    TS(xt, xt, st(2), st(6), AL.subtract, AL.mult, (kx, kt), (kx,))
                else:
                    TS(sq, xt, st(2), st(6), AL.subtract, AL.mult, (kx, kt), (ks,))
            for fb in range(8):
                cbs = castB[:, fb * 128:(fb + 1) * 128]
                kcb, kxn, kx2 = f"cB{fb}", f"xns{fb}", f"x2s{fb}"
                pcr = psC[:, (fb % 4) * 128:(fb % 4 + 1) * 128]
                kpc = f"psC{fb % 4}"
                pt2 = psT if fb % 2 == 0 else psT2
                kpt = "psT" if fb % 2 == 0 else "psT2"
                P.emit("tensor", lambda e, pcr=pcr, xt=xt, fb=fb: e.transpose(
                    pcr, xt[:, fb * 128:(fb + 1) * 128], ident_f[:, :]),
                    reads=(kx, "ident_f", kpc), writes=(kpc,))
                COPY(cbs, pcr, (kpc, kcb), (kcb,))
                SCOPY(xnstage[:, fb * 128:(fb + 1) * 128], cbs, (kcb, kxn), (kxn,))
                MM(psA[0:8, 0:128], gwT_f[:, fb * 8:(fb + 1) * 8], cbs,
                   ("gwT_f", kcb), ("psA",), start=(fb == 0), stop=(fb == 7))
                P.emit("tensor", lambda e, pt2=pt2, sq=sq, fb=fb: e.transpose(
                    pt2[:, :], sq[:, fb * 128:(fb + 1) * 128], ident_f[:, :]),
                    reads=(ks, "ident_f", kpt), writes=(kpt,))
                P.emit("vector", lambda e, fb=fb, pt2=pt2: e.tensor_scalar(
                    out=x2stage[:, fb * 128:(fb + 1) * 128], in0=pt2[:, :],
                    scalar1=consts[:, 40 + fb:41 + fb], scalar2=consts[:, 48 + fb:49 + fb],
                    op0=AL.mult, op1=AL.add),
                    reads=(kpt, "consts", kx2), writes=(kx2,))
            ST(bass.AP(ag1x, it * 128, [[TPC, 128], [128 * TPC, 8], [1, 128]]),
               xnstage[:, :], "ag1x", tuple(f"xns{fb}" for fb in range(8)))
            ST(bass.AP(ag1z, it * 128, [[TPC, 128], [128 * TPC, 8], [1, 128]]),
               x2stage[:, :], "ag1z", tuple(f"x2s{fb}" for fb in range(8)))
            TS(lg_fm[:, it * 128:(it + 1) * 128], psA[0:8, 0:128], gb_s[:, 0:1], None,
               AL.add, AL.add, ("psA", "gb_s"), ("lg_fm",))

        for it in range(NT):
            Ccol = lambda j, it=it: rcols[:, it * 10 + j: it * 10 + j + 1]
            ohs = oh_tm[:, it * 8:(it + 1) * 8]
            lgs = lg_tm[:, it * 8:(it + 1) * 8]
            P.emit("tensor", lambda e, it=it: e.transpose(
                psT[:, 0:8], lg_fm[:, it * 128:(it + 1) * 128], ident_f[0:8, 0:8]),
                reads=("lg_fm", "ident_f"), writes=("psT",))
            COPY(lgs, psT[:, 0:8], ("psT",), ("lg_tm",))
            RED(Ccol(0), lgs, AL.max, ("lg_tm",), (f"r{it}",), negate=True)
            ACT(ohs, lgs, AF.Exp, ("lg_tm", f"r{it}"), ("oh_tm",), bias=Ccol(0))
            RED(Ccol(1), ohs, AL.add, ("oh_tm",), (f"r{it}",))
            P.emit("vector", lambda e, it=it: e.reciprocal(
                rcols[:, it * 10 + 2: it * 10 + 3], rcols[:, it * 10 + 1: it * 10 + 2]),
                reads=(f"r{it}",), writes=(f"r{it}",))
            TS(Ccol(3), Ccol(0), -1.0, None, AL.mult, AL.add, (f"r{it}",), (f"r{it}",))
            TS(ohs, lgs, Ccol(3), None, AL.is_equal, AL.add,
               ("lg_tm", f"r{it}"), ("oh_tm",))
            TT(pos_tm[:, it * 8:(it + 1) * 8], ohs, iota8[:, :], AL.mult,
               ("oh_tm", "iota8"), ("pos_tm",))
            RED(Ccol(4), pos_tm[:, it * 8:(it + 1) * 8], AL.add, ("pos_tm",), (f"r{it}",))
            MM(psT2[0:8, it:it + 1], ohs, onesf[:, 0:1], ("oh_tm", "onesf"), ("psT2",),
               start=True, stop=True)
        cnts_c = moepk[0:8, 3968:3984]
        cnts16 = moepk[0:16, 3840:3848]
        mb16 = moepk[0:16, 3848:3856]
        mrow16 = moepk[0:1, 3856:3984]
        COPY(cnts_c, psT2[0:8, 0:16], ("psT2", "cnts16"), ("cnts16",))
        P.emit("tensor", lambda e: e.transpose(
            psT[0:16, 0:8], cnts_c, ident_f[0:8, 0:8]),
            reads=("cnts16", "ident_f"), writes=("psT",))
        COPY(cnts16, psT[0:16, 0:8], ("psT", "cnts16"), ("cnts16",))
        MM(ps8[0:1, 0:8], onesf[0:16, 0:1], cnts16, ("onesf", "cnts16"), ("ps8",),
           start=True, stop=True)
        COPY(rtmp8[0:1, 0:8], ps8[0:1, 0:8], ("ps8",), ("cntcol",))
        ST(cntin.ap(), rtmp8[0:1, 0:8], "cntin", ("cntcol",))

        CC("AllGather", cntin.ap().opt(), cntout.ap().opt(), ("cntin",), ("cntout",))
        LD(cnt_s[:, :], cntout.ap(), "cnt_s", reads=("cntout",))
        MM(ps8[0:1, 0:8], prefm_s[:, :], cnt_s[:, :], ("prefm_s", "cnt_s"), ("ps8",),
           start=True, stop=True)
        COPY(rtmp8[0:1, 0:8], ps8[0:1, 0:8], ("ps8",), ("baserow",))
        MM(ps8[0:128, 0:8], ones1f[0:1, 0:128], rtmp8[0:1, 0:8], ("ones1f", "baserow"),
           ("ps8",), start=True, stop=True)
        COPY(base8[:, :], ps8[0:128, 0:8], ("ps8",), ("base8",))
        MM(psT2[0:16, 0:8], tri_f[0:16, 0:16], cnts16, ("tri_f", "cnts16"), ("psT2",),
           start=True, stop=True)
        TT(mb16, psT2[0:16, 0:8], cnts16, AL.subtract, ("psT2", "cnts16", "mb16"),
           ("mb16",))
        TT(mb16, mb16, base8[0:16, 0:8], AL.add, ("mb16", "base8"), ("mb16",))
        for it in range(NT):
            LD(mrow16[0:1, it * 8:(it + 1) * 8], mb16[it:it + 1, 0:8], "mrow16",
               reads=("mb16", "mrow16"), eng="sync")

        for it in range(NT):
            Ccol = lambda j, it=it: rcols[:, it * 10 + j: it * 10 + j + 1]
            ohs = oh_tm[:, it * 8:(it + 1) * 8]
            MM(psT2[:, 0:8], tri_f[:, :], ohs, ("tri_f", "oh_tm"), ("psT2",),
               start=True, stop=True)
            COPY(pos_tm[:, it * 8:(it + 1) * 8], psT2[:, 0:8], ("psT2",), ("pos_tm",))
            MM(ps8[0:128, 0:8], ones1f[0:1, 0:128], mrow16[0:1, it * 8:(it + 1) * 8],
               ("ones1f", "mrow16"), ("ps8",), start=True, stop=True)
            TT(rtmp8[:, :], pos_tm[:, it * 8:(it + 1) * 8], ps8[0:128, 0:8], AL.add,
               ("pos_tm", "ps8"), ("rtmp8",))
            TS(rtmp8[:, :], rtmp8[:, :], 1.0, None, AL.subtract, AL.add,
               ("rtmp8",), ("rtmp8",))
            TT(rtmp8[:, :], rtmp8[:, :], ohs, AL.mult, ("rtmp8", "oh_tm"), ("rtmp8",))
            RED(Ccol(9), rtmp8[:, :], AL.add, ("rtmp8",), (f"r{it}",))
            TS(Ccol(6), Ccol(9), float(CAP), None, AL.is_lt, AL.add, (f"r{it}",), (f"r{it}",))
            TT(Ccol(5), Ccol(9), Ccol(6), AL.mult, (f"r{it}",), (f"r{it}",))
            TT(Ccol(7), Ccol(2), Ccol(6), AL.mult, (f"r{it}",), (f"r{it}",))
            P.emit("tensor", lambda e, it=it: e.transpose(
                psT[0:4, 0:128], rcols[:, it * 10 + 4: it * 10 + 8], ident_f[:, :]),
                reads=(f"r{it}", "ident_f"), writes=("psT",))
            mst = moepk[0:4, 3584 + (it % 2) * 128: 3712 + (it % 2) * 128]
            SCOPY(mst, psT[0:4, 0:128], ("psT", f"mst{it % 2}"), (f"mst{it % 2}",))
            ST(metain.ap()[0:4, it * 128:(it + 1) * 128], mst, "metain", (f"mst{it % 2}",))
        CC("AllGather", ag1z.ap().opt(), ag1zout.ap().opt(), ("ag1z",), ("ag1zout",))
        CC("AllGather", metain.ap().opt(), metaout.ap().opt(), ("metain",), ("metaout",))
        CC("AllGather", ag1x.ap().opt(), ag1xout.ap().opt(), ("ag1x",), ("ag1xout",))
        for half in range(2):
            dyn_ld(myx2.ap()[half * EH:(half + 1) * EH, :], ag1zout,
                   half, 0, [[TPC, EH], [1, TPC]], "myx2", reads=("ag1zout",))

        PRI = []
        BULK = []
        _ps = [0]

        def take_ps():
            i = _ps[0] % 2
            _ps[0] += 1
            return (psA, "psA") if i == 0 else (psC, "psC")

        def pop_fill():
            if BULK:
                BULK.pop(0)()

        def inproj_pieces(ck):
            half = ck // 4
            tcl = ck % 4
            ps_list = []

            def p_x2sl(ck=ck, half=half, tcl=tcl):
                for k in range(8):
                    LD(x2sl_v[:, k * 512:(k + 1) * 512],
                       myx2.ap()[half * EH + k * 128: half * EH + (k + 1) * 128,
                                 tcl * 512:(tcl + 1) * 512],
                       "x2sl", reads=("myx2", "x2sl"))
            ps_list.append(p_x2sl)
            for m in range(16):
                def p_m(ck=ck, m=m):
                    par = m % 2
                    LD(wpar_v[:, par * 1024:(par + 1) * 1024],
                       winsp.ap()[m * 128:(m + 1) * 128, :], f"wpar{par}",
                       reads=("winsp", f"wpar{par}"))
                    ps, pk = take_ps()
                    MMC(ps[:, :],
                        [(wpar_v[:, par * 1024 + k * 128: par * 1024 + (k + 1) * 128],
                          x2sl_v[:, k * 512:(k + 1) * 512]) for k in range(8)],
                        (f"wpar{par}", "x2sl"), (pk,))
                    ev = uev_v[:, (m % 8) * 512:(m % 8 + 1) * 512]
                    evk = f"uev{m % 8}"
                    ACT(ev, ps[:, :], AF.Silu, (pk, "consts", evk), (evk,),
                        bias=consts[:, m:m + 1])
                    if m == 7:
                        ST(bass.AP(u_sp, ck * 512, [[TB, 128], [128 * TB, 8], [1, 512]]),
                           uev_v[:, 0:4096], "u_sp", tuple(f"uev{j}" for j in range(8)))
                        MMC(ps96[:, :],
                            [(wot[:, j * 96:(j + 1) * 96], uev_v[:, j * 512:(j + 1) * 512])
                             for j in range(8)],
                            ("wot",) + tuple(f"uev{j}" for j in range(8)), ("ps96",))
                        ev96 = rbcev_v[:, (ck % 2) * 512:(ck % 2 + 1) * 512]
                        COPY(ev96, ps96[:, :], ("ps96", f"rbcev{ck % 2}"),
                             (f"rbcev{ck % 2}",))
                        ST(ag3in[ck * 96:(ck + 1) * 96, :], ev96, "ag3in",
                           (f"rbcev{ck % 2}",))
                    if m == 15:
                        ST(bass.AP(gate_sp, ck * 512, [[TB, 128], [128 * TB, 8], [1, 512]]),
                           uev_v[:, 0:4096], "gate_sp", tuple(f"uev{j}" for j in range(8)))
                ps_list.append(p_m)
            return ps_list

        def ag3cc(ck):
            CC("AllGather", ag3in.ap()[ck * 96:(ck + 1) * 96, :].opt(),
               ag3out.ap()[ck * 768:(ck + 1) * 768, :].opt(),
               ("ag3in",), (f"ag3o{ck}",))

        def rbc(ck):
            dyn_ld(AR2[0:96, 1024:2048], ag3out, 2, ck * 8 * 96 * 512,
                   [[512, 96], [96 * 512, 2], [1, 512]], "art0",
                   reads=(f"ag3o{ck}", "art0", "art1"))
            TT(art[0][0:96, :], art[0][0:96, :], art[1][0:96, :], AL.add,
               ("art0",), ("art0", "art1"))
            TS(art[0][0:96, :], art[0][0:96, :], rbcb_s[:, :], None, AL.add, AL.add,
               ("art0", "rbcb_s"), ("art0",))
            COPY(rbf2_v, art[0][0:64, :], ("art0", "rbf2"), ("rbf2",))
            COPY(bmcm_b[:, ck * 512:(ck + 1) * 512], art[0][64:96, :],
                 ("art0", "bmcm"), ("bmcm",))
            TS(bmcm_b[0:16, ck * 512:(ck + 1) * 512],
               bmcm_b[0:16, ck * 512:(ck + 1) * 512], -1.0, None, AL.mult, AL.add,
               ("bmcm",), ("bmcm",))

        def delta(ck):
            groups = [(0, 3), (3, 3), (6, 2)]
            for g0, gn in groups:
                for j in range(gn):
                    blk = g0 + j
                    ps, pk = take_ps()
                    MM(ps[:, :], wsc[0:64, blk * 128:(blk + 1) * 128], rbf2_v,
                       ("wsc", "rbf2"), (pk,), start=True, stop=True)
                    ACT(art[j], ps[:, :], AF.Sigmoid,
                        (pk, "consts", "cvals", f"art{j}"), (f"art{j}",),
                        bias=consts[:, 16 + blk:17 + blk], scale=cvals[:, 2:3])
                for j in range(gn):
                    blk = g0 + j
                    db = dsgb[j % 2]
                    kdb = f"dsgb{j % 2}"
                    ACT(db, art[j], AF.Ln, (f"art{j}", kdb), (kdb,))
                    ST(lns_sp[blk * 128:(blk + 1) * 128, ck * 512:(ck + 1) * 512], db,
                       "lns_sp", (kdb,), eng="sync")

        def gather_pieces():
            out = []

            def p_f():
                FENCE(("slabx",), ("llistbc", "xsf"))
                FENCE(("xsfo",), ())
            out.append(p_f)
            for fb in range(8):
                def p_fb(fb=fb):
                    for r8 in range(NCORE):
                        LD(slabx_v[:, r8 * TPC:(r8 + 1) * TPC],
                           ag1xout.ap()[r8 * D + fb * 128: r8 * D + (fb + 1) * 128, :],
                           f"slx{r8}", reads=("ag1xout", f"slx{r8}"))
                    for part in range(3):
                        w = 1024 if part < 2 else 512
                        P.emit("gpsimd", lambda e, part=part, w=w: e.indirect_copy(
                            xsf_v[:, part * 1024: part * 1024 + w],
                            slabx_v[:, 0:NCORE * TPC],
                            cidx[:, part * 64: part * 64 + w // 16], True),
                            reads=tuple(f"slx{r}" for r in range(NCORE)) + ("cidx", "xsfo"),
                            writes=("xsfo",))
                    ST(xs_sp[fb * 128:(fb + 1) * 128, :], xsf_v, "xs_sp", ("xsfo",))
                out.append(p_fb)
            return out

        ag4state = [False, False]

        def ffn_pieces():
            out = []

            def p_fence():
                FENCE(("xssl", "h1"), ("xsfo", "slabx"))
            out.append(p_fence)
            for tc5 in range(5):
                def p_xl(tc5=tc5):
                    for k in range(8):
                        LD(xssl_v[:, k * 512:(k + 1) * 512],
                           xs_sp.ap()[k * 128:(k + 1) * 128, tc5 * 512:(tc5 + 1) * 512],
                           "xssl", reads=("xs_sp", "xssl"))
                out.append(p_xl)
                for f in range(32):
                    def p_f(tc5=tc5, f=f):
                        par = f % 2
                        LD(w1t[:, par * 1024:(par + 1) * 1024],
                           w1bf.ap()[f * 128:(f + 1) * 128, :], f"w1t{par}",
                           reads=("w1bf", f"w1t{par}"))
                        ps, pk = take_ps()
                        MMC(ps[:, :],
                            [(w1t[:, par * 1024 + k * 128: par * 1024 + (k + 1) * 128],
                              xssl_v[:, k * 512:(k + 1) * 512]) for k in range(8)],
                            (f"w1t{par}", "xssl"), (pk,))
                        ACT(h1_v[:, f * 512:(f + 1) * 512], ps[:, :], AF.Relu,
                            (pk, "b1_s", "h1"), ("h1",), bias=b1_s[:, f:f + 1])
                    out.append(p_f)
                for d8 in range(8):
                    def p_w2ld(d8=d8):
                        LD(w2t_v[:, :], w2bf.ap()[d8 * 128:(d8 + 1) * 128, :], "w2t",
                           reads=("w2bf", "w2t"))
                    out.append(p_w2ld)
                    psd = (psB, "psB")
                    for g in range(4):
                        def p_sub(tc5=tc5, d8=d8, g=g, psd=psd):
                            ps, pk = psd
                            MMC(ps[:, :],
                                [(w2t_v[:, (g * 8 + f) * 128:(g * 8 + f + 1) * 128],
                                  h1_v[:, (g * 8 + f) * 512:(g * 8 + f + 1) * 512])
                                 for f in range(8)],
                                ("w2t", "h1"), (pk,), first=(g == 0), last=(g == 3))
                        out.append(p_sub)

                    def p_ev(tc5=tc5, d8=d8, psd=psd):
                        ps, pk = psd
                        fe = ffev_v[:, 0:512]
                        TS(fe, ps[:, :], b2_s[:, d8:d8 + 1], None, AL.add, AL.add,
                           (pk, "b2_s", "ffev0"), ("ffev0",))
                        ST(ag4in[d8 * 128:(d8 + 1) * 128, tc5 * 512:(tc5 + 1) * 512], fe,
                           "ag4in", ("ffev0",))
                    out.append(p_ev)
            return out

        def scan_q(q, mid=None):
            while PRI:
                PRI.pop(0)()
            if q == 0:
                FENCE(("bbc", "cbc"), ("lbctab",))
                FENCE(("a0", "a1", "ym0", "ym1", "h0", "h1p", "yac"), ())
                FENCE(("ygt0", "ygt1"), ())
            for n in range(N):
                MM(psC[:, :], selmat[:, n * 128:(n + 1) * 128],
                   bmcm_b[:, q * QT:(q + 1) * QT],
                   ("bmcm", "selmat"), ("psC",), start=True, stop=True)
                COPY(bbc_v[:, n * QT:(n + 1) * QT], psC[:, :], ("psC", "bbc"), ("bbc",))
                MM(psC[:, :], selmat[:, (16 + n) * 128:(17 + n) * 128],
                   bmcm_b[:, q * QT:(q + 1) * QT],
                   ("bmcm", "selmat"), ("psC",), start=True, stop=True)
                COPY(cbc_v[:, n * QT:(n + 1) * QT], psC[:, :], ("psC", "cbc"), ("cbc",))
            for blk in range(NBLK):
                if blk == 2 and mid is not None:
                    mid()
                if blk in (3, 6):
                    npop = (len(PRI) + 1) // 2 if blk == 3 else len(PRI)
                    for _ in range(npop):
                        if PRI:
                            PRI.pop(0)()
                pb = blk % 2
                lk, uk, gk, ygk = f"lns{pb}", f"ut{pb}", f"gt{pb}", f"ygt{pb}"
                lns = lns_t[:, pb * QT:(pb + 1) * QT]
                ut = u_t[:, pb * QT:(pb + 1) * QT]
                gt = gate_t[:, pb * QT:(pb + 1) * QT]
                ygs = ygt[:, pb * QT:(pb + 1) * QT]
                LD(lns, lns_sp.ap()[blk * 128:(blk + 1) * 128, q * QT:(q + 1) * QT],
                   lk, reads=("lns_sp", lk))
                LD(ut, u_sp.ap()[blk * 128:(blk + 1) * 128, q * QT:(q + 1) * QT],
                   uk, reads=("u_sp", uk))
                LD(gt, gate_sp.ap()[blk * 128:(blk + 1) * 128, q * QT:(q + 1) * QT],
                   gk, reads=("gate_sp", gk))
                TT(ndu[:, :], lns, ut, AL.mult, (lk, uk, "ndu"), ("ndu",))
                for n in range(N):
                    pn = n % 2
                    ak, hk, mk = f"a{pn}", (f"h{pn}" if pn == 0 else "h1p"), f"ym{pn}"
                    av = a_v[:, pn * 512:(pn + 1) * 512]
                    hv = h_v[:, pn * 512:(pn + 1) * 512]
                    mv = ymul_v[:, pn * 512:(pn + 1) * 512]
                    ACT(av, lns, AF.Exp, (lk, "nega", ak), (ak,),
                        scale=nega[:, blk * N + n: blk * N + n + 1])
                    TT(b_t[:, :], ndu[:, :], bbc_v[:, n * QT:(n + 1) * QT], AL.mult,
                       ("ndu", "bbc", "b_t"), ("b_t",))
                    P.emit("vector", lambda e, q=q, blk=blk, n=n, hv=hv, av=av:
                           e.tensor_tensor_scan(
                        out=hv, data0=av, data1=b_t[:, :],
                        initial=(0.0 if q == 0 else carryc[:, blk * N + n: blk * N + n + 1]),
                        op0=AL.mult, op1=AL.add),
                        reads=(ak, "b_t", hk) + (() if q == 0 else ("carryc",)),
                        writes=(hk,))
                    COPY(carryc[:, blk * N + n: blk * N + n + 1], hv[:, QT - 1:QT],
                         (hk, "carryc"), ("carryc",))
                    TT(mv, hv, cbc_v[:, n * QT:(n + 1) * QT], AL.mult,
                       (hk, "cbc", mk), (mk,))
                    if n == 0:
                        COPY(yacc_v, mv, (mk, "yac"), ("yac",))
                    else:
                        TT(yacc_v, yacc_v, mv, AL.add, ("yac", mk), ("yac",))
                    pop_fill()
                STT(yacc_v, ut, consts[:, 32 + blk:33 + blk], yacc_v,
                    AL.mult, AL.add, (uk, "consts", "yac"), ("yac",))
                TT(ygs, yacc_v, gt, AL.mult, ("yac", gk, ygk), (ygk,))
                ST(ag5in[q * EH + blk * 128: q * EH + (blk + 1) * 128, :], ygs,
                   "ag5in", (ygk,), eng="sync")

        def ag5cc(q):
            CC("AllGather", ag5in.ap()[q * EH:(q + 1) * EH, :].opt(),
               ag5out.ap()[q * 8 * EH:(q + 1) * 8 * EH, :].opt(),
               ("ag5in",), (f"ag5o{q}",))

        FENCE(("x2sl",), ())
        FENCE(("psC",), ("psC0", "psC1", "psC2", "psC3"))
        for k in range(8):
            LD(castB[:, 0:96], wdbc_my[k * 128:(k + 1) * 128, :], "castB")
            COPY(wot[:, k * 96:(k + 1) * 96],
                 castB[:, 0:96], ("castB", "wot"), ("wot",))
        LD(castA[0:64, :], wdt_my.ap(), "castA")
        COPY(wsc[0:64, 0:1024], castA[0:64, :], ("castA", "wsc"), ("wsc",))
        FENCE(("rbcev0", "rbcev1"), ("castB",))
        FENCE(("spb0", "spb1"), ("castA",))
        FENCE(("spo0", "spo1"), ())

        _spn = [0]

        def spill_tile(src, rr, cc, dst, dstw, key):
            j = _spn[0] % 2
            _spn[0] += 1
            bsl = castA[:, j * 512:(j + 1) * 512]
            osl = (stile if j == 0 else resid).bitcast(BF16)[:, 512:1024]
            out_ap = bass.AP(dst, cc * 4 * 128 * dstw + rr * 128,
                             [[dstw, 128], [128 * dstw, 4], [1, 128]])
            kb, ko = f"spb{j}", f"spo{j}"
            LD(bsl, src[rr * 128:(rr + 1) * 128, cc * 512:(cc + 1) * 512], kb,
               reads=(kb,), eng="scalar")
            SCOPY(osl, bsl, (kb, ko), (ko,))
            ST(out_ap, osl, key, (ko,), eng="scalar")

        for rr in range(8):
            for cc in range(4):
                spill_tile(win_my.ap(), rr, cc, winsp, D, "winsp")

        def spill_pieces():
            out = []
            for rr in range(8):
                for cc in range(8):
                    out.append(lambda rr=rr, cc=cc: spill_tile(
                        w1e.ap(), rr, cc, w1bf, D, "w1bf"))
            for rr in range(32):
                for cc in range(2):
                    out.append(lambda rr=rr, cc=cc: spill_tile(
                        w2e.ap(), rr, cc, w2bf, FF, "w2bf"))
            for rr in range(16):
                for cc in range(2):
                    out.append(lambda rr=rr, cc=cc: spill_tile(
                        wout_i.ap(), rr, cc, woutbf, E, "woutbf"))
            return out

        FENCE(("diagD",), ())
        COPY(identb_v[:, 0:128], ident_f[:, :], ("ident_f", "diagD"), ("diagD",))
        for blk in range(NBLK):
            TS(diagD_v[:, blk * 128:(blk + 1) * 128], identb_v[:, 0:128],
               consts[:, 32 + blk:33 + blk], None, AL.mult, AL.add,
               ("diagD", "consts"), ("diagD",))

        for go in inproj_pieces(0):
            go()
        ag3cc(0)
        rbc(0)
        delta(0)
        for go in inproj_pieces(1):
            go()
        ag3cc(1)
        FENCE(("lbctab",), ("xtA", "xtB", "sq0", "sq1", "lg_fm"))
        FENCE(("castA",), ("spb0", "spb1"))
        FENCE(("xsf",), ("llistbc",))
        for r8 in range(NCORE):
            LD(castA[:, r8 * 16:(r8 + 1) * 16],
               bass.AP(metaout, (r8 * 4 + 0) * TPC, [[1, 128], [128, 16]]),
               "castA", reads=("metaout",))
            LD(castA[:, 128 + r8 * 16: 128 + (r8 + 1) * 16],
               bass.AP(metaout, (r8 * 4 + 2) * TPC, [[1, 128], [128, 16]]),
               "castA", reads=("metaout",))
        TS(castB[:, 0:128], castA[:, 0:128], myexp_s[:, 0:1], None, AL.is_equal, AL.add,
           ("castA", "myexp_s", "castB"), ("castB",))
        TT(castA[:, 256:384], castB[:, 0:128], castA[:, 128:256], AL.mult,
           ("castB", "castA"), ("castA",))
        MM(psT[:, :], tri_f[:, :], castA[:, 256:384], ("tri_f", "castA"), ("psT",),
           start=True, stop=True)
        COPY(castA[:, 384:512], psT[:, :], ("psT", "castA"), ("castA",))
        LD(castB[0:1, 128:256], castA[127:128, 384:512], "castB", reads=("castA",),
           eng="sync")
        P.emit("tensor", lambda e: e.transpose(
            psT2[:, 0:1], castB[0:1, 128:256], ident_f[0:1, 0:1]),
            reads=("castB", "ident_f"), writes=("psT2",))
        COPY(castB[:, 256:257], psT2[:, 0:1], ("psT2", "castB"), ("castB",))
        MM(psT2[:, 0:1], tri_f[:, :], castB[:, 256:257], ("tri_f", "castB"), ("psT2",),
           start=True, stop=True)
        COPY(castB[:, 257:258], psT2[:, 0:1], ("psT2", "castB"), ("castB",))
        TT(castB[:, 258:259], castB[:, 257:258], castB[:, 256:257], AL.subtract,
           ("castB",), ("castB",))
        P.emit("tensor", lambda e: e.transpose(
            psT2[0:1, 0:128], castB[:, 258:259], ident_f[:, :]),
            reads=("castB", "ident_f"), writes=("psT2",))
        COPY(castB[0:1, 384:512], psT2[0:1, 0:128], ("psT2", "castB"), ("castB",))
        for pp in range(16):
            LD(moepk[pp:pp + 1, 0:128], castB[0:1, 384:512], "moepk",
               reads=("castB",), eng="sync")
        acc_i = moepk[:, 1024:1184]
        acc_p = moepk[:, 1184:1344]
        cmp_t = moepk[:, 1344:1504]
        tmp_t = moepk[:, 1504:1664]
        MEMSET(acc_i, 0.0, ("moepk",))
        MEMSET(acc_p, 0.0, ("moepk",))
        for i in range(128):
            TS(cmp_t, tokid[:, 0:160], moepk[:, i:i + 1], None, AL.is_ge, AL.add,
               ("tokid", "moepk"), ("moepk",))
            TT(acc_i, acc_i, cmp_t, AL.add, ("moepk",), ("moepk",))
            TS(tmp_t, cmp_t, moepk[:, i:i + 1], None, AL.mult, AL.add,
               ("moepk",), ("moepk",))
            TT(acc_p, acc_p, tmp_t, AL.max, ("moepk",), ("moepk",))
        TS(acc_i, acc_i, 1.0, None, AL.subtract, AL.add, ("moepk",), ("moepk",))
        TT(cmp_t, tokid[:, 0:160], acc_p, AL.subtract, ("tokid", "moepk"), ("moepk",))
        TS(tmp_t, acc_i, 128.0, None, AL.mult, AL.add, ("moepk",), ("moepk",))
        TT(tmp_t, tmp_t, cmp_t, AL.add, ("moepk",), ("moepk",))
        P.emit("vector", lambda e: e.tensor_scalar_min(tmp_t, tmp_t, 16383.0),
               reads=("moepk",), writes=("moepk",))
        COPY(comp16[:, 0:160], tmp_t, ("moepk", "comp16"), ("comp16",))
        for g in range(8):
            P.emit("sync", lambda e, g=g: e.dma_start(
                out=cidx[g * 16:(g + 1) * 16, :], in_=comp16[:, :]),
                reads=("comp16",), writes=("cidx",), dma=True)
        FENCE(("llistbc",), ("h1",))
        for i in range(128):
            TS(ARB2[:, i * 128:(i + 1) * 128], iotar[:, :], castA[:, 384 + i:385 + i],
               None, AL.is_ge, AL.add, ("iotar", "castA", "llistbc"), ("llistbc",))
        for ch in range(32):
            MM(psC[0:1, 0:512], onescb[:, 0:1], ARB2[:, ch * 512:(ch + 1) * 512],
               ("onescb", "llistbc"), ("psC",), start=True, stop=True)
            COPY(castB.bitcast(BF16)[0:1, 1024:1536], psC[0:1, 0:512],
                 ("psC", "castB"), ("castB",))
            MM(psA[:, :], ones1b[0:1, :], castB.bitcast(BF16)[0:1, 1024:1536],
               ("ones1b", "castB"), ("psA",), start=True, stop=True)
            COPY(ARBIG.bitcast(BF16)[:, ch * 512:(ch + 1) * 512], psA[:, :],
                 ("psA", "lbctab"), ("lbctab",))
        mr0 = moepk[:, 128:288]
        mr1 = moepk[:, 288:448]
        idr = moepk[:, 448:608]
        for r2 in range(2):
            mr = mr0 if r2 == 0 else mr1
            TS(mr, tmp_t, float(8192 * r2), None, AL.is_ge, AL.add, ("moepk",), ("moepk",))
            if r2 == 0:
                TS(idr, tmp_t, 8192.0, None, AL.is_lt, AL.add, ("moepk",), ("moepk",))
                TT(mr, mr, idr, AL.mult, ("moepk",), ("moepk",))
            TS(idr, tmp_t, float(8192 * r2), None, AL.subtract, AL.add,
               ("moepk",), ("moepk",))
            TT(idr, idr, mr, AL.mult, ("moepk",), ("moepk",))
            COPY(comp16[:, 0:160], idr, ("moepk", "comp16"), ("comp16",))
            for g in range(8):
                P.emit("sync", lambda e, g=g: e.dma_start(
                    out=cidx[g * 16:(g + 1) * 16, :], in_=comp16[:, :]),
                    reads=("comp16",), writes=("cidx",), dma=True)
            for part in range(3):
                w = 1024 if part < 2 else 512
                P.emit("gpsimd", lambda e, part=part, w=w, r2=r2: e.indirect_copy(
                    ARB2[:, r2 * 2560 + part * 1024:
                         r2 * 2560 + part * 1024 + w],
                    ARBIG.bitcast(BF16)[:, 8192 * r2: 8192 * (r2 + 1)],
                    cidx[:, part * 64: part * 64 + w // 16], True),
                    reads=("lbctab", "cidx", "xsf"), writes=("xsf",))
            ST(lv_hbm.ap()[0:1, :], ARB2[0:1, r2 * 2560: r2 * 2560 + 2560],
               f"lv_hbm{r2}", ("xsf",), eng="gpsimd")
            LD(moepk.bitcast(BF16)[:, 7600 + r2 * 160: 7760 + r2 * 160],
               bass.AP(lv_hbm, 0, [[1, 16], [16, 160]]), f"lvint{r2}",
               reads=(f"lv_hbm{r2}", "moepk"))
        COPY(cmp_t, moepk.bitcast(BF16)[:, 7600:7760], ("moepk",), ("moepk",))
        TT(cmp_t, cmp_t, mr0, AL.mult, ("moepk",), ("moepk",))
        COPY(idr, moepk.bitcast(BF16)[:, 7760:7920], ("moepk",), ("moepk",))
        TT(idr, idr, mr1, AL.mult, ("moepk",), ("moepk",))
        TT(cmp_t, cmp_t, idr, AL.add, ("moepk",), ("moepk",))
        TS(tmp_t, acc_i, 128.0, None, AL.mult, AL.add, ("moepk",), ("moepk",))
        TT(tmp_t, tmp_t, cmp_t, AL.add, ("moepk",), ("moepk",))
        P.emit("vector", lambda e: e.tensor_scalar_min(tmp_t, tmp_t, 16383.0),
               reads=("moepk",), writes=("moepk",))
        COPY(comp16[:, 0:160], tmp_t, ("moepk", "comp16"), ("comp16",))
        for g in range(8):
            P.emit("sync", lambda e, g=g: e.dma_start(
                out=cidx[g * 16:(g + 1) * 16, :], in_=comp16[:, :]),
                reads=("comp16",), writes=("cidx",), dma=True)

        LD(midx_v[:, 0:128], bass.AP(metain, 0, [[1, 16], [16, 128]]), "moepk",
           reads=("metain",))
        LD(mkeep_v[:, 0:128], bass.AP(metain, TPC, [[1, 16], [16, 128]]), "moepk",
           reads=("metain",))
        STT(mwork_v[:, 0:128], midx_v[:, 0:128], cvals[0:16, 4:5], mkeep_v[:, 0:128],
            AL.mult, AL.add, ("moepk", "cvals"), ("moepk",))
        for r3 in range(3):
            mr = moepk[:, 608 + r3 * 128: 736 + r3 * 128]
            TS(mr, mwork_v[:, 0:128], float(8192 * r3), None, AL.is_ge, AL.add,
               ("moepk",), ("moepk",))
            if r3 < 2:
                TS(moepk[:, 992:1120] if False else midx_v[:, 128:256],
                   mwork_v[:, 0:128], float(8192 * (r3 + 1)), None, AL.is_lt, AL.add,
                   ("moepk",), ("moepk",))
                TT(mr, mr, midx_v[:, 128:256], AL.mult, ("moepk",), ("moepk",))
            TS(midx_v[:, 128:256], mwork_v[:, 0:128], float(8192 * r3), None,
               AL.subtract, AL.add, ("moepk",), ("moepk",))
            TT(midx_v[:, 128:256], midx_v[:, 128:256], mr, AL.mult,
               ("moepk",), ("moepk",))
            COPY(gidx16[0:16, r3 * 128:(r3 + 1) * 128], midx_v[:, 128:256],
                 ("moepk", "gidx16"), ("gidx16",))
            for g in range(1, 8):
                P.emit("sync", lambda e, g=g, r3=r3: e.dma_start(
                    out=gidx16[g * 16:(g + 1) * 16, r3 * 128:(r3 + 1) * 128],
                    in_=gidx16[0:16, r3 * 128:(r3 + 1) * 128]),
                    reads=("gidx16",), writes=("gidx16",), dma=True)
        gxr = rstat[:, 0:16]
        FENCE(("gxr",), tuple(f"st{it}" for it in range(NT)))
        for r3 in range(3):
            mr = moepk[:, 608 + r3 * 128: 736 + r3 * 128]
            P.emit("tensor", lambda e, mr=mr: e.transpose(
                psT[0:128, 0:16], mr, ident_f[0:16, 0:16]),
                reads=("moepk", "ident_f"), writes=("psT",))
            SCOPY(gxr, psT[0:128, 0:16], ("psT", "gxr"), ("gxr",))
            ST(bass.AP(mrow_hbm, r3 * TPC, [[16, 128], [1, 16]]), gxr,
               f"mrowh{r3}", ("gxr",), eng="gpsimd")

        FENCE(("spb0", "spb1"), ("castA",))
        BULK.extend(gather_pieces())
        BULK.extend(spill_pieces())
        BULK.extend(ffn_pieces())
        for ck in range(NQ):
            if ck + 2 < NQ:
                PRI.extend(inproj_pieces(ck + 2))

            def mid(ck=ck):
                if ck + 1 < NQ:
                    rbc(ck + 1)
                    delta(ck + 1)
            scan_q(ck, mid=mid)
            while PRI:
                PRI.pop(0)()
            if ck + 2 < NQ:
                ag3cc(ck + 2)
            if ck >= NQ - 3 and not BULK and not ag4state[0]:
                CC("AllGather", ag4in.ap().opt(), ag4out.ap().opt(),
                   ("ag4in",), ("ag4out",))
                ag4state[0] = True
            ag5cc(ck)
        while BULK:
            BULK.pop(0)()
        if not ag4state[0]:
            CC("AllGather", ag4in.ap().opt(), ag4out.ap().opt(), ("ag4in",), ("ag4out",))

        FENCE(("gvmbc",), ("a0", "a1", "ym0", "ym1", "h0", "h1p", "yac",
                           "art0", "art1", "art2"))
        FENCE(("gvbc", "graw"), ("xssl", "xsfo"))
        FENCE(("castA", "castB"), ("spb0", "spb1", "rbcev0", "rbcev1"))
        FENCE(("gvrow",), ("moepk",))
        for r3 in range(3):
            for hh in range(2):
                LD(castA[0:1, :], mrow_hbm.ap()[r3:r3 + 1, hh * 1024:(hh + 1) * 1024],
                   "castA", reads=(f"mrowh{r3}", "castA"))
                LD(castB[0:1, :], metain.ap()[3:4, hh * 1024:(hh + 1) * 1024],
                   "castB", reads=("metain", "castB"))
                TT(castA[0:1, :], castA[0:1, :], castB[0:1, :], AL.mult,
                   ("castA", "castB"), ("castA",))
                COPY(gvrow[:, hh * 1024:(hh + 1) * 1024], castA[0:1, :],
                     ("castA", "gvrow"), ("gvrow",))
            for tc in range(4):
                MM(psC[:, :], ones1b[0:1, :], gvrow[0:1, tc * 512:(tc + 1) * 512],
                   ("ones1b", "gvrow"), ("psC",), start=True, stop=True)
                COPY(gvm_bc[:, r3 * 2048 + tc * 512: r3 * 2048 + (tc + 1) * 512],
                     psC[:, :], ("psC", "gvmbc"), ("gvmbc",))
        FENCE(("gfb",), ("h1", "slabx"))
        FENCE(("gacck",), ("yac", "art2"))
        FENCE(tuple(f"sl{r8}" for r8 in range(NCORE)),
              ("bbc", "cbc", "wpar0", "wpar1", "diagD"))
        gacc2 = AR2.bitcast(BF16)[:, 6144:8192]
        for fb in range(8):
            for r8 in range(NCORE):
                LD(slab_v[:, r8 * CAP:(r8 + 1) * CAP],
                   ag4out.ap()[r8 * D + fb * 128: r8 * D + (fb + 1) * 128, :],
                   f"sl{r8}", reads=("ag4out", f"sl{r8}"))
            t2 = gfb_v[:, fb * TPC:(fb + 1) * TPC]

            for r3 in range(3):
                dst = (gacc_v, gtmp_v, t2)[r3]

                def icpair(e, dst=dst, r3=r3):
                    last = None
                    for part in range(2):
                        last = e.indirect_copy(
                            dst[:, part * 1024:(part + 1) * 1024],
                            slab_v[:, 8192 * r3: min(8192 * (r3 + 1), NCORE * CAP)],
                            gidx16[:, r3 * 128 + part * 64: r3 * 128 + (part + 1) * 64],
                            True)
                    return last
                P.emit("gpsimd", icpair,
                       reads=tuple(f"sl{r8}" for r8 in range(NCORE)) +
                             ("gidx16", f"gr{r3}") + (("gfb",) if r3 == 2 else ()),
                       writes=(f"gr{r3}",))
            TT(gacc2, gacc_v, gvm_bc[:, 0:2048], AL.mult,
               ("gr0", "gvmbc", "gacck"), ("gacck",))
            TT(gtmp_v, gtmp_v, gvm_bc[:, 2048:4096], AL.mult,
               ("gr1", "gvmbc"), ("gr1",))
            TT(gacc2, gacc2, gtmp_v, AL.add, ("gacck", "gr1"), ("gacck",))
            TT(t2, t2, gvm_bc[:, 4096:6144], AL.mult, ("gr2", "gvmbc"), ("gr2",))
            TT(gacc2, gacc2, t2, AL.add, ("gacck", "gr2"), ("gacck",))
            ACT(gfb_v[:, fb * TPC:(fb + 1) * TPC], gacc2, AF.Silu,
                ("gacck", "gfb", "gr2"), ("gfb",))

        FENCE(("ysl",), ("gvmbc", "gacck"))
        FENCE(("ottm",), ("x2sl", "w2t"))
        FENCE(("wot0", "wot1"), ("w1t0", "w1t1", "wot", "wsc"))
        FENCE(("stile", "resid", "sgt"), ("dsgb0", "dsgb1", "spo0", "spo1", "ffev0"))
        FENCE(("xrow_v",), ("gvbc", "graw"))
        for tc in range(4):
            for half in range(2):
                dyn_ld(ysl_v[:, half * 4096:(half + 1) * 4096], ag5out, 4 + half,
                       tc * 8 * EH * 512,
                       [[512, 128], [128 * 512, 8], [1, 512]], "ysl",
                       reads=(f"ag5o{tc}", f"ag5o{4 + tc}", "ysl"))
            for d8 in range(8):
                par = d8 % 2
                wbuf = wot if par == 0 else w1t
                LD(wbuf[:, 0:2048], woutbf.ap()[d8 * 128:(d8 + 1) * 128, :],
                   f"wot{par}", reads=("woutbf", f"wot{par}"))
                ps, pk = take_ps()
                MMC(ps[:, :],
                    [(wbuf[:, eb * 128:(eb + 1) * 128],
                      ysl_v[:, eb * 512:(eb + 1) * 512]) for eb in range(16)],
                    (f"wot{par}", "ysl"), (pk,))
                TS(stile[:, :], ps[:, :], consts[:, 24 + d8:25 + d8], None, AL.add, AL.add,
                   (pk, "consts", "stile"), ("stile",))
                LD(resid.bitcast(BF16)[:, 512:1024],
                   ag1x.ap()[d8 * 128:(d8 + 1) * 128, tc * 512:(tc + 1) * 512],
                   "resid", reads=("ag1x", "resid"))
                TT(stile[:, :], stile[:, :], resid.bitcast(BF16)[:, 512:1024], AL.add,
                   ("stile", "resid"), ("stile",))
                TT(sgt[:, :], stile[:, :],
                   gfb_v[:, d8 * TPC + tc * 512: d8 * TPC + (tc + 1) * 512],
                   AL.mult, ("stile", "gfb", "sgt"), ("sgt",))
                for sub in range(4):
                    pst = psT if sub % 2 == 0 else psT2
                    pstk = "psT" if sub % 2 == 0 else "psT2"
                    TRANS(pst[:, :], sgt[:, sub * 128:(sub + 1) * 128], ("sgt",), (pstk,))
                    SCOPY(ottm_v[:, sub * 1024 + d8 * 128: sub * 1024 + (d8 + 1) * 128],
                          pst[:, :], (pstk, "ottm"), ("ottm",))
            for sub in range(4):
                it = tc * 4 + sub
                LD(xrow_v, xrows[it * 128:(it + 1) * 128, :], "xrow_v",
                   reads=("xrow_v",))
                TT(ottm_v[:, sub * 1024:(sub + 1) * 1024],
                   ottm_v[:, sub * 1024:(sub + 1) * 1024], xrow_v, AL.add,
                   ("xrow_v", "ottm"), ("ottm",))
                ST(out_rows[it * 128:(it + 1) * 128, :], ottm_v[:, sub * 1024:(sub + 1) * 1024],
                   "out", ("ottm",))

        if limit is None:
            P.finalize_outputs()
        else:
            P.limit = limit

        @block.tensor
        def _(e):
            P.replay("tensor", e, sems)

        @block.vector
        def _(e):
            P.replay("vector", e, sems)

        @block.scalar
        def _(e):
            P.replay("scalar", e, sems)

        @block.gpsimd
        def _(e):
            P.replay("gpsimd", e, sems)

        @block.sync
        def _(e):
            P.replay("sync", e, sems)

    return nc


def _in_maps(inputs):
    f32 = lambda a: np.ascontiguousarray(a, dtype=np.float32)
    x = f32(inputs["L0_attn_x"]).reshape(NCORE * TPC, D)
    W_in = f32(inputs["L0_attn_W_in"])
    b_in = f32(inputs["L0_attn_b_in"])
    W_delta = f32(inputs["L0_attn_W_delta"])
    b_delta = f32(inputs["L0_attn_b_delta"])
    W_dt = f32(inputs["L0_attn_W_dt"])
    b_dt = f32(inputs["L0_attn_b_dt"])
    W_B = f32(inputs["L0_attn_W_B"])
    b_B = f32(inputs["L0_attn_b_B"])
    W_C = f32(inputs["L0_attn_W_C"])
    b_C = f32(inputs["L0_attn_b_C"])
    A_log = f32(inputs["L0_attn_A_log"])
    Dv = f32(inputs["L0_attn_D"])
    W_out = f32(inputs["L0_attn_W_out"])
    b_out = f32(inputs["L0_attn_b_out"])
    ln_g = f32(inputs["L0_attn_ln_g"])
    ln_b = f32(inputs["L0_attn_ln_b"])
    gate_w = f32(inputs["L0_ffn_gate_w"])
    gate_b = f32(inputs["L0_ffn_gate_b"])
    w1 = f32(inputs["L0_ffn_w1"])
    b1 = f32(inputs["L0_ffn_b1"])
    w2 = f32(inputs["L0_ffn_w2"])
    b2 = f32(inputs["L0_ffn_b2"])

    maps = []
    for c in range(NCORE):
        b, h = c // 2, c % 2
        sl = slice(h * EH, (h + 1) * EH)
        win = np.concatenate([W_in[:, h * EH:(h + 1) * EH],
                              W_in[:, E + h * EH: E + (h + 1) * EH]], axis=1)
        binv = np.concatenate([b_in[h * EH:(h + 1) * EH],
                               b_in[E + h * EH: E + (h + 1) * EH]]).reshape(16, 128)
        wdbc = np.concatenate([W_delta[sl], W_B[sl], W_C[sl]], axis=1)
        rbcb = np.concatenate([b_delta, b_B, b_C]).reshape(96, 1)
        bases = np.array([[
            (2 * b) * D * TPC,
            (2 * b + 1) * D * TPC,
            (2 * b) * 96 * 512,
            (2 * b + 1) * 96 * 512,
            4 * h * 8 * EH * 512 + (2 * b) * EH * 512,
            4 * h * 8 * EH * 512 + (2 * b + 1) * EH * 512,
            0, 0]], dtype=np.int32)
        maps.append({
            "xrows": x[c * TPC:(c + 1) * TPC],
            "win_my": win,
            "bin_my": binv,
            "bdt_my": b_dt[sl].reshape(NBLK, 128),
            "wdbc_my": wdbc,
            "rbc_bias": rbcb,
            "wdt_my": np.ascontiguousarray(W_dt[:, sl]),
            "alog_my": A_log[sl],
            "d_my": Dv[sl].reshape(NBLK, 128),
            "wout_i": W_out,
            "bout_i": b_out.reshape(NBLK, 128),
            "lng_i": ln_g.reshape(NBLK, 128),
            "lnb_i": ln_b.reshape(NBLK, 128),
            "gwT": np.ascontiguousarray(gate_w.T),
            "gb_i": gate_b.reshape(NE, 1),
            "w1e": w1[c],
            "w2e": w2[c],
            "b1e": b1[c].reshape(32, 128),
            "b2e": b2[c].reshape(NBLK, 128),
            "prefmask": (np.arange(NE) < c).astype(np.float32).reshape(NE, 1),
            "myexp_i": np.full((128, 1), c, np.float32),
            "bases_i": bases,
        })
    return maps


_NC_CACHE = {}


def kernel(**inputs):
    if "nc" not in _NC_CACHE:
        _NC_CACHE["nc"] = build_nc()
    nc = _NC_CACHE["nc"]
    maps = _in_maps(inputs)
    res = run_bass_kernel_spmd(nc, maps, core_ids=list(range(NCORE)))
    out = np.concatenate([r["out_rows"] for r in res.results], axis=0)
    return out.reshape(4, 4096, 1024)


# version 21
# speedup vs baseline: 1.0093x; 1.0062x over previous
import numpy as np
import contextlib
import concourse.bass as bass
import concourse.mybir as mybir
from concourse.bass_utils import run_bass_kernel_spmd

F32 = mybir.dt.float32
BF16 = mybir.dt.bfloat16
U16 = mybir.dt.uint16
U32 = mybir.dt.uint32
I32 = mybir.dt.int32
AL = mybir.AluOpType
AF = mybir.ActivationFunctionType
AX = mybir.AxisListType.X

NCORE = 8
TPC = 2048
TB = 4096
D = 1024
E = 2048
EH = 1024
NBLK = 8
N = 16
R = 64
FF = 4096
CAP = 2560
NE = 8
QT = 512
NQ = TB // QT
NT = TPC // 128
EPS = 1e-5


class Prog:

    ENGINES = ("tensor", "vector", "scalar", "gpsimd", "sync")
    QPOOL = 10

    def __init__(self):
        self.stream = {e: [] for e in self.ENGINES}
        self.cnt = {}
        self.events = {}
        self.readers = {}
        self.seen = {e: {} for e in self.ENGINES}
        self.groups = {}
        self.rr = {}
        self.semcum = {}
        self.dma_sems = set()

    def _qsem(self, q, i):
        name = f"{q}_q{i}"
        self.dma_sems.add(name)
        return name

    def _open(self, q):
        g = self.groups.get(q)
        if g is not None and not g["closed"]:
            return g
        i = self.rr.get(q, 0)
        self.rr[q] = i + 1
        sem = self._qsem(q, i % self.QPOOL)
        base = self.semcum.get(sem, 0)
        if base > 0:
            self.gseq = getattr(self, "gseq", 0) + 1
            self.stream[q].append((None, [(sem, base)], None, self.gseq))
            self.seen[q][sem] = max(self.seen[q].get(sem, 0), base)
        g = {"sem": sem, "base": base, "n": 0, "closed": False}
        self.groups[q] = g
        return g

    def _close(self, g):
        if not g["closed"]:
            g["closed"] = True
            g["total"] = g["base"] + 16 * g["n"]
            self.semcum[g["sem"]] = g["total"]

    def emit(self, eng, fn, reads=(), writes=(), dma=False):
        import sys as _sys
        f = _sys._getframe(1)
        tag = f"{f.f_lineno}"
        if f.f_code.co_name in ("LD", "ST", "MM", "MMC", "SCOPY", "ACT", "COPY", "TT",
                                "TS", "STT", "RED", "MEMSET", "CC", "FENCE", "TRANS", "VOP"):
            f2 = f.f_back
            tag = f"{f.f_code.co_name}@{f2.f_lineno}"
        self.tags = getattr(self, "tags", [])
        waits = {}

        def need(ev):
            if ev is None:
                return
            if isinstance(ev, dict):
                self._close(ev)
                s, v = ev["sem"], ev["total"]
            else:
                s, v = ev
            if self.seen[eng].get(s, 0) >= v:
                return
            if waits.get(s, 0) < v:
                waits[s] = v

        for k in reads:
            need(self.events.get(k))
        for k in writes:
            need(self.events.get(k))
            for ev in self.readers.get(k, ()):
                need(ev)
        if dma:
            g = self._open(eng)
            g["n"] += 1
            ev = g
            inc = (g["sem"], 16)
        else:
            sem = eng
            self.cnt[sem] = self.cnt.get(sem, 0) + 1
            ev = (sem, self.cnt[sem])
            inc = (sem, 1)
        for k in reads:
            self.readers.setdefault(k, []).append(ev)
        for k in writes:
            self.events[k] = ev
            self.readers[k] = []
        for s, v in waits.items():
            self.seen[eng][s] = max(self.seen[eng].get(s, 0), v)
        self.gseq = getattr(self, "gseq", 0) + 1
        self.tags.append((self.gseq, eng, tag))
        self.stream[eng].append((fn, sorted(waits.items()), inc, self.gseq))

    def finalize_outputs(self):
        for q in ("gpsimd", "sync", "scalar"):
            g = self.groups.get(q)
            if g is not None:
                self._close(g)
            for i in range(self.QPOOL):
                sem = f"{q}_q{i}"
                tot = self.semcum.get(sem, 0)
                if tot > 0 and self.seen[q].get(sem, 0) < tot:
                    self.gseq = getattr(self, "gseq", 0) + 1
                    self.stream[q].append((None, [(sem, tot)], None, self.gseq))
                    self.seen[q][sem] = tot

    limit = None

    def replay(self, eng_name, e, sems):
        for fn, waits, inc, seq in self.stream[eng_name]:
            if self.limit is not None and seq > self.limit:
                continue
            for s, v in waits:
                e.wait_ge(sems[s], v)
            if fn is None:
                continue
            instr = fn(e)
            if inc is not None and instr is not None:
                instr.then_inc(sems[inc[0]], inc[1])


def build_nc(limit=None):
    nc = bass.Bass("TRN2", target_bir_lowering=False)

    def ein(name, shape, dt=F32):
        return nc.dram_tensor(name, shape, dt, kind="ExternalInput")

    xrows = ein("xrows", [TPC, D])
    win_my = ein("win_my", [D, 2048])
    bin_my = ein("bin_my", [16, 128])
    bdt_my = ein("bdt_my", [NBLK, 128])
    wdbc_my = ein("wdbc_my", [EH, 96])
    rbc_bias = ein("rbc_bias", [96, 1])
    wdt_my = ein("wdt_my", [R, EH])
    alog_my = ein("alog_my", [EH, N])
    d_my = ein("d_my", [NBLK, 128])
    wout_i = ein("wout_i", [E, D])
    bout_i = ein("bout_i", [NBLK, 128])
    lng_i = ein("lng_i", [NBLK, 128])
    lnb_i = ein("lnb_i", [NBLK, 128])
    gwT = ein("gwT", [D, NE])
    gb_i = ein("gb_i", [NE, 1])
    w1e = ein("w1e", [D, FF])
    w2e = ein("w2e", [FF, D])
    b1e = ein("b1e", [32, 128])
    b2e = ein("b2e", [NBLK, 128])
    prefmask = ein("prefmask", [NE, 1])
    myexp_i = ein("myexp_i", [128, 1])
    bases_i = ein("bases_i", [1, 8], I32)
    out_rows = nc.dram_tensor("out_rows", [TPC, D], F32, kind="ExternalOutput")

    dr = lambda name, shape, dt: nc.dram_tensor(name, shape, dt)
    ag1z = dr("ag1z", [D, TPC], BF16)
    ag1zout = dr("ag1zout", [NCORE * D, TPC], BF16)
    ag1x = dr("ag1x", [D, TPC], BF16)
    ag1xout = dr("ag1xout", [NCORE * D, TPC], BF16)
    cntin = dr("cntin", [1, NE], F32)
    cntout = dr("cntout", [NCORE, NE], F32)
    metain = dr("metain", [4, TPC], F32)
    metaout = dr("metaout", [NCORE * 4, TPC], F32)
    ag3in = dr("ag3in", [NQ * 96, QT], F32)
    ag3out = dr("ag3out", [NQ * NCORE * 96, QT], F32)
    ag4in = dr("ag4in", [D, CAP], BF16)
    ag4out = dr("ag4out", [NCORE * D, CAP], BF16)
    ag5in = dr("ag5in", [NQ * EH, QT], BF16)
    ag5out = dr("ag5out", [NQ * NCORE * EH, QT], BF16)
    u_sp = dr("u_sp", [EH, TB], BF16)
    gate_sp = dr("gate_sp", [EH, TB], BF16)
    lns_sp = dr("lns_sp", [EH, TB], BF16)
    winsp = dr("winsp", [E, D], BF16)
    w1bf = dr("w1bf", [FF, D], BF16)
    w2bf = dr("w2bf", [D, FF], BF16)
    woutbf = dr("woutbf", [D, E], BF16)
    xs_sp = dr("xs_sp", [D, CAP], BF16)
    myx2 = dr("myx2", [2 * EH, TPC], BF16)
    lv_hbm = dr("lv_hbm", [1, CAP], BF16)
    mrow_hbm = dr("mrow_hbm", [3, TPC], F32)

    P = Prog()

    import contextlib
    with contextlib.ExitStack() as _stack:
        _E = _stack.enter_context
        _stack.enter_context(nc.allow_non_contiguous_dma(
            reason="small transposed parameter loads / column stores"))
        onesf = _E(nc.sbuf_tensor("onesf", [128, 8], F32))
        ident_f = _E(nc.sbuf_tensor("ident_f", [128, 128], F32))
        tri_f = _E(nc.sbuf_tensor("tri_f", [128, 128], F32))
        ones1b = _E(nc.sbuf_tensor("ones1b", [1, 128], BF16))
        nones1b = _E(nc.sbuf_tensor("nones1b", [1, 128], BF16))
        ones1f = _E(nc.sbuf_tensor("ones1f", [1, 128], F32))
        iota8 = _E(nc.sbuf_tensor("iota8", [128, 8], F32))
        consts = _E(nc.sbuf_tensor("consts", [128, 64], F32))
        b1_s = _E(nc.sbuf_tensor("b1_s", [128, 32], F32))
        b2_s = _E(nc.sbuf_tensor("b2_s", [128, 8], F32))
        gwT_f = _E(nc.sbuf_tensor("gwT_f", [128, 64], F32))
        gb_s = _E(nc.sbuf_tensor("gb_s", [8, 1], F32))
        rbcb_s = _E(nc.sbuf_tensor("rbcb_s", [96, 1], F32))
        prefm_s = _E(nc.sbuf_tensor("prefm_s", [8, 1], F32))
        myexp_s = _E(nc.sbuf_tensor("myexp_s", [128, 1], F32))
        iotar = _E(nc.sbuf_tensor("iotar", [128, 128], F32))
        onescb = _E(nc.sbuf_tensor("onescb", [128, 2], BF16))
        bases_s = _E(nc.sbuf_tensor("bases_s", [1, 8], I32))
        nega = _E(nc.sbuf_tensor("nega", [128, NBLK * N], F32))
        fencebuf = _E(nc.sbuf_tensor("fencebuf", [1, 4], F32))
        cvals = _E(nc.sbuf_tensor("cvals", [128, 8], F32))
        ARBIG = _E(nc.sbuf_tensor("ARBIG", [128, 10240], F32))
        ARB2 = _E(nc.sbuf_tensor("ARB2", [128, 16384], BF16))
        AR2 = _E(nc.sbuf_tensor("AR2", [128, 4096], F32))
        ARX = _E(nc.sbuf_tensor("ARX", [128, 4096], BF16))
        ARW = _E(nc.sbuf_tensor("ARW", [128, 8192], BF16))
        ARE = _E(nc.sbuf_tensor("ARE", [128, 2048], F32))
        bmcm_b = _E(nc.sbuf_tensor("bmcm_b", [32, TB], BF16))
        castA = _E(nc.sbuf_tensor("castA", [128, 1024], F32))
        castB = _E(nc.sbuf_tensor("castB", [128, 1024], F32))
        moepk = _E(nc.sbuf_tensor("moepk", [16, 4096], F32))
        tokid = _E(nc.sbuf_tensor("tokid", [16, 1024], F32))
        comp16 = _E(nc.sbuf_tensor("comp16", [16, CAP // 16], U16))
        wsc = _E(nc.sbuf_tensor("wsc", [64, 2048], BF16))
        cidx = _E(nc.sbuf_tensor("cidx", [128, CAP // 16], U16))
        gidx16 = _E(nc.sbuf_tensor("gidx16", [128, 384], U16))
        selmat = _E(nc.sbuf_tensor("selmat", [32, 4096], BF16))
        w1t = _E(nc.sbuf_tensor("w1t", [128, 2048], BF16))
        wot = _E(nc.sbuf_tensor("wot", [128, 2048], BF16))
        lns_t = _E(nc.sbuf_tensor("lns_t", [128, 2 * QT], BF16))
        u_t = _E(nc.sbuf_tensor("u_t", [128, 2 * QT], BF16))
        gate_t = _E(nc.sbuf_tensor("gate_t", [128, 2 * QT], BF16))
        ndu = _E(nc.sbuf_tensor("ndu", [128, QT], BF16))
        b_t = _E(nc.sbuf_tensor("b_t", [128, QT], BF16))
        ygt = _E(nc.sbuf_tensor("ygt", [128, 2 * QT], BF16))
        rstat = _E(nc.sbuf_tensor("rstat", [128, 128], F32))
        carryc = _E(nc.sbuf_tensor("carryc", [128, NBLK * N], F32))
        lg_tm = _E(nc.sbuf_tensor("lg_tm", [128, NT * 8], F32))
        oh_tm = _E(nc.sbuf_tensor("oh_tm", [128, NT * 8], F32))
        pos_tm = _E(nc.sbuf_tensor("pos_tm", [128, NT * 8], F32))
        rcols = _E(nc.sbuf_tensor("rcols", [128, NT * 10], F32))
        base8 = _E(nc.sbuf_tensor("base8", [128, 8], F32))
        cnt_s = _E(nc.sbuf_tensor("cnt_s", [8, 8], F32))
        rtmp8 = _E(nc.sbuf_tensor("rtmp8", [128, 8], F32))
        stile = _E(nc.sbuf_tensor("stile", [128, 512], F32))
        resid = _E(nc.sbuf_tensor("resid", [128, 512], F32))
        sgt = _E(nc.sbuf_tensor("sgt", [128, 512], F32))
        x2stage = _E(nc.sbuf_tensor("x2stage", [128, 1024], BF16))
        xnstage = _E(nc.sbuf_tensor("xnstage", [128, 1024], BF16))
        psA = _E(nc.psum_tensor("psA", [128, 512], F32))
        psB = _E(nc.psum_tensor("psB", [128, 512], F32))
        psC = _E(nc.psum_tensor("psC", [128, 512], F32))
        ps96 = _E(nc.psum_tensor("ps96", [96, 512], F32))
        psT = _E(nc.psum_tensor("psT", [128, 128], F32))
        psT2 = _E(nc.psum_tensor("psT2", [128, 128], F32))
        psY = _E(nc.psum_tensor("psY", [128, 512], F32))
        ps8 = _E(nc.psum_tensor("ps8", [128, 8], F32))
        s_tensor = _E(nc.semaphore("s_tensor"))
        s_vector = _E(nc.semaphore("s_vector"))
        s_scalar = _E(nc.semaphore("s_scalar"))
        s_gpsimd = _E(nc.semaphore("s_gpsimd"))
        s_sync = _E(nc.semaphore("s_sync"))
        qsems = {}
        for q in ("sync", "gpsimd", "scalar"):
            for i in range(Prog.QPOOL):
                qsems[f"{q}_q{i}"] = _E(nc.semaphore(f"{q}_q{i}"))
        block = _E(nc.Block())
        sems = {
            "tensor": s_tensor, "vector": s_vector, "scalar": s_scalar,
            "gpsimd": s_gpsimd, "sync": s_sync,
        }
        sems.update(qsems)

        xtA = ARBIG[:, 0:1024]
        xtB = ARBIG[:, 1024:2048]
        sqA = ARBIG[:, 2048:3072]
        sqB = ARBIG[:, 5120:6144]
        lg_fm = ARBIG[0:8, 3072:5120]
        rbct_v = ARBIG[0:96, 0:4096]
        slab_v = ARBIG[:, 0:10240].bitcast(BF16)
        bbc_v = ARBIG[:, 0:4096].bitcast(BF16)
        cbc_v = ARBIG[:, 4096:8192].bitcast(BF16)
        wpar_v = ARBIG.bitcast(BF16)[:, 16384:18432]
        diagD_v = ARBIG.bitcast(BF16)[:, 18432:19456]
        identb_v = ARBIG.bitcast(BF16)[:, 19456:19584]
        slabx_v = ARB2
        h1_v = ARB2
        gfb_v = ARB2
        a_v = AR2[:, 0:1024]
        art = [AR2[:, 1024:1536], AR2[:, 1536:2048],
               AR2[:, 3584:4096]]
        yacc_v = AR2[:, 3072:3584]
        h_v = AR2.bitcast(BF16)[:, 4096:5120]
        ymul_v = AR2.bitcast(BF16)[:, 5120:6144]
        gvm_bc = AR2.bitcast(BF16)
        ysl_v = AR2[:, 0:4096].bitcast(BF16)
        xsf_v = ARX[:, 0:CAP]
        xssl_v = ARX
        gacc_v = ARX[:, 0:2048]
        gtmp_v = ARX[:, 2048:4096]
        xrow_v = ARX.bitcast(F32)[:, 0:1024]
        x2sl_v = ARW[:, 0:4096]
        w2t_v = ARW[:, 4096:8192]
        ottm_v = ARW.bitcast(F32)
        uev_v = ARE.bitcast(BF16)
        rbcev_v = castB[0:96, 0:1024]
        ffev_v = sgt.bitcast(BF16)[:, 0:512]
        rbf2_v = wsc[0:64, 1024:1536]
        gvrow = moepk.bitcast(BF16)[0:1, 4096:6144]
        dsgb = [stile.bitcast(BF16)[:, 0:512],
                resid.bitcast(BF16)[:, 0:512]]
        midx_v = moepk[:, 0:1024]
        mslot_v = moepk[:, 1024:2048]
        mkeep_v = moepk[:, 2048:3072]
        mwork_v = moepk[:, 3072:4096]

        def LD(dst, src, key, reads=(), eng="sync"):
            P.emit(eng, lambda e, d=dst, s=src: e.dma_start(out=d, in_=s),
                   reads=tuple(reads), writes=(key,), dma=True)

        def ST(dst, src, key, reads, eng="gpsimd"):
            P.emit(eng, lambda e, d=dst, s=src: e.dma_start(out=d, in_=s),
                   reads=tuple(reads), writes=(key,), dma=True)

        def MM(outp, lhsT, rhs, reads, writes, start, stop):
            P.emit("tensor",
                   lambda e, o=outp, l=lhsT, r=rhs, s1=start, s2=stop:
                   e.matmul(o, l, r, start=s1, stop=s2),
                   reads=tuple(reads), writes=tuple(writes))

        def MMC(outp, pairs, reads, writes, first=True, last=True):
            def fn(e, o=outp, ps=tuple(pairs), first=first, last=last):
                n = len(ps)
                for i, (l, r) in enumerate(ps):
                    mm = e.matmul(o, l, r, start=(first and i == 0),
                                  stop=(last and i == n - 1))
                return mm
            P.emit("tensor", fn, reads=tuple(reads), writes=tuple(writes))

        def SCOPY(outp, inp, reads, writes):
            P.emit("scalar", lambda e, o=outp, i=inp: e.copy(out=o, in_=i),
                   reads=tuple(reads), writes=tuple(writes))

        def TRANS(outp, inp, reads, writes):
            P.emit("tensor",
                   lambda e, o=outp, i=inp: e.transpose(o, i, ident_f[:, :]),
                   reads=tuple(reads) + ("ident_f",), writes=tuple(writes))

        def ACT(outp, inp, func, reads, writes, bias=0.0, scale=1.0):
            P.emit("scalar",
                   lambda e, o=outp, i=inp, f=func, b=bias, s=scale:
                   e.activation(out=o, in_=i, func=f, bias=b, scale=s),
                   reads=tuple(reads), writes=tuple(writes))

        def COPY(outp, inp, reads, writes, eng="vector"):
            P.emit(eng, lambda e, o=outp, i=inp: e.tensor_copy(out=o, in_=i),
                   reads=tuple(reads), writes=tuple(writes))

        def TT(outp, in0, in1, op, reads, writes, eng="vector"):
            P.emit(eng, lambda e, o=outp, a=in0, b=in1, p=op:
                   e.tensor_tensor(out=o, in0=a, in1=b, op=p),
                   reads=tuple(reads), writes=tuple(writes))

        def TS(outp, in0, s1, s2, op0, op1, reads, writes, eng="vector"):
            if s2 is None:
                op1 = AL.bypass
            P.emit(eng, lambda e, o=outp, a=in0, x=s1, y=s2, p=op0, q=op1:
                   e.tensor_scalar(out=o, in0=a, scalar1=x, scalar2=y, op0=p, op1=q),
                   reads=tuple(reads), writes=tuple(writes))

        def STT(outp, in0, scal, in1, op0, op1, reads, writes, eng="vector"):
            P.emit(eng, lambda e, o=outp, a=in0, s=scal, b=in1, p=op0, q=op1:
                   e.scalar_tensor_tensor(out=o, in0=a, scalar=s, in1=b, op0=p, op1=q),
                   reads=tuple(reads), writes=tuple(writes))

        def RED(outp, inp, op, reads, writes, negate=False):
            P.emit("vector", lambda e, o=outp, i=inp, p=op, n=negate:
                   e.tensor_reduce(o, i, axis=AX, op=p, negate=n),
                   reads=tuple(reads), writes=tuple(writes))

        def MEMSET(ap, val, writes, eng="vector"):
            P.emit(eng, lambda e, a=ap, v=val: e.memset(a, v), reads=(), writes=tuple(writes))

        def CC(kind, ins_ap, outs_ap, reads, writes):
            P.emit("gpsimd", lambda e, i=ins_ap, u=outs_ap, k=kind:
                   e.collective_compute(k, AL.bypass, replica_groups=[list(range(NCORE))],
                                        ins=[i], outs=[u]),
                   reads=tuple(reads), writes=tuple(writes))
            v = P.cnt["gpsimd"]
            P.gseq = getattr(P, "gseq", 0) + 1
            P.stream["gpsimd"].append((None, [("gpsimd", v)], None, P.gseq))
            P.seen["gpsimd"]["gpsimd"] = max(P.seen["gpsimd"].get("gpsimd", 0), v)

        def FENCE(new_keys, old_keys):
            if isinstance(new_keys, str):
                new_keys = (new_keys,)
            P.emit("vector", lambda e: e.memset(fencebuf[:, :], 0.0),
                   reads=(), writes=tuple(new_keys) + tuple(old_keys) + ("fencebuf",))

        _reg_cache = {}

        def dyn_ld(dst, handle, base_idx, const_off, apdims, key, reads=()):
            def fn(e, d=dst, h=handle, bi=base_idx, co=const_off, ap=apdims):
                rk = id(e)
                if rk not in _reg_cache:
                    _reg_cache[rk] = e.alloc_register("dynoff")
                r = _reg_cache[rk]
                e.reg_load(r, bases_s[0:1, bi:bi + 1])
                e.reg_add(r, r, co)
                try:
                    return e.dma_start(out=d, in_=bass.AP(h, r, ap))
                except Exception as ex:
                    print("DYN FAIL bi=", bi, "co=", co, ap, d.offset, ex)
                    raise
            P.emit("sync", fn, reads=tuple(reads) + ("bases_s",), writes=(key,), dma=True)

        MEMSET(onesf[:, :], 1.0, ("onesf",))
        MEMSET(cvals[:, 0:1], EPS, ("cvals",))
        MEMSET(cvals[:, 1:2], -0.5, ("cvals",))
        MEMSET(cvals[:, 2:3], -1.0, ("cvals",))
        MEMSET(cvals[:, 3:4], 1.0 / D, ("cvals",))
        MEMSET(cvals[:, 4:5], float(CAP), ("cvals",))
        MEMSET(ones1b[:, :], 1.0, ("ones1b",))
        MEMSET(nones1b[:, :], -1.0, ("nones1b",))
        MEMSET(ones1f[:, :], 1.0, ("ones1f",))
        MEMSET(carryc[:, :], 0.0, ("carryc",))
        P.emit("gpsimd", lambda e: e.affine_select(
            ident_f[:, :], onesf[:, 0:1].broadcast_to((128, 128)), pattern=[[1, 128]],
            base=0, channel_multiplier=-1, compare_op=AL.is_equal, fill=0.0),
            reads=("onesf",), writes=("ident_f",))
        P.emit("gpsimd", lambda e: e.affine_select(
            tri_f[:, :], onesf[:, 0:1].broadcast_to((128, 128)), pattern=[[1, 128]],
            base=0, channel_multiplier=-1, compare_op=AL.is_ge, fill=0.0),
            reads=("onesf",), writes=("tri_f",))
        P.emit("gpsimd", lambda e: e.iota(
            iota8[:, :], pattern=[[1, 8]], base=0, channel_multiplier=0,
            allow_small_or_imprecise_dtypes=True), reads=(), writes=("iota8",))
        P.emit("gpsimd", lambda e: e.iota(
            tokid[:, :], pattern=[[16, 1024]], base=0, channel_multiplier=1,
            allow_small_or_imprecise_dtypes=True), reads=(), writes=("tokid",))
        MEMSET(ARB2[0:32, 0:4096], 1.0, ("h1",))
        MEMSET(onescb[:, :], 1.0, ("onescb",))
        P.emit("gpsimd", lambda e: e.iota(
            iotar[:, :], pattern=[[1, 128]], base=0, channel_multiplier=0,
            allow_small_or_imprecise_dtypes=True), reads=(), writes=("iotar",))
        P.emit("gpsimd", lambda e: e.affine_select(
            selmat[:, :], ARB2[0:32, 0:4096], pattern=[[1, 32], [0, 128]], base=0,
            channel_multiplier=-1, compare_op=AL.is_equal, fill=0.0),
            reads=("h1",), writes=("selmat",))

        LD(consts[:, 0:16], bin_my.ap().transpose([1, 0]), "consts", eng="gpsimd")
        LD(consts[:, 56:64], bdt_my.ap().transpose([1, 0]), "consts", eng="gpsimd")
        LD(consts[:, 24:32], bout_i.ap().transpose([1, 0]), "consts", eng="gpsimd")
        LD(consts[:, 32:40], d_my.ap().transpose([1, 0]), "consts", eng="gpsimd")
        LD(consts[:, 40:48], lng_i.ap().transpose([1, 0]), "consts", eng="gpsimd")
        LD(consts[:, 48:56], lnb_i.ap().transpose([1, 0]), "consts", eng="gpsimd")
        TS(consts[:, 16:24], consts[:, 56:64], -1.0, None, AL.mult, AL.add,
           ("consts",), ("consts",))
        LD(b1_s[:, :], b1e.ap().transpose([1, 0]), "b1_s", eng="gpsimd")
        LD(b2_s[:, :], b2e.ap().transpose([1, 0]), "b2_s", eng="gpsimd")
        LD(gb_s[:, :], gb_i.ap(), "gb_s")
        LD(rbcb_s[:, :], rbc_bias.ap(), "rbcb_s")
        LD(prefm_s[:, :], prefmask.ap(), "prefm_s")
        LD(myexp_s[:, :], myexp_i.ap(), "myexp_s")
        LD(bases_s[:, :], bases_i.ap(), "bases_s")
        for k in range(8):
            LD(gwT_f[:, k * 8:(k + 1) * 8], gwT[k * 128:(k + 1) * 128, :], "gwT_f")
        for i in range(NBLK):
            LD(castA[:, i * N:(i + 1) * N], alog_my[i * 128:(i + 1) * 128, :], "castA")
        ACT(nega[:, :], castA[:, 0:NBLK * N], AF.Exp, ("castA",), ("nega",))

        for it in range(NT):
            xt = xtA if it % 2 == 0 else xtB
            kx = "xtA" if it % 2 == 0 else "xtB"
            sq = sqA if it % 2 == 0 else sqB
            ks = f"sq{it % 2}"
            st = lambda j: rstat[:, it * 8 + j: it * 8 + j + 1]
            kt = f"st{it}"
            LD(xt, xrows[it * 128:(it + 1) * 128, :], kx)
            for ln2 in range(2):
                RED(st(0), xt, AL.add, (kx,), (kt,))
                TT(sq, xt, xt, AL.mult, (kx, ks), (ks,), eng="gpsimd")
                RED(st(1), sq, AL.add, (ks,), (kt,))
                TS(st(2), st(0), 1.0 / D, None, AL.mult, AL.add, (kt,), (kt,))
                TT(st(3), st(2), st(2), AL.mult, (kt,), (kt,))
                STT(st(4), st(1), cvals[:, 3:4], st(3), AL.mult,
                    AL.subtract, (kt, "cvals"), (kt,))
                ACT(st(5), st(4), AF.Ln, (kt, "cvals"), (kt,), bias=cvals[:, 0:1])
                ACT(st(6), st(5), AF.Exp, (kt, "cvals"), (kt,), scale=cvals[:, 1:2])
                if ln2 == 0:
                    TS(xt, xt, st(2), st(6), AL.subtract, AL.mult, (kx, kt), (kx,))
                else:
                    TS(sq, xt, st(2), st(6), AL.subtract, AL.mult, (kx, kt), (ks,))
            for fb in range(8):
                cbs = castB[:, fb * 128:(fb + 1) * 128]
                kcb, kxn, kx2 = f"cB{fb}", f"xns{fb}", f"x2s{fb}"
                pcr = psC[:, (fb % 4) * 128:(fb % 4 + 1) * 128]
                kpc = f"psC{fb % 4}"
                pt2 = psT if fb % 2 == 0 else psT2
                kpt = "psT" if fb % 2 == 0 else "psT2"
                P.emit("tensor", lambda e, pcr=pcr, xt=xt, fb=fb: e.transpose(
                    pcr, xt[:, fb * 128:(fb + 1) * 128], ident_f[:, :]),
                    reads=(kx, "ident_f", kpc), writes=(kpc,))
                COPY(cbs, pcr, (kpc, kcb), (kcb,))
                SCOPY(xnstage[:, fb * 128:(fb + 1) * 128], cbs, (kcb, kxn), (kxn,))
                MM(psA[0:8, 0:128], gwT_f[:, fb * 8:(fb + 1) * 8], cbs,
                   ("gwT_f", kcb), ("psA",), start=(fb == 0), stop=(fb == 7))
                P.emit("tensor", lambda e, pt2=pt2, sq=sq, fb=fb: e.transpose(
                    pt2[:, :], sq[:, fb * 128:(fb + 1) * 128], ident_f[:, :]),
                    reads=(ks, "ident_f", kpt), writes=(kpt,))
                P.emit("vector", lambda e, fb=fb, pt2=pt2: e.tensor_scalar(
                    out=x2stage[:, fb * 128:(fb + 1) * 128], in0=pt2[:, :],
                    scalar1=consts[:, 40 + fb:41 + fb], scalar2=consts[:, 48 + fb:49 + fb],
                    op0=AL.mult, op1=AL.add),
                    reads=(kpt, "consts", kx2), writes=(kx2,))
            ST(bass.AP(ag1x, it * 128, [[TPC, 128], [128 * TPC, 8], [1, 128]]),
               xnstage[:, :], "ag1x", tuple(f"xns{fb}" for fb in range(8)))
            ST(bass.AP(ag1z, it * 128, [[TPC, 128], [128 * TPC, 8], [1, 128]]),
               x2stage[:, :], "ag1z", tuple(f"x2s{fb}" for fb in range(8)))
            TS(lg_fm[:, it * 128:(it + 1) * 128], psA[0:8, 0:128], gb_s[:, 0:1], None,
               AL.add, AL.add, ("psA", "gb_s"), ("lg_fm",))

        for it in range(NT):
            Ccol = lambda j, it=it: rcols[:, it * 10 + j: it * 10 + j + 1]
            ohs = oh_tm[:, it * 8:(it + 1) * 8]
            lgs = lg_tm[:, it * 8:(it + 1) * 8]
            P.emit("tensor", lambda e, it=it: e.transpose(
                psT[:, 0:8], lg_fm[:, it * 128:(it + 1) * 128], ident_f[0:8, 0:8]),
                reads=("lg_fm", "ident_f"), writes=("psT",))
            COPY(lgs, psT[:, 0:8], ("psT",), ("lg_tm",))
            RED(Ccol(0), lgs, AL.max, ("lg_tm",), (f"r{it}",), negate=True)
            ACT(ohs, lgs, AF.Exp, ("lg_tm", f"r{it}"), ("oh_tm",), bias=Ccol(0))
            RED(Ccol(1), ohs, AL.add, ("oh_tm",), (f"r{it}",))
            P.emit("vector", lambda e, it=it: e.reciprocal(
                rcols[:, it * 10 + 2: it * 10 + 3], rcols[:, it * 10 + 1: it * 10 + 2]),
                reads=(f"r{it}",), writes=(f"r{it}",))
            TS(Ccol(3), Ccol(0), -1.0, None, AL.mult, AL.add, (f"r{it}",), (f"r{it}",))
            TS(ohs, lgs, Ccol(3), None, AL.is_equal, AL.add,
               ("lg_tm", f"r{it}"), ("oh_tm",))
            TT(pos_tm[:, it * 8:(it + 1) * 8], ohs, iota8[:, :], AL.mult,
               ("oh_tm", "iota8"), ("pos_tm",))
            RED(Ccol(4), pos_tm[:, it * 8:(it + 1) * 8], AL.add, ("pos_tm",), (f"r{it}",))
            MM(psT2[0:8, it:it + 1], ohs, onesf[:, 0:1], ("oh_tm", "onesf"), ("psT2",),
               start=True, stop=True)
        cnts_c = moepk[0:8, 3968:3984]
        cnts16 = moepk[0:16, 3840:3848]
        mb16 = moepk[0:16, 3848:3856]
        mrow16 = moepk[0:1, 3856:3984]
        COPY(cnts_c, psT2[0:8, 0:16], ("psT2", "cnts16"), ("cnts16",))
        P.emit("tensor", lambda e: e.transpose(
            psT[0:16, 0:8], cnts_c, ident_f[0:8, 0:8]),
            reads=("cnts16", "ident_f"), writes=("psT",))
        COPY(cnts16, psT[0:16, 0:8], ("psT", "cnts16"), ("cnts16",))
        MM(ps8[0:1, 0:8], onesf[0:16, 0:1], cnts16, ("onesf", "cnts16"), ("ps8",),
           start=True, stop=True)
        COPY(rtmp8[0:1, 0:8], ps8[0:1, 0:8], ("ps8",), ("cntcol",))
        ST(cntin.ap(), rtmp8[0:1, 0:8], "cntin", ("cntcol",))

        CC("AllGather", cntin.ap().opt(), cntout.ap().opt(), ("cntin",), ("cntout",))
        LD(cnt_s[:, :], cntout.ap(), "cnt_s", reads=("cntout",))
        MM(ps8[0:1, 0:8], prefm_s[:, :], cnt_s[:, :], ("prefm_s", "cnt_s"), ("ps8",),
           start=True, stop=True)
        COPY(rtmp8[0:1, 0:8], ps8[0:1, 0:8], ("ps8",), ("baserow",))
        MM(ps8[0:128, 0:8], ones1f[0:1, 0:128], rtmp8[0:1, 0:8], ("ones1f", "baserow"),
           ("ps8",), start=True, stop=True)
        COPY(base8[:, :], ps8[0:128, 0:8], ("ps8",), ("base8",))
        MM(psT2[0:16, 0:8], tri_f[0:16, 0:16], cnts16, ("tri_f", "cnts16"), ("psT2",),
           start=True, stop=True)
        TT(mb16, psT2[0:16, 0:8], cnts16, AL.subtract, ("psT2", "cnts16", "mb16"),
           ("mb16",))
        TT(mb16, mb16, base8[0:16, 0:8], AL.add, ("mb16", "base8"), ("mb16",))
        for it in range(NT):
            LD(mrow16[0:1, it * 8:(it + 1) * 8], mb16[it:it + 1, 0:8], "mrow16",
               reads=("mb16", "mrow16"), eng="sync")

        for it in range(NT):
            Ccol = lambda j, it=it: rcols[:, it * 10 + j: it * 10 + j + 1]
            ohs = oh_tm[:, it * 8:(it + 1) * 8]
            MM(psT2[:, 0:8], tri_f[:, :], ohs, ("tri_f", "oh_tm"), ("psT2",),
               start=True, stop=True)
            COPY(pos_tm[:, it * 8:(it + 1) * 8], psT2[:, 0:8], ("psT2",), ("pos_tm",))
            MM(ps8[0:128, 0:8], ones1f[0:1, 0:128], mrow16[0:1, it * 8:(it + 1) * 8],
               ("ones1f", "mrow16"), ("ps8",), start=True, stop=True)
            TT(rtmp8[:, :], pos_tm[:, it * 8:(it + 1) * 8], ps8[0:128, 0:8], AL.add,
               ("pos_tm", "ps8"), ("rtmp8",))
            TS(rtmp8[:, :], rtmp8[:, :], 1.0, None, AL.subtract, AL.add,
               ("rtmp8",), ("rtmp8",))
            TT(rtmp8[:, :], rtmp8[:, :], ohs, AL.mult, ("rtmp8", "oh_tm"), ("rtmp8",))
            RED(Ccol(9), rtmp8[:, :], AL.add, ("rtmp8",), (f"r{it}",))
            TS(Ccol(6), Ccol(9), float(CAP), None, AL.is_lt, AL.add, (f"r{it}",), (f"r{it}",))
            TT(Ccol(5), Ccol(9), Ccol(6), AL.mult, (f"r{it}",), (f"r{it}",))
            TT(Ccol(7), Ccol(2), Ccol(6), AL.mult, (f"r{it}",), (f"r{it}",))
            P.emit("tensor", lambda e, it=it: e.transpose(
                psT[0:4, 0:128], rcols[:, it * 10 + 4: it * 10 + 8], ident_f[:, :]),
                reads=(f"r{it}", "ident_f"), writes=("psT",))
            mst = moepk[0:4, 3584 + (it % 2) * 128: 3712 + (it % 2) * 128]
            SCOPY(mst, psT[0:4, 0:128], ("psT", f"mst{it % 2}"), (f"mst{it % 2}",))
            ST(metain.ap()[0:4, it * 128:(it + 1) * 128], mst, "metain", (f"mst{it % 2}",))
        CC("AllGather", ag1z.ap().opt(), ag1zout.ap().opt(), ("ag1z",), ("ag1zout",))
        CC("AllGather", metain.ap().opt(), metaout.ap().opt(), ("metain",), ("metaout",))
        CC("AllGather", ag1x.ap().opt(), ag1xout.ap().opt(), ("ag1x",), ("ag1xout",))
        for half in range(2):
            dyn_ld(myx2.ap()[half * EH:(half + 1) * EH, :], ag1zout,
                   half, 0, [[TPC, EH], [1, TPC]], "myx2", reads=("ag1zout",))

        PRI = []
        BULK = []
        _ps = [0]

        def take_ps():
            i = _ps[0] % 2
            _ps[0] += 1
            return (psA, "psA") if i == 0 else (psC, "psC")

        def pop_fill():
            if BULK:
                BULK.pop(0)()

        def inproj_pieces(ck):
            half = ck // 4
            tcl = ck % 4
            ps_list = []

            def p_x2sl(ck=ck, half=half, tcl=tcl):
                for k in range(8):
                    LD(x2sl_v[:, k * 512:(k + 1) * 512],
                       myx2.ap()[half * EH + k * 128: half * EH + (k + 1) * 128,
                                 tcl * 512:(tcl + 1) * 512],
                       "x2sl", reads=("myx2", "x2sl"))
            ps_list.append(p_x2sl)
            for m in range(16):
                def p_m(ck=ck, m=m):
                    par = m % 2
                    LD(wpar_v[:, par * 1024:(par + 1) * 1024],
                       winsp.ap()[m * 128:(m + 1) * 128, :], f"wpar{par}",
                       reads=("winsp", f"wpar{par}"))
                    ps, pk = take_ps()
                    MMC(ps[:, :],
                        [(wpar_v[:, par * 1024 + k * 128: par * 1024 + (k + 1) * 128],
                          x2sl_v[:, k * 512:(k + 1) * 512]) for k in range(8)],
                        (f"wpar{par}", "x2sl"), (pk,))
                    ev = uev_v[:, (m % 8) * 512:(m % 8 + 1) * 512]
                    evk = f"uev{m % 8}"
                    ACT(ev, ps[:, :], AF.Silu, (pk, "consts", evk), (evk,),
                        bias=consts[:, m:m + 1])
                    if m == 7:
                        ST(bass.AP(u_sp, ck * 512, [[TB, 128], [128 * TB, 8], [1, 512]]),
                           uev_v[:, 0:4096], "u_sp", tuple(f"uev{j}" for j in range(8)))
                        MMC(ps96[:, :],
                            [(wot[:, j * 96:(j + 1) * 96], uev_v[:, j * 512:(j + 1) * 512])
                             for j in range(8)],
                            ("wot",) + tuple(f"uev{j}" for j in range(8)), ("ps96",))
                        ev96 = rbcev_v[:, (ck % 2) * 512:(ck % 2 + 1) * 512]
                        COPY(ev96, ps96[:, :], ("ps96", f"rbcev{ck % 2}"),
                             (f"rbcev{ck % 2}",))
                        ST(ag3in[ck * 96:(ck + 1) * 96, :], ev96, "ag3in",
                           (f"rbcev{ck % 2}",))
                    if m == 15:
                        ST(bass.AP(gate_sp, ck * 512, [[TB, 128], [128 * TB, 8], [1, 512]]),
                           uev_v[:, 0:4096], "gate_sp", tuple(f"uev{j}" for j in range(8)))
                ps_list.append(p_m)
            return ps_list

        def ag3cc(ck):
            CC("AllGather", ag3in.ap()[ck * 96:(ck + 1) * 96, :].opt(),
               ag3out.ap()[ck * 768:(ck + 1) * 768, :].opt(),
               ("ag3in",), (f"ag3o{ck}",))

        def rbc(ck):
            dyn_ld(AR2[0:96, 1024:2048], ag3out, 2, ck * 8 * 96 * 512,
                   [[512, 96], [96 * 512, 2], [1, 512]], "art0",
                   reads=(f"ag3o{ck}", "art0", "art1"))
            TT(art[0][0:96, :], art[0][0:96, :], art[1][0:96, :], AL.add,
               ("art0",), ("art0", "art1"))
            TS(art[0][0:96, :], art[0][0:96, :], rbcb_s[:, :], None, AL.add, AL.add,
               ("art0", "rbcb_s"), ("art0",))
            COPY(rbf2_v, art[0][0:64, :], ("art0", "rbf2"), ("rbf2",))
            COPY(bmcm_b[:, ck * 512:(ck + 1) * 512], art[0][64:96, :],
                 ("art0", "bmcm"), ("bmcm",))
            TS(bmcm_b[0:16, ck * 512:(ck + 1) * 512],
               bmcm_b[0:16, ck * 512:(ck + 1) * 512], -1.0, None, AL.mult, AL.add,
               ("bmcm",), ("bmcm",))

        def delta(ck):
            groups = [(0, 3), (3, 3), (6, 2)]
            for g0, gn in groups:
                for j in range(gn):
                    blk = g0 + j
                    ps, pk = take_ps()
                    MM(ps[:, :], wsc[0:64, blk * 128:(blk + 1) * 128], rbf2_v,
                       ("wsc", "rbf2"), (pk,), start=True, stop=True)
                    ACT(art[j], ps[:, :], AF.Sigmoid,
                        (pk, "consts", "cvals", f"art{j}"), (f"art{j}",),
                        bias=consts[:, 16 + blk:17 + blk], scale=cvals[:, 2:3])
                for j in range(gn):
                    blk = g0 + j
                    db = dsgb[j % 2]
                    kdb = f"dsgb{j % 2}"
                    ACT(db, art[j], AF.Ln, (f"art{j}", kdb), (kdb,))
                    ST(lns_sp[blk * 128:(blk + 1) * 128, ck * 512:(ck + 1) * 512], db,
                       "lns_sp", (kdb,), eng="sync")

        def gather_pieces():
            out = []

            def p_f():
                FENCE(("slabx",), ("llistbc", "xsf"))
                FENCE(("xsfo",), ())
            out.append(p_f)
            for fb in range(8):
                def p_fb(fb=fb):
                    for r8 in range(NCORE):
                        LD(slabx_v[:, r8 * TPC:(r8 + 1) * TPC],
                           ag1xout.ap()[r8 * D + fb * 128: r8 * D + (fb + 1) * 128, :],
                           f"slx{r8}", reads=("ag1xout", f"slx{r8}"))
                    for part in range(3):
                        w = 1024 if part < 2 else 512
                        P.emit("gpsimd", lambda e, part=part, w=w: e.indirect_copy(
                            xsf_v[:, part * 1024: part * 1024 + w],
                            slabx_v[:, 0:NCORE * TPC],
                            cidx[:, part * 64: part * 64 + w // 16], True),
                            reads=tuple(f"slx{r}" for r in range(NCORE)) + ("cidx", "xsfo"),
                            writes=("xsfo",))
                    ST(xs_sp[fb * 128:(fb + 1) * 128, :], xsf_v, "xs_sp", ("xsfo",))
                out.append(p_fb)
            return out

        ag4state = [False, False]

        def ffn_pieces():
            out = []

            def p_fence():
                FENCE(("xssl", "h1"), ("xsfo", "slabx"))
            out.append(p_fence)
            for tc5 in range(5):
                def p_xl(tc5=tc5):
                    for k in range(8):
                        LD(xssl_v[:, k * 512:(k + 1) * 512],
                           xs_sp.ap()[k * 128:(k + 1) * 128, tc5 * 512:(tc5 + 1) * 512],
                           "xssl", reads=("xs_sp", "xssl"))
                out.append(p_xl)
                for f in range(32):
                    def p_f(tc5=tc5, f=f):
                        par = f % 2
                        LD(w1t[:, par * 1024:(par + 1) * 1024],
                           w1bf.ap()[f * 128:(f + 1) * 128, :], f"w1t{par}",
                           reads=("w1bf", f"w1t{par}"))
                        ps, pk = take_ps()
                        MMC(ps[:, :],
                            [(w1t[:, par * 1024 + k * 128: par * 1024 + (k + 1) * 128],
                              xssl_v[:, k * 512:(k + 1) * 512]) for k in range(8)],
                            (f"w1t{par}", "xssl"), (pk,))
                        ACT(h1_v[:, f * 512:(f + 1) * 512], ps[:, :], AF.Relu,
                            (pk, "b1_s", "h1"), ("h1",), bias=b1_s[:, f:f + 1])
                    out.append(p_f)
                for d8 in range(8):
                    def p_w2ld(d8=d8):
                        LD(w2t_v[:, :], w2bf.ap()[d8 * 128:(d8 + 1) * 128, :], "w2t",
                           reads=("w2bf", "w2t"))
                    out.append(p_w2ld)
                    psd = (psB, "psB")
                    for g in range(4):
                        def p_sub(tc5=tc5, d8=d8, g=g, psd=psd):
                            ps, pk = psd
                            MMC(ps[:, :],
                                [(w2t_v[:, (g * 8 + f) * 128:(g * 8 + f + 1) * 128],
                                  h1_v[:, (g * 8 + f) * 512:(g * 8 + f + 1) * 512])
                                 for f in range(8)],
                                ("w2t", "h1"), (pk,), first=(g == 0), last=(g == 3))
                        out.append(p_sub)

                    def p_ev(tc5=tc5, d8=d8, psd=psd):
                        ps, pk = psd
                        fe = ffev_v[:, 0:512]
                        TS(fe, ps[:, :], b2_s[:, d8:d8 + 1], None, AL.add, AL.add,
                           (pk, "b2_s", "ffev0"), ("ffev0",))
                        ST(ag4in[d8 * 128:(d8 + 1) * 128, tc5 * 512:(tc5 + 1) * 512], fe,
                           "ag4in", ("ffev0",))
                    out.append(p_ev)
            return out

        def scan_q(q, mid=None):
            while PRI:
                PRI.pop(0)()
            if q == 0:
                FENCE(("bbc", "cbc"), ("lbctab",))
                FENCE(("a0", "a1", "ym0", "ym1", "h0", "h1p", "yac"), ())
                FENCE(("ygt0", "ygt1"), ())
            for n in range(N):
                MM(psC[:, :], selmat[:, n * 128:(n + 1) * 128],
                   bmcm_b[:, q * QT:(q + 1) * QT],
                   ("bmcm", "selmat"), ("psC",), start=True, stop=True)
                SCOPY(bbc_v[:, n * QT:(n + 1) * QT], psC[:, :], ("psC", "bbc"), ("bbc",))
                MM(psC[:, :], selmat[:, (16 + n) * 128:(17 + n) * 128],
                   bmcm_b[:, q * QT:(q + 1) * QT],
                   ("bmcm", "selmat"), ("psC",), start=True, stop=True)
                SCOPY(cbc_v[:, n * QT:(n + 1) * QT], psC[:, :], ("psC", "cbc"), ("cbc",))
            for blk in range(NBLK):
                if blk == 2 and mid is not None:
                    mid()
                if blk in (3, 6):
                    npop = (len(PRI) + 1) // 2 if blk == 3 else len(PRI)
                    for _ in range(npop):
                        if PRI:
                            PRI.pop(0)()
                pb = blk % 2
                lk, uk, gk, ygk = f"lns{pb}", f"ut{pb}", f"gt{pb}", f"ygt{pb}"
                lns = lns_t[:, pb * QT:(pb + 1) * QT]
                ut = u_t[:, pb * QT:(pb + 1) * QT]
                gt = gate_t[:, pb * QT:(pb + 1) * QT]
                ygs = ygt[:, pb * QT:(pb + 1) * QT]
                LD(lns, lns_sp.ap()[blk * 128:(blk + 1) * 128, q * QT:(q + 1) * QT],
                   lk, reads=("lns_sp", lk))
                LD(ut, u_sp.ap()[blk * 128:(blk + 1) * 128, q * QT:(q + 1) * QT],
                   uk, reads=("u_sp", uk))
                LD(gt, gate_sp.ap()[blk * 128:(blk + 1) * 128, q * QT:(q + 1) * QT],
                   gk, reads=("gate_sp", gk))
                TT(ndu[:, :], lns, ut, AL.mult, (lk, uk, "ndu"), ("ndu",))
                for n in range(N):
                    pn = n % 2
                    ak, hk, mk = f"a{pn}", (f"h{pn}" if pn == 0 else "h1p"), f"ym{pn}"
                    av = a_v[:, pn * 512:(pn + 1) * 512]
                    hv = h_v[:, pn * 512:(pn + 1) * 512]
                    mv = ymul_v[:, pn * 512:(pn + 1) * 512]
                    ACT(av, lns, AF.Exp, (lk, "nega", ak), (ak,),
                        scale=nega[:, blk * N + n: blk * N + n + 1])
                    TT(b_t[:, :], ndu[:, :], bbc_v[:, n * QT:(n + 1) * QT], AL.mult,
                       ("ndu", "bbc", "b_t"), ("b_t",))
                    P.emit("vector", lambda e, q=q, blk=blk, n=n, hv=hv, av=av:
                           e.tensor_tensor_scan(
                        out=hv, data0=av, data1=b_t[:, :],
                        initial=(0.0 if q == 0 else carryc[:, blk * N + n: blk * N + n + 1]),
                        op0=AL.mult, op1=AL.add),
                        reads=(ak, "b_t", hk) + (() if q == 0 else ("carryc",)),
                        writes=(hk,))
                    COPY(carryc[:, blk * N + n: blk * N + n + 1], hv[:, QT - 1:QT],
                         (hk, "carryc"), ("carryc",))
                    TT(mv, hv, cbc_v[:, n * QT:(n + 1) * QT], AL.mult,
                       (hk, "cbc", mk), (mk,))
                    if n == 0:
                        COPY(yacc_v, mv, (mk, "yac"), ("yac",))
                    else:
                        TT(yacc_v, yacc_v, mv, AL.add, ("yac", mk), ("yac",))
                    if n % 2 == 1:
                        pop_fill()
                STT(yacc_v, ut, consts[:, 32 + blk:33 + blk], yacc_v,
                    AL.mult, AL.add, (uk, "consts", "yac"), ("yac",))
                TT(ygs, yacc_v, gt, AL.mult, ("yac", gk, ygk), (ygk,))
                ST(ag5in[q * EH + blk * 128: q * EH + (blk + 1) * 128, :], ygs,
                   "ag5in", (ygk,), eng="sync")

        def ag5cc(q):
            CC("AllGather", ag5in.ap()[q * EH:(q + 1) * EH, :].opt(),
               ag5out.ap()[q * 8 * EH:(q + 1) * 8 * EH, :].opt(),
               ("ag5in",), (f"ag5o{q}",))

        FENCE(("x2sl",), ())
        FENCE(("psC",), ("psC0", "psC1", "psC2", "psC3"))
        for k in range(8):
            LD(castB[:, 0:96], wdbc_my[k * 128:(k + 1) * 128, :], "castB")
            COPY(wot[:, k * 96:(k + 1) * 96],
                 castB[:, 0:96], ("castB", "wot"), ("wot",))
        LD(castA[0:64, :], wdt_my.ap(), "castA")
        COPY(wsc[0:64, 0:1024], castA[0:64, :], ("castA", "wsc"), ("wsc",))
        FENCE(("rbcev0", "rbcev1"), ("castB",))
        FENCE(("spb0", "spb1"), ("castA",))
        FENCE(("spo0", "spo1"), ())

        _spn = [0]

        def spill_tile(src, rr, cc, dst, dstw, key):
            j = _spn[0] % 2
            _spn[0] += 1
            bsl = castA[:, j * 512:(j + 1) * 512]
            osl = (stile if j == 0 else resid).bitcast(BF16)[:, 512:1024]
            out_ap = bass.AP(dst, cc * 4 * 128 * dstw + rr * 128,
                             [[dstw, 128], [128 * dstw, 4], [1, 128]])
            kb, ko = f"spb{j}", f"spo{j}"
            LD(bsl, src[rr * 128:(rr + 1) * 128, cc * 512:(cc + 1) * 512], kb,
               reads=(kb,), eng="scalar")
            SCOPY(osl, bsl, (kb, ko), (ko,))
            ST(out_ap, osl, key, (ko,), eng="scalar")

        for rr in range(8):
            for cc in range(4):
                spill_tile(win_my.ap(), rr, cc, winsp, D, "winsp")

        def spill_pieces():
            out = []
            for rr in range(8):
                for cc in range(8):
                    out.append(lambda rr=rr, cc=cc: spill_tile(
                        w1e.ap(), rr, cc, w1bf, D, "w1bf"))
            for rr in range(32):
                for cc in range(2):
                    out.append(lambda rr=rr, cc=cc: spill_tile(
                        w2e.ap(), rr, cc, w2bf, FF, "w2bf"))
            for rr in range(16):
                for cc in range(2):
                    out.append(lambda rr=rr, cc=cc: spill_tile(
                        wout_i.ap(), rr, cc, woutbf, E, "woutbf"))
            return out

        FENCE(("diagD",), ())
        COPY(identb_v[:, 0:128], ident_f[:, :], ("ident_f", "diagD"), ("diagD",))
        for blk in range(NBLK):
            TS(diagD_v[:, blk * 128:(blk + 1) * 128], identb_v[:, 0:128],
               consts[:, 32 + blk:33 + blk], None, AL.mult, AL.add,
               ("diagD", "consts"), ("diagD",))

        for go in inproj_pieces(0):
            go()
        ag3cc(0)
        rbc(0)
        delta(0)
        for go in inproj_pieces(1):
            go()
        ag3cc(1)
        FENCE(("lbctab",), ("xtA", "xtB", "sq0", "sq1", "lg_fm"))
        FENCE(("castA",), ("spb0", "spb1"))
        FENCE(("xsf",), ("llistbc",))
        for r8 in range(NCORE):
            LD(castA[:, r8 * 16:(r8 + 1) * 16],
               bass.AP(metaout, (r8 * 4 + 0) * TPC, [[1, 128], [128, 16]]),
               "castA", reads=("metaout",))
            LD(castA[:, 128 + r8 * 16: 128 + (r8 + 1) * 16],
               bass.AP(metaout, (r8 * 4 + 2) * TPC, [[1, 128], [128, 16]]),
               "castA", reads=("metaout",))
        TS(castB[:, 0:128], castA[:, 0:128], myexp_s[:, 0:1], None, AL.is_equal, AL.add,
           ("castA", "myexp_s", "castB"), ("castB",))
        TT(castA[:, 256:384], castB[:, 0:128], castA[:, 128:256], AL.mult,
           ("castB", "castA"), ("castA",))
        MM(psT[:, :], tri_f[:, :], castA[:, 256:384], ("tri_f", "castA"), ("psT",),
           start=True, stop=True)
        COPY(castA[:, 384:512], psT[:, :], ("psT", "castA"), ("castA",))
        LD(castB[0:1, 128:256], castA[127:128, 384:512], "castB", reads=("castA",),
           eng="sync")
        P.emit("tensor", lambda e: e.transpose(
            psT2[:, 0:1], castB[0:1, 128:256], ident_f[0:1, 0:1]),
            reads=("castB", "ident_f"), writes=("psT2",))
        COPY(castB[:, 256:257], psT2[:, 0:1], ("psT2", "castB"), ("castB",))
        MM(psT2[:, 0:1], tri_f[:, :], castB[:, 256:257], ("tri_f", "castB"), ("psT2",),
           start=True, stop=True)
        COPY(castB[:, 257:258], psT2[:, 0:1], ("psT2", "castB"), ("castB",))
        TT(castB[:, 258:259], castB[:, 257:258], castB[:, 256:257], AL.subtract,
           ("castB",), ("castB",))
        P.emit("tensor", lambda e: e.transpose(
            psT2[0:1, 0:128], castB[:, 258:259], ident_f[:, :]),
            reads=("castB", "ident_f"), writes=("psT2",))
        COPY(castB[0:1, 384:512], psT2[0:1, 0:128], ("psT2", "castB"), ("castB",))
        for pp in range(16):
            LD(moepk[pp:pp + 1, 0:128], castB[0:1, 384:512], "moepk",
               reads=("castB",), eng="sync")
        acc_i = moepk[:, 1024:1184]
        acc_p = moepk[:, 1184:1344]
        cmp_t = moepk[:, 1344:1504]
        tmp_t = moepk[:, 1504:1664]
        MEMSET(acc_i, 0.0, ("moepk",))
        MEMSET(acc_p, 0.0, ("moepk",))
        for i in range(128):
            TS(cmp_t, tokid[:, 0:160], moepk[:, i:i + 1], None, AL.is_ge, AL.add,
               ("tokid", "moepk"), ("moepk",))
            TT(acc_i, acc_i, cmp_t, AL.add, ("moepk",), ("moepk",))
            TS(tmp_t, cmp_t, moepk[:, i:i + 1], None, AL.mult, AL.add,
               ("moepk",), ("moepk",))
            TT(acc_p, acc_p, tmp_t, AL.max, ("moepk",), ("moepk",))
        TS(acc_i, acc_i, 1.0, None, AL.subtract, AL.add, ("moepk",), ("moepk",))
        TT(cmp_t, tokid[:, 0:160], acc_p, AL.subtract, ("tokid", "moepk"), ("moepk",))
        TS(tmp_t, acc_i, 128.0, None, AL.mult, AL.add, ("moepk",), ("moepk",))
        TT(tmp_t, tmp_t, cmp_t, AL.add, ("moepk",), ("moepk",))
        P.emit("vector", lambda e: e.tensor_scalar_min(tmp_t, tmp_t, 16383.0),
               reads=("moepk",), writes=("moepk",))
        COPY(comp16[:, 0:160], tmp_t, ("moepk", "comp16"), ("comp16",))
        for g in range(8):
            P.emit("sync", lambda e, g=g: e.dma_start(
                out=cidx[g * 16:(g + 1) * 16, :], in_=comp16[:, :]),
                reads=("comp16",), writes=("cidx",), dma=True)
        FENCE(("llistbc",), ("h1",))
        for i in range(128):
            TS(ARB2[:, i * 128:(i + 1) * 128], iotar[:, :], castA[:, 384 + i:385 + i],
               None, AL.is_ge, AL.add, ("iotar", "castA", "llistbc"), ("llistbc",))
        for ch in range(32):
            MM(psC[0:1, 0:512], onescb[:, 0:1], ARB2[:, ch * 512:(ch + 1) * 512],
               ("onescb", "llistbc"), ("psC",), start=True, stop=True)
            COPY(castB.bitcast(BF16)[0:1, 1024:1536], psC[0:1, 0:512],
                 ("psC", "castB"), ("castB",))
            MM(psA[:, :], ones1b[0:1, :], castB.bitcast(BF16)[0:1, 1024:1536],
               ("ones1b", "castB"), ("psA",), start=True, stop=True)
            COPY(ARBIG.bitcast(BF16)[:, ch * 512:(ch + 1) * 512], psA[:, :],
                 ("psA", "lbctab"), ("lbctab",))
        mr0 = moepk[:, 128:288]
        mr1 = moepk[:, 288:448]
        idr = moepk[:, 448:608]
        for r2 in range(2):
            mr = mr0 if r2 == 0 else mr1
            TS(mr, tmp_t, float(8192 * r2), None, AL.is_ge, AL.add, ("moepk",), ("moepk",))
            if r2 == 0:
                TS(idr, tmp_t, 8192.0, None, AL.is_lt, AL.add, ("moepk",), ("moepk",))
                TT(mr, mr, idr, AL.mult, ("moepk",), ("moepk",))
            TS(idr, tmp_t, float(8192 * r2), None, AL.subtract, AL.add,
               ("moepk",), ("moepk",))
            TT(idr, idr, mr, AL.mult, ("moepk",), ("moepk",))
            COPY(comp16[:, 0:160], idr, ("moepk", "comp16"), ("comp16",))
            for g in range(8):
                P.emit("sync", lambda e, g=g: e.dma_start(
                    out=cidx[g * 16:(g + 1) * 16, :], in_=comp16[:, :]),
                    reads=("comp16",), writes=("cidx",), dma=True)
            for part in range(3):
                w = 1024 if part < 2 else 512
                P.emit("gpsimd", lambda e, part=part, w=w, r2=r2: e.indirect_copy(
                    ARB2[:, r2 * 2560 + part * 1024:
                         r2 * 2560 + part * 1024 + w],
                    ARBIG.bitcast(BF16)[:, 8192 * r2: 8192 * (r2 + 1)],
                    cidx[:, part * 64: part * 64 + w // 16], True),
                    reads=("lbctab", "cidx", "xsf"), writes=("xsf",))
            ST(lv_hbm.ap()[0:1, :], ARB2[0:1, r2 * 2560: r2 * 2560 + 2560],
               f"lv_hbm{r2}", ("xsf",), eng="gpsimd")
            LD(moepk.bitcast(BF16)[:, 7600 + r2 * 160: 7760 + r2 * 160],
               bass.AP(lv_hbm, 0, [[1, 16], [16, 160]]), f"lvint{r2}",
               reads=(f"lv_hbm{r2}", "moepk"))
        COPY(cmp_t, moepk.bitcast(BF16)[:, 7600:7760], ("moepk",), ("moepk",))
        TT(cmp_t, cmp_t, mr0, AL.mult, ("moepk",), ("moepk",))
        COPY(idr, moepk.bitcast(BF16)[:, 7760:7920], ("moepk",), ("moepk",))
        TT(idr, idr, mr1, AL.mult, ("moepk",), ("moepk",))
        TT(cmp_t, cmp_t, idr, AL.add, ("moepk",), ("moepk",))
        TS(tmp_t, acc_i, 128.0, None, AL.mult, AL.add, ("moepk",), ("moepk",))
        TT(tmp_t, tmp_t, cmp_t, AL.add, ("moepk",), ("moepk",))
        P.emit("vector", lambda e: e.tensor_scalar_min(tmp_t, tmp_t, 16383.0),
               reads=("moepk",), writes=("moepk",))
        COPY(comp16[:, 0:160], tmp_t, ("moepk", "comp16"), ("comp16",))
        for g in range(8):
            P.emit("sync", lambda e, g=g: e.dma_start(
                out=cidx[g * 16:(g + 1) * 16, :], in_=comp16[:, :]),
                reads=("comp16",), writes=("cidx",), dma=True)

        LD(midx_v[:, 0:128], bass.AP(metain, 0, [[1, 16], [16, 128]]), "moepk",
           reads=("metain",))
        LD(mkeep_v[:, 0:128], bass.AP(metain, TPC, [[1, 16], [16, 128]]), "moepk",
           reads=("metain",))
        STT(mwork_v[:, 0:128], midx_v[:, 0:128], cvals[0:16, 4:5], mkeep_v[:, 0:128],
            AL.mult, AL.add, ("moepk", "cvals"), ("moepk",))
        for r3 in range(3):
            mr = moepk[:, 608 + r3 * 128: 736 + r3 * 128]
            TS(mr, mwork_v[:, 0:128], float(8192 * r3), None, AL.is_ge, AL.add,
               ("moepk",), ("moepk",))
            if r3 < 2:
                TS(moepk[:, 992:1120] if False else midx_v[:, 128:256],
                   mwork_v[:, 0:128], float(8192 * (r3 + 1)), None, AL.is_lt, AL.add,
                   ("moepk",), ("moepk",))
                TT(mr, mr, midx_v[:, 128:256], AL.mult, ("moepk",), ("moepk",))
            TS(midx_v[:, 128:256], mwork_v[:, 0:128], float(8192 * r3), None,
               AL.subtract, AL.add, ("moepk",), ("moepk",))
            TT(midx_v[:, 128:256], midx_v[:, 128:256], mr, AL.mult,
               ("moepk",), ("moepk",))
            COPY(gidx16[0:16, r3 * 128:(r3 + 1) * 128], midx_v[:, 128:256],
                 ("moepk", "gidx16"), ("gidx16",))
            for g in range(1, 8):
                P.emit("sync", lambda e, g=g, r3=r3: e.dma_start(
                    out=gidx16[g * 16:(g + 1) * 16, r3 * 128:(r3 + 1) * 128],
                    in_=gidx16[0:16, r3 * 128:(r3 + 1) * 128]),
                    reads=("gidx16",), writes=("gidx16",), dma=True)
        gxr = rstat[:, 0:16]
        FENCE(("gxr",), tuple(f"st{it}" for it in range(NT)))
        for r3 in range(3):
            mr = moepk[:, 608 + r3 * 128: 736 + r3 * 128]
            P.emit("tensor", lambda e, mr=mr: e.transpose(
                psT[0:128, 0:16], mr, ident_f[0:16, 0:16]),
                reads=("moepk", "ident_f"), writes=("psT",))
            SCOPY(gxr, psT[0:128, 0:16], ("psT", "gxr"), ("gxr",))
            ST(bass.AP(mrow_hbm, r3 * TPC, [[16, 128], [1, 16]]), gxr,
               f"mrowh{r3}", ("gxr",), eng="gpsimd")

        FENCE(("spb0", "spb1"), ("castA",))
        BULK.extend(gather_pieces())
        BULK.extend(spill_pieces())
        BULK.extend(ffn_pieces())
        for ck in range(NQ):
            if ck + 2 < NQ:
                PRI.extend(inproj_pieces(ck + 2))

            def mid(ck=ck):
                if ck + 1 < NQ:
                    rbc(ck + 1)
                    delta(ck + 1)
            scan_q(ck, mid=mid)
            while PRI:
                PRI.pop(0)()
            if ck + 2 < NQ:
                ag3cc(ck + 2)
            if ck >= NQ - 3 and not BULK and not ag4state[0]:
                CC("AllGather", ag4in.ap().opt(), ag4out.ap().opt(),
                   ("ag4in",), ("ag4out",))
                ag4state[0] = True
            ag5cc(ck)
        while BULK:
            BULK.pop(0)()
        if not ag4state[0]:
            CC("AllGather", ag4in.ap().opt(), ag4out.ap().opt(), ("ag4in",), ("ag4out",))

        FENCE(("gvmbc",), ("a0", "a1", "ym0", "ym1", "h0", "h1p", "yac",
                           "art0", "art1", "art2"))
        FENCE(("gvbc", "graw"), ("xssl", "xsfo"))
        FENCE(("castA", "castB"), ("spb0", "spb1", "rbcev0", "rbcev1"))
        FENCE(("gvrow",), ("moepk",))
        for r3 in range(3):
            for hh in range(2):
                LD(castA[0:1, :], mrow_hbm.ap()[r3:r3 + 1, hh * 1024:(hh + 1) * 1024],
                   "castA", reads=(f"mrowh{r3}", "castA"))
                LD(castB[0:1, :], metain.ap()[3:4, hh * 1024:(hh + 1) * 1024],
                   "castB", reads=("metain", "castB"))
                TT(castA[0:1, :], castA[0:1, :], castB[0:1, :], AL.mult,
                   ("castA", "castB"), ("castA",))
                COPY(gvrow[:, hh * 1024:(hh + 1) * 1024], castA[0:1, :],
                     ("castA", "gvrow"), ("gvrow",))
            for tc in range(4):
                MM(psC[:, :], ones1b[0:1, :], gvrow[0:1, tc * 512:(tc + 1) * 512],
                   ("ones1b", "gvrow"), ("psC",), start=True, stop=True)
                COPY(gvm_bc[:, r3 * 2048 + tc * 512: r3 * 2048 + (tc + 1) * 512],
                     psC[:, :], ("psC", "gvmbc"), ("gvmbc",))
        FENCE(("gfb",), ("h1", "slabx"))
        FENCE(("gacck",), ("yac", "art2"))
        FENCE(tuple(f"sl{r8}" for r8 in range(NCORE)),
              ("bbc", "cbc", "wpar0", "wpar1", "diagD"))
        gacc2 = AR2.bitcast(BF16)[:, 6144:8192]
        for fb in range(8):
            for r8 in range(NCORE):
                LD(slab_v[:, r8 * CAP:(r8 + 1) * CAP],
                   ag4out.ap()[r8 * D + fb * 128: r8 * D + (fb + 1) * 128, :],
                   f"sl{r8}", reads=("ag4out", f"sl{r8}"))
            t2 = gfb_v[:, fb * TPC:(fb + 1) * TPC]

            for r3 in range(3):
                dst = (gacc_v, gtmp_v, t2)[r3]

                def icpair(e, dst=dst, r3=r3):
                    last = None
                    for part in range(2):
                        last = e.indirect_copy(
                            dst[:, part * 1024:(part + 1) * 1024],
                            slab_v[:, 8192 * r3: min(8192 * (r3 + 1), NCORE * CAP)],
                            gidx16[:, r3 * 128 + part * 64: r3 * 128 + (part + 1) * 64],
                            True)
                    return last
                P.emit("gpsimd", icpair,
                       reads=tuple(f"sl{r8}" for r8 in range(NCORE)) +
                             ("gidx16", f"gr{r3}") + (("gfb",) if r3 == 2 else ()),
                       writes=(f"gr{r3}",))
            TT(gacc2, gacc_v, gvm_bc[:, 0:2048], AL.mult,
               ("gr0", "gvmbc", "gacck"), ("gacck",))
            TT(gtmp_v, gtmp_v, gvm_bc[:, 2048:4096], AL.mult,
               ("gr1", "gvmbc"), ("gr1",))
            TT(gacc2, gacc2, gtmp_v, AL.add, ("gacck", "gr1"), ("gacck",))
            TT(t2, t2, gvm_bc[:, 4096:6144], AL.mult, ("gr2", "gvmbc"), ("gr2",))
            TT(gacc2, gacc2, t2, AL.add, ("gacck", "gr2"), ("gacck",))
            ACT(gfb_v[:, fb * TPC:(fb + 1) * TPC], gacc2, AF.Silu,
                ("gacck", "gfb", "gr2"), ("gfb",))

        FENCE(("ysl",), ("gvmbc", "gacck"))
        FENCE(("ottm",), ("x2sl", "w2t"))
        FENCE(("wot0", "wot1"), ("w1t0", "w1t1", "wot", "wsc"))
        FENCE(("stile", "resid", "sgt"), ("dsgb0", "dsgb1", "spo0", "spo1", "ffev0"))
        FENCE(("xrow_v",), ("gvbc", "graw"))
        for tc in range(4):
            for half in range(2):
                dyn_ld(ysl_v[:, half * 4096:(half + 1) * 4096], ag5out, 4 + half,
                       tc * 8 * EH * 512,
                       [[512, 128], [128 * 512, 8], [1, 512]], "ysl",
                       reads=(f"ag5o{tc}", f"ag5o{4 + tc}", "ysl"))
            for d8 in range(8):
                par = d8 % 2
                wbuf = wot if par == 0 else w1t
                LD(wbuf[:, 0:2048], woutbf.ap()[d8 * 128:(d8 + 1) * 128, :],
                   f"wot{par}", reads=("woutbf", f"wot{par}"))
                ps, pk = take_ps()
                MMC(ps[:, :],
                    [(wbuf[:, eb * 128:(eb + 1) * 128],
                      ysl_v[:, eb * 512:(eb + 1) * 512]) for eb in range(16)],
                    (f"wot{par}", "ysl"), (pk,))
                TS(stile[:, :], ps[:, :], consts[:, 24 + d8:25 + d8], None, AL.add, AL.add,
                   (pk, "consts", "stile"), ("stile",))
                LD(resid.bitcast(BF16)[:, 512:1024],
                   ag1x.ap()[d8 * 128:(d8 + 1) * 128, tc * 512:(tc + 1) * 512],
                   "resid", reads=("ag1x", "resid"))
                TT(stile[:, :], stile[:, :], resid.bitcast(BF16)[:, 512:1024], AL.add,
                   ("stile", "resid"), ("stile",))
                TT(sgt[:, :], stile[:, :],
                   gfb_v[:, d8 * TPC + tc * 512: d8 * TPC + (tc + 1) * 512],
                   AL.mult, ("stile", "gfb", "sgt"), ("sgt",))
                for sub in range(4):
                    pst = psT if sub % 2 == 0 else psT2
                    pstk = "psT" if sub % 2 == 0 else "psT2"
                    TRANS(pst[:, :], sgt[:, sub * 128:(sub + 1) * 128], ("sgt",), (pstk,))
                    SCOPY(ottm_v[:, sub * 1024 + d8 * 128: sub * 1024 + (d8 + 1) * 128],
                          pst[:, :], (pstk, "ottm"), ("ottm",))
            for sub in range(4):
                it = tc * 4 + sub
                LD(xrow_v, xrows[it * 128:(it + 1) * 128, :], "xrow_v",
                   reads=("xrow_v",))
                TT(ottm_v[:, sub * 1024:(sub + 1) * 1024],
                   ottm_v[:, sub * 1024:(sub + 1) * 1024], xrow_v, AL.add,
                   ("xrow_v", "ottm"), ("ottm",))
                ST(out_rows[it * 128:(it + 1) * 128, :], ottm_v[:, sub * 1024:(sub + 1) * 1024],
                   "out", ("ottm",))

        if limit is None:
            P.finalize_outputs()
        else:
            P.limit = limit

        @block.tensor
        def _(e):
            P.replay("tensor", e, sems)

        @block.vector
        def _(e):
            P.replay("vector", e, sems)

        @block.scalar
        def _(e):
            P.replay("scalar", e, sems)

        @block.gpsimd
        def _(e):
            P.replay("gpsimd", e, sems)

        @block.sync
        def _(e):
            P.replay("sync", e, sems)

    return nc


def _in_maps(inputs):
    f32 = lambda a: np.ascontiguousarray(a, dtype=np.float32)
    x = f32(inputs["L0_attn_x"]).reshape(NCORE * TPC, D)
    W_in = f32(inputs["L0_attn_W_in"])
    b_in = f32(inputs["L0_attn_b_in"])
    W_delta = f32(inputs["L0_attn_W_delta"])
    b_delta = f32(inputs["L0_attn_b_delta"])
    W_dt = f32(inputs["L0_attn_W_dt"])
    b_dt = f32(inputs["L0_attn_b_dt"])
    W_B = f32(inputs["L0_attn_W_B"])
    b_B = f32(inputs["L0_attn_b_B"])
    W_C = f32(inputs["L0_attn_W_C"])
    b_C = f32(inputs["L0_attn_b_C"])
    A_log = f32(inputs["L0_attn_A_log"])
    Dv = f32(inputs["L0_attn_D"])
    W_out = f32(inputs["L0_attn_W_out"])
    b_out = f32(inputs["L0_attn_b_out"])
    ln_g = f32(inputs["L0_attn_ln_g"])
    ln_b = f32(inputs["L0_attn_ln_b"])
    gate_w = f32(inputs["L0_ffn_gate_w"])
    gate_b = f32(inputs["L0_ffn_gate_b"])
    w1 = f32(inputs["L0_ffn_w1"])
    b1 = f32(inputs["L0_ffn_b1"])
    w2 = f32(inputs["L0_ffn_w2"])
    b2 = f32(inputs["L0_ffn_b2"])

    maps = []
    for c in range(NCORE):
        b, h = c // 2, c % 2
        sl = slice(h * EH, (h + 1) * EH)
        win = np.concatenate([W_in[:, h * EH:(h + 1) * EH],
                              W_in[:, E + h * EH: E + (h + 1) * EH]], axis=1)
        binv = np.concatenate([b_in[h * EH:(h + 1) * EH],
                               b_in[E + h * EH: E + (h + 1) * EH]]).reshape(16, 128)
        wdbc = np.concatenate([W_delta[sl], W_B[sl], W_C[sl]], axis=1)
        rbcb = np.concatenate([b_delta, b_B, b_C]).reshape(96, 1)
        bases = np.array([[
            (2 * b) * D * TPC,
            (2 * b + 1) * D * TPC,
            (2 * b) * 96 * 512,
            (2 * b + 1) * 96 * 512,
            4 * h * 8 * EH * 512 + (2 * b) * EH * 512,
            4 * h * 8 * EH * 512 + (2 * b + 1) * EH * 512,
            0, 0]], dtype=np.int32)
        maps.append({
            "xrows": x[c * TPC:(c + 1) * TPC],
            "win_my": win,
            "bin_my": binv,
            "bdt_my": b_dt[sl].reshape(NBLK, 128),
            "wdbc_my": wdbc,
            "rbc_bias": rbcb,
            "wdt_my": np.ascontiguousarray(W_dt[:, sl]),
            "alog_my": A_log[sl],
            "d_my": Dv[sl].reshape(NBLK, 128),
            "wout_i": W_out,
            "bout_i": b_out.reshape(NBLK, 128),
            "lng_i": ln_g.reshape(NBLK, 128),
            "lnb_i": ln_b.reshape(NBLK, 128),
            "gwT": np.ascontiguousarray(gate_w.T),
            "gb_i": gate_b.reshape(NE, 1),
            "w1e": w1[c],
            "w2e": w2[c],
            "b1e": b1[c].reshape(32, 128),
            "b2e": b2[c].reshape(NBLK, 128),
            "prefmask": (np.arange(NE) < c).astype(np.float32).reshape(NE, 1),
            "myexp_i": np.full((128, 1), c, np.float32),
            "bases_i": bases,
        })
    return maps


_NC_CACHE = {}


def kernel(**inputs):
    if "nc" not in _NC_CACHE:
        _NC_CACHE["nc"] = build_nc()
    nc = _NC_CACHE["nc"]
    maps = _in_maps(inputs)
    res = run_bass_kernel_spmd(nc, maps, core_ids=list(range(NCORE)))
    out = np.concatenate([r["out_rows"] for r in res.results], axis=0)
    return out.reshape(4, 4096, 1024)


# version 22
# speedup vs baseline: 1.0289x; 1.0194x over previous
import numpy as np
import contextlib
import concourse.bass as bass
import concourse.mybir as mybir
from concourse.bass_utils import run_bass_kernel_spmd

F32 = mybir.dt.float32
BF16 = mybir.dt.bfloat16
U16 = mybir.dt.uint16
U32 = mybir.dt.uint32
I32 = mybir.dt.int32
AL = mybir.AluOpType
AF = mybir.ActivationFunctionType
AX = mybir.AxisListType.X

NCORE = 8
TPC = 2048
TB = 4096
D = 1024
E = 2048
EH = 1024
NBLK = 8
N = 16
R = 64
FF = 4096
CAP = 2560
NE = 8
QT = 512
NQ = TB // QT
NT = TPC // 128
EPS = 1e-5


class Prog:

    ENGINES = ("tensor", "vector", "scalar", "gpsimd", "sync")
    QPOOL = 10

    def __init__(self):
        self.stream = {e: [] for e in self.ENGINES}
        self.cnt = {}
        self.events = {}
        self.readers = {}
        self.seen = {e: {} for e in self.ENGINES}
        self.groups = {}
        self.rr = {}
        self.semcum = {}
        self.dma_sems = set()

    def _qsem(self, q, i):
        name = f"{q}_q{i}"
        self.dma_sems.add(name)
        return name

    def _open(self, q):
        g = self.groups.get(q)
        if g is not None and not g["closed"]:
            return g
        i = self.rr.get(q, 0)
        self.rr[q] = i + 1
        sem = self._qsem(q, i % self.QPOOL)
        base = self.semcum.get(sem, 0)
        if base > 0:
            self.gseq = getattr(self, "gseq", 0) + 1
            self.stream[q].append((None, [(sem, base)], None, self.gseq))
            self.seen[q][sem] = max(self.seen[q].get(sem, 0), base)
        g = {"sem": sem, "base": base, "n": 0, "closed": False}
        self.groups[q] = g
        return g

    def _close(self, g):
        if not g["closed"]:
            g["closed"] = True
            g["total"] = g["base"] + 16 * g["n"]
            self.semcum[g["sem"]] = g["total"]

    def emit(self, eng, fn, reads=(), writes=(), dma=False):
        import sys as _sys
        f = _sys._getframe(1)
        tag = f"{f.f_lineno}"
        if f.f_code.co_name in ("LD", "ST", "MM", "MMC", "SCOPY", "ACT", "COPY", "TT",
                                "TS", "STT", "RED", "MEMSET", "CC", "FENCE", "TRANS", "VOP"):
            f2 = f.f_back
            tag = f"{f.f_code.co_name}@{f2.f_lineno}"
        self.tags = getattr(self, "tags", [])
        waits = {}

        def need(ev):
            if ev is None:
                return
            if isinstance(ev, dict):
                self._close(ev)
                s, v = ev["sem"], ev["total"]
            else:
                s, v = ev
            if self.seen[eng].get(s, 0) >= v:
                return
            if waits.get(s, 0) < v:
                waits[s] = v

        for k in reads:
            need(self.events.get(k))
        for k in writes:
            need(self.events.get(k))
            for ev in self.readers.get(k, ()):
                need(ev)
        if dma:
            g = self._open(eng)
            g["n"] += 1
            ev = g
            inc = (g["sem"], 16)
        else:
            sem = eng
            self.cnt[sem] = self.cnt.get(sem, 0) + 1
            ev = (sem, self.cnt[sem])
            inc = (sem, 1)
        for k in reads:
            self.readers.setdefault(k, []).append(ev)
        for k in writes:
            self.events[k] = ev
            self.readers[k] = []
        for s, v in waits.items():
            self.seen[eng][s] = max(self.seen[eng].get(s, 0), v)
        self.gseq = getattr(self, "gseq", 0) + 1
        self.tags.append((self.gseq, eng, tag))
        self.stream[eng].append((fn, sorted(waits.items()), inc, self.gseq))

    def finalize_outputs(self):
        for q in ("gpsimd", "sync", "scalar"):
            g = self.groups.get(q)
            if g is not None:
                self._close(g)
            for i in range(self.QPOOL):
                sem = f"{q}_q{i}"
                tot = self.semcum.get(sem, 0)
                if tot > 0 and self.seen[q].get(sem, 0) < tot:
                    self.gseq = getattr(self, "gseq", 0) + 1
                    self.stream[q].append((None, [(sem, tot)], None, self.gseq))
                    self.seen[q][sem] = tot

    limit = None

    def replay(self, eng_name, e, sems):
        for fn, waits, inc, seq in self.stream[eng_name]:
            if self.limit is not None and seq > self.limit:
                continue
            for s, v in waits:
                e.wait_ge(sems[s], v)
            if fn is None:
                continue
            instr = fn(e)
            if inc is not None and instr is not None:
                instr.then_inc(sems[inc[0]], inc[1])


def build_nc(limit=None):
    nc = bass.Bass("TRN2", target_bir_lowering=False)

    def ein(name, shape, dt=F32):
        return nc.dram_tensor(name, shape, dt, kind="ExternalInput")

    xrows = ein("xrows", [TPC, D])
    win_my = ein("win_my", [D, 2048])
    bin_my = ein("bin_my", [16, 128])
    bdt_my = ein("bdt_my", [NBLK, 128])
    wdbc_my = ein("wdbc_my", [EH, 96])
    rbc_bias = ein("rbc_bias", [96, 1])
    wdt_my = ein("wdt_my", [R, EH])
    alog_my = ein("alog_my", [EH, N])
    d_my = ein("d_my", [NBLK, 128])
    wout_i = ein("wout_i", [E, D])
    bout_i = ein("bout_i", [NBLK, 128])
    lng_i = ein("lng_i", [NBLK, 128])
    lnb_i = ein("lnb_i", [NBLK, 128])
    gwT = ein("gwT", [D, NE])
    gb_i = ein("gb_i", [NE, 1])
    w1e = ein("w1e", [D, FF])
    w2e = ein("w2e", [FF, D])
    b1e = ein("b1e", [32, 128])
    b2e = ein("b2e", [NBLK, 128])
    prefmask = ein("prefmask", [NE, 1])
    myexp_i = ein("myexp_i", [128, 1])
    bases_i = ein("bases_i", [1, 8], I32)
    out_rows = nc.dram_tensor("out_rows", [TPC, D], F32, kind="ExternalOutput")

    dr = lambda name, shape, dt: nc.dram_tensor(name, shape, dt)
    ag1z = dr("ag1z", [D, TPC], BF16)
    ag1zout = dr("ag1zout", [NCORE * D, TPC], BF16)
    ag1x = dr("ag1x", [D, TPC], BF16)
    ag1xout = dr("ag1xout", [NCORE * D, TPC], BF16)
    cntin = dr("cntin", [1, NE], F32)
    cntout = dr("cntout", [NCORE, NE], F32)
    metain = dr("metain", [4, TPC], F32)
    metaout = dr("metaout", [NCORE * 4, TPC], F32)
    ag3in = dr("ag3in", [NQ * 96, QT], F32)
    ag3out = dr("ag3out", [NQ * NCORE * 96, QT], F32)
    ag4in = dr("ag4in", [D, CAP], BF16)
    ag4out = dr("ag4out", [NCORE * D, CAP], BF16)
    ag5in = dr("ag5in", [NQ * EH, QT], BF16)
    ag5out = dr("ag5out", [NQ * NCORE * EH, QT], BF16)
    u_sp = dr("u_sp", [EH, TB], BF16)
    gate_sp = dr("gate_sp", [EH, TB], BF16)
    lns_sp = dr("lns_sp", [EH, TB], BF16)
    winsp = dr("winsp", [E, D], BF16)
    w1bf = dr("w1bf", [FF, D], BF16)
    w2bf = dr("w2bf", [D, FF], BF16)
    woutbf = dr("woutbf", [D, E], BF16)
    xs_sp = dr("xs_sp", [D, CAP], BF16)
    myx2 = dr("myx2", [2 * EH, TPC], BF16)
    lv_hbm = dr("lv_hbm", [1, CAP], BF16)
    mrow_hbm = dr("mrow_hbm", [3, TPC], F32)

    P = Prog()

    import contextlib
    with contextlib.ExitStack() as _stack:
        _E = _stack.enter_context
        _stack.enter_context(nc.allow_non_contiguous_dma(
            reason="small transposed parameter loads / column stores"))
        onesf = _E(nc.sbuf_tensor("onesf", [128, 8], F32))
        ident_f = _E(nc.sbuf_tensor("ident_f", [128, 128], F32))
        tri_f = _E(nc.sbuf_tensor("tri_f", [128, 128], F32))
        ones1b = _E(nc.sbuf_tensor("ones1b", [1, 128], BF16))
        nones1b = _E(nc.sbuf_tensor("nones1b", [1, 128], BF16))
        ones1f = _E(nc.sbuf_tensor("ones1f", [1, 128], F32))
        iota8 = _E(nc.sbuf_tensor("iota8", [128, 8], F32))
        consts = _E(nc.sbuf_tensor("consts", [128, 64], F32))
        b1_s = _E(nc.sbuf_tensor("b1_s", [128, 32], F32))
        b2_s = _E(nc.sbuf_tensor("b2_s", [128, 8], F32))
        gwT_f = _E(nc.sbuf_tensor("gwT_f", [128, 64], F32))
        gb_s = _E(nc.sbuf_tensor("gb_s", [8, 1], F32))
        rbcb_s = _E(nc.sbuf_tensor("rbcb_s", [96, 1], F32))
        prefm_s = _E(nc.sbuf_tensor("prefm_s", [8, 1], F32))
        myexp_s = _E(nc.sbuf_tensor("myexp_s", [128, 1], F32))
        iotar = _E(nc.sbuf_tensor("iotar", [128, 128], F32))
        onescb = _E(nc.sbuf_tensor("onescb", [128, 2], BF16))
        bases_s = _E(nc.sbuf_tensor("bases_s", [1, 8], I32))
        nega = _E(nc.sbuf_tensor("nega", [128, NBLK * N], F32))
        fencebuf = _E(nc.sbuf_tensor("fencebuf", [1, 4], F32))
        cvals = _E(nc.sbuf_tensor("cvals", [128, 8], F32))
        ARBIG = _E(nc.sbuf_tensor("ARBIG", [128, 10240], F32))
        ARB2 = _E(nc.sbuf_tensor("ARB2", [128, 16384], BF16))
        AR2 = _E(nc.sbuf_tensor("AR2", [128, 4096], F32))
        ARX = _E(nc.sbuf_tensor("ARX", [128, 4096], BF16))
        ARW = _E(nc.sbuf_tensor("ARW", [128, 8192], BF16))
        ARE = _E(nc.sbuf_tensor("ARE", [128, 2048], F32))
        bmcm_b = _E(nc.sbuf_tensor("bmcm_b", [32, TB], BF16))
        castA = _E(nc.sbuf_tensor("castA", [128, 1024], F32))
        castB = _E(nc.sbuf_tensor("castB", [128, 1024], F32))
        moepk = _E(nc.sbuf_tensor("moepk", [16, 4096], F32))
        tokid = _E(nc.sbuf_tensor("tokid", [16, 1024], F32))
        comp16 = _E(nc.sbuf_tensor("comp16", [16, CAP // 16], U16))
        wsc = _E(nc.sbuf_tensor("wsc", [64, 2048], BF16))
        cidx = _E(nc.sbuf_tensor("cidx", [128, CAP // 16], U16))
        gidx16 = _E(nc.sbuf_tensor("gidx16", [128, 384], U16))
        selmat = _E(nc.sbuf_tensor("selmat", [32, 4096], BF16))
        w1t = _E(nc.sbuf_tensor("w1t", [128, 2048], BF16))
        wot = _E(nc.sbuf_tensor("wot", [128, 2048], BF16))
        lns_t = _E(nc.sbuf_tensor("lns_t", [128, 2 * QT], BF16))
        u_t = _E(nc.sbuf_tensor("u_t", [128, 2 * QT], BF16))
        gate_t = _E(nc.sbuf_tensor("gate_t", [128, 2 * QT], BF16))
        ndu = _E(nc.sbuf_tensor("ndu", [128, QT], BF16))
        b_t = _E(nc.sbuf_tensor("b_t", [128, QT], BF16))
        ygt = _E(nc.sbuf_tensor("ygt", [128, 2 * QT], BF16))
        rstat = _E(nc.sbuf_tensor("rstat", [128, 128], F32))
        carryc = _E(nc.sbuf_tensor("carryc", [128, NBLK * N], F32))
        lg_tm = _E(nc.sbuf_tensor("lg_tm", [128, NT * 8], F32))
        oh_tm = _E(nc.sbuf_tensor("oh_tm", [128, NT * 8], F32))
        pos_tm = _E(nc.sbuf_tensor("pos_tm", [128, NT * 8], F32))
        rcols = _E(nc.sbuf_tensor("rcols", [128, NT * 10], F32))
        base8 = _E(nc.sbuf_tensor("base8", [128, 8], F32))
        cnt_s = _E(nc.sbuf_tensor("cnt_s", [8, 8], F32))
        rtmp8 = _E(nc.sbuf_tensor("rtmp8", [128, 8], F32))
        stile = _E(nc.sbuf_tensor("stile", [128, 512], F32))
        resid = _E(nc.sbuf_tensor("resid", [128, 512], F32))
        sgt = _E(nc.sbuf_tensor("sgt", [128, 512], F32))
        x2stage = _E(nc.sbuf_tensor("x2stage", [128, 1024], BF16))
        xnstage = _E(nc.sbuf_tensor("xnstage", [128, 1024], BF16))
        psA = _E(nc.psum_tensor("psA", [128, 512], F32))
        psB = _E(nc.psum_tensor("psB", [128, 512], F32))
        psC = _E(nc.psum_tensor("psC", [128, 512], F32))
        ps96 = _E(nc.psum_tensor("ps96", [96, 512], F32))
        psT = _E(nc.psum_tensor("psT", [128, 128], F32))
        psT2 = _E(nc.psum_tensor("psT2", [128, 128], F32))
        psY = _E(nc.psum_tensor("psY", [128, 512], F32))
        ps8 = _E(nc.psum_tensor("ps8", [128, 8], F32))
        s_tensor = _E(nc.semaphore("s_tensor"))
        s_vector = _E(nc.semaphore("s_vector"))
        s_scalar = _E(nc.semaphore("s_scalar"))
        s_gpsimd = _E(nc.semaphore("s_gpsimd"))
        s_sync = _E(nc.semaphore("s_sync"))
        qsems = {}
        for q in ("sync", "gpsimd", "scalar"):
            for i in range(Prog.QPOOL):
                qsems[f"{q}_q{i}"] = _E(nc.semaphore(f"{q}_q{i}"))
        block = _E(nc.Block())
        sems = {
            "tensor": s_tensor, "vector": s_vector, "scalar": s_scalar,
            "gpsimd": s_gpsimd, "sync": s_sync,
        }
        sems.update(qsems)

        xtA = ARBIG[:, 0:1024]
        xtB = ARBIG[:, 1024:2048]
        sqA = ARBIG[:, 2048:3072]
        sqB = ARBIG[:, 5120:6144]
        lg_fm = ARBIG[0:8, 3072:5120]
        rbct_v = ARBIG[0:96, 0:4096]
        slab_v = ARBIG[:, 0:10240].bitcast(BF16)
        bbc_v = ARBIG[:, 0:4096].bitcast(BF16)
        cbc_v = ARBIG[:, 4096:8192].bitcast(BF16)
        wpar_v = ARBIG.bitcast(BF16)[:, 16384:18432]
        diagD_v = ARBIG.bitcast(BF16)[:, 18432:19456]
        identb_v = ARBIG.bitcast(BF16)[:, 19456:19584]
        slabx_v = ARB2
        h1_v = ARB2
        gfb_v = ARB2
        a_v = AR2[:, 0:1024]
        art = [AR2[:, 1024:1536], AR2[:, 1536:2048],
               AR2[:, 3584:4096]]
        yacc_v = AR2[:, 3072:3584]
        h_v = AR2.bitcast(BF16)[:, 4096:5120]
        ymul_v = AR2.bitcast(BF16)[:, 5120:6144]
        gvm_bc = AR2.bitcast(BF16)
        ysl_v = AR2[:, 0:4096].bitcast(BF16)
        xsf_v = ARX[:, 0:CAP]
        xssl_v = ARX
        gacc_v = ARX[:, 0:2048]
        gtmp_v = ARX[:, 2048:4096]
        xrow_v = ARX.bitcast(F32)[:, 0:1024]
        x2sl_v = ARW[:, 0:4096]
        w2t_v = ARW[:, 4096:8192]
        ottm_v = ARW.bitcast(F32)
        uev_v = ARE.bitcast(BF16)
        rbcev_v = castB[0:96, 0:1024]
        ffev_v = sgt.bitcast(BF16)[:, 0:512]
        rbf2_v = wsc[0:64, 1024:1536]
        gvrow = moepk.bitcast(BF16)[0:1, 4096:6144]
        dsgb = [stile.bitcast(BF16)[:, 0:512],
                resid.bitcast(BF16)[:, 0:512]]
        midx_v = moepk[:, 0:1024]
        mslot_v = moepk[:, 1024:2048]
        mkeep_v = moepk[:, 2048:3072]
        mwork_v = moepk[:, 3072:4096]

        def LD(dst, src, key, reads=(), eng="sync"):
            P.emit(eng, lambda e, d=dst, s=src: e.dma_start(out=d, in_=s),
                   reads=tuple(reads), writes=(key,), dma=True)

        def ST(dst, src, key, reads, eng="gpsimd"):
            P.emit(eng, lambda e, d=dst, s=src: e.dma_start(out=d, in_=s),
                   reads=tuple(reads), writes=(key,), dma=True)

        def MM(outp, lhsT, rhs, reads, writes, start, stop):
            P.emit("tensor",
                   lambda e, o=outp, l=lhsT, r=rhs, s1=start, s2=stop:
                   e.matmul(o, l, r, start=s1, stop=s2),
                   reads=tuple(reads), writes=tuple(writes))

        def MMC(outp, pairs, reads, writes, first=True, last=True):
            def fn(e, o=outp, ps=tuple(pairs), first=first, last=last):
                n = len(ps)
                for i, (l, r) in enumerate(ps):
                    mm = e.matmul(o, l, r, start=(first and i == 0),
                                  stop=(last and i == n - 1))
                return mm
            P.emit("tensor", fn, reads=tuple(reads), writes=tuple(writes))

        def SCOPY(outp, inp, reads, writes):
            P.emit("scalar", lambda e, o=outp, i=inp: e.copy(out=o, in_=i),
                   reads=tuple(reads), writes=tuple(writes))

        def TRANS(outp, inp, reads, writes):
            P.emit("tensor",
                   lambda e, o=outp, i=inp: e.transpose(o, i, ident_f[:, :]),
                   reads=tuple(reads) + ("ident_f",), writes=tuple(writes))

        def ACT(outp, inp, func, reads, writes, bias=0.0, scale=1.0):
            P.emit("scalar",
                   lambda e, o=outp, i=inp, f=func, b=bias, s=scale:
                   e.activation(out=o, in_=i, func=f, bias=b, scale=s),
                   reads=tuple(reads), writes=tuple(writes))

        def COPY(outp, inp, reads, writes, eng="vector"):
            P.emit(eng, lambda e, o=outp, i=inp: e.tensor_copy(out=o, in_=i),
                   reads=tuple(reads), writes=tuple(writes))

        def TT(outp, in0, in1, op, reads, writes, eng="vector"):
            P.emit(eng, lambda e, o=outp, a=in0, b=in1, p=op:
                   e.tensor_tensor(out=o, in0=a, in1=b, op=p),
                   reads=tuple(reads), writes=tuple(writes))

        def TS(outp, in0, s1, s2, op0, op1, reads, writes, eng="vector"):
            if s2 is None:
                op1 = AL.bypass
            P.emit(eng, lambda e, o=outp, a=in0, x=s1, y=s2, p=op0, q=op1:
                   e.tensor_scalar(out=o, in0=a, scalar1=x, scalar2=y, op0=p, op1=q),
                   reads=tuple(reads), writes=tuple(writes))

        def STT(outp, in0, scal, in1, op0, op1, reads, writes, eng="vector"):
            P.emit(eng, lambda e, o=outp, a=in0, s=scal, b=in1, p=op0, q=op1:
                   e.scalar_tensor_tensor(out=o, in0=a, scalar=s, in1=b, op0=p, op1=q),
                   reads=tuple(reads), writes=tuple(writes))

        def RED(outp, inp, op, reads, writes, negate=False):
            P.emit("vector", lambda e, o=outp, i=inp, p=op, n=negate:
                   e.tensor_reduce(o, i, axis=AX, op=p, negate=n),
                   reads=tuple(reads), writes=tuple(writes))

        def MEMSET(ap, val, writes, eng="vector"):
            P.emit(eng, lambda e, a=ap, v=val: e.memset(a, v), reads=(), writes=tuple(writes))

        def CC(kind, ins_ap, outs_ap, reads, writes):
            P.emit("gpsimd", lambda e, i=ins_ap, u=outs_ap, k=kind:
                   e.collective_compute(k, AL.bypass, replica_groups=[list(range(NCORE))],
                                        ins=[i], outs=[u]),
                   reads=tuple(reads), writes=tuple(writes))
            v = P.cnt["gpsimd"]
            P.gseq = getattr(P, "gseq", 0) + 1
            P.stream["gpsimd"].append((None, [("gpsimd", v)], None, P.gseq))
            P.seen["gpsimd"]["gpsimd"] = max(P.seen["gpsimd"].get("gpsimd", 0), v)

        def FENCE(new_keys, old_keys):
            if isinstance(new_keys, str):
                new_keys = (new_keys,)
            P.emit("vector", lambda e: e.memset(fencebuf[:, :], 0.0),
                   reads=(), writes=tuple(new_keys) + tuple(old_keys) + ("fencebuf",))

        _reg_cache = {}

        def dyn_ld(dst, handle, base_idx, const_off, apdims, key, reads=()):
            def fn(e, d=dst, h=handle, bi=base_idx, co=const_off, ap=apdims):
                rk = id(e)
                if rk not in _reg_cache:
                    _reg_cache[rk] = e.alloc_register("dynoff")
                r = _reg_cache[rk]
                e.reg_load(r, bases_s[0:1, bi:bi + 1])
                e.reg_add(r, r, co)
                try:
                    return e.dma_start(out=d, in_=bass.AP(h, r, ap))
                except Exception as ex:
                    print("DYN FAIL bi=", bi, "co=", co, ap, d.offset, ex)
                    raise
            P.emit("sync", fn, reads=tuple(reads) + ("bases_s",), writes=(key,), dma=True)

        MEMSET(onesf[:, :], 1.0, ("onesf",))
        MEMSET(cvals[:, 0:1], EPS, ("cvals",))
        MEMSET(cvals[:, 1:2], -0.5, ("cvals",))
        MEMSET(cvals[:, 2:3], -1.0, ("cvals",))
        MEMSET(cvals[:, 3:4], 1.0 / D, ("cvals",))
        MEMSET(cvals[:, 4:5], float(CAP), ("cvals",))
        MEMSET(ones1b[:, :], 1.0, ("ones1b",))
        MEMSET(nones1b[:, :], -1.0, ("nones1b",))
        MEMSET(ones1f[:, :], 1.0, ("ones1f",))
        MEMSET(carryc[:, :], 0.0, ("carryc",))
        P.emit("gpsimd", lambda e: e.affine_select(
            ident_f[:, :], onesf[:, 0:1].broadcast_to((128, 128)), pattern=[[1, 128]],
            base=0, channel_multiplier=-1, compare_op=AL.is_equal, fill=0.0),
            reads=("onesf",), writes=("ident_f",))
        P.emit("gpsimd", lambda e: e.affine_select(
            tri_f[:, :], onesf[:, 0:1].broadcast_to((128, 128)), pattern=[[1, 128]],
            base=0, channel_multiplier=-1, compare_op=AL.is_ge, fill=0.0),
            reads=("onesf",), writes=("tri_f",))
        P.emit("gpsimd", lambda e: e.iota(
            iota8[:, :], pattern=[[1, 8]], base=0, channel_multiplier=0,
            allow_small_or_imprecise_dtypes=True), reads=(), writes=("iota8",))
        P.emit("gpsimd", lambda e: e.iota(
            tokid[:, :], pattern=[[16, 1024]], base=0, channel_multiplier=1,
            allow_small_or_imprecise_dtypes=True), reads=(), writes=("tokid",))
        MEMSET(ARB2[0:32, 0:4096], 1.0, ("h1",))
        MEMSET(onescb[:, :], 1.0, ("onescb",))
        P.emit("gpsimd", lambda e: e.iota(
            iotar[:, :], pattern=[[1, 128]], base=0, channel_multiplier=0,
            allow_small_or_imprecise_dtypes=True), reads=(), writes=("iotar",))
        P.emit("gpsimd", lambda e: e.affine_select(
            selmat[:, :], ARB2[0:32, 0:4096], pattern=[[1, 32], [0, 128]], base=0,
            channel_multiplier=-1, compare_op=AL.is_equal, fill=0.0),
            reads=("h1",), writes=("selmat",))

        LD(consts[:, 0:16], bin_my.ap().transpose([1, 0]), "consts", eng="gpsimd")
        LD(consts[:, 56:64], bdt_my.ap().transpose([1, 0]), "consts", eng="gpsimd")
        LD(consts[:, 24:32], bout_i.ap().transpose([1, 0]), "consts", eng="gpsimd")
        LD(consts[:, 32:40], d_my.ap().transpose([1, 0]), "consts", eng="gpsimd")
        LD(consts[:, 40:48], lng_i.ap().transpose([1, 0]), "consts", eng="gpsimd")
        LD(consts[:, 48:56], lnb_i.ap().transpose([1, 0]), "consts", eng="gpsimd")
        TS(consts[:, 16:24], consts[:, 56:64], -1.0, None, AL.mult, AL.add,
           ("consts",), ("consts",))
        LD(b1_s[:, :], b1e.ap().transpose([1, 0]), "b1_s", eng="gpsimd")
        LD(b2_s[:, :], b2e.ap().transpose([1, 0]), "b2_s", eng="gpsimd")
        LD(gb_s[:, :], gb_i.ap(), "gb_s")
        LD(rbcb_s[:, :], rbc_bias.ap(), "rbcb_s")
        LD(prefm_s[:, :], prefmask.ap(), "prefm_s")
        LD(myexp_s[:, :], myexp_i.ap(), "myexp_s")
        LD(bases_s[:, :], bases_i.ap(), "bases_s")
        for k in range(8):
            LD(gwT_f[:, k * 8:(k + 1) * 8], gwT[k * 128:(k + 1) * 128, :], "gwT_f")
        for i in range(NBLK):
            LD(castA[:, i * N:(i + 1) * N], alog_my[i * 128:(i + 1) * 128, :], "castA")
        ACT(nega[:, :], castA[:, 0:NBLK * N], AF.Exp, ("castA",), ("nega",))

        for it in range(NT):
            xt = xtA if it % 2 == 0 else xtB
            kx = "xtA" if it % 2 == 0 else "xtB"
            sq = sqA if it % 2 == 0 else sqB
            ks = f"sq{it % 2}"
            st = lambda j: rstat[:, it * 8 + j: it * 8 + j + 1]
            kt = f"st{it}"
            LD(xt, xrows[it * 128:(it + 1) * 128, :], kx)
            for ln2 in range(2):
                RED(st(0), xt, AL.add, (kx,), (kt,))
                TT(sq, xt, xt, AL.mult, (kx, ks), (ks,), eng="gpsimd")
                RED(st(1), sq, AL.add, (ks,), (kt,))
                TS(st(2), st(0), 1.0 / D, None, AL.mult, AL.add, (kt,), (kt,))
                TT(st(3), st(2), st(2), AL.mult, (kt,), (kt,))
                STT(st(4), st(1), cvals[:, 3:4], st(3), AL.mult,
                    AL.subtract, (kt, "cvals"), (kt,))
                ACT(st(5), st(4), AF.Ln, (kt, "cvals"), (kt,), bias=cvals[:, 0:1])
                ACT(st(6), st(5), AF.Exp, (kt, "cvals"), (kt,), scale=cvals[:, 1:2])
                if ln2 == 0:
                    TS(xt, xt, st(2), st(6), AL.subtract, AL.mult, (kx, kt), (kx,))
                else:
                    TS(sq, xt, st(2), st(6), AL.subtract, AL.mult, (kx, kt), (ks,))
            for fb in range(8):
                cbs = castB[:, fb * 128:(fb + 1) * 128]
                kcb, kxn, kx2 = f"cB{fb}", f"xns{fb}", f"x2s{fb}"
                pcr = psC[:, (fb % 4) * 128:(fb % 4 + 1) * 128]
                kpc = f"psC{fb % 4}"
                pt2 = psT if fb % 2 == 0 else psT2
                kpt = "psT" if fb % 2 == 0 else "psT2"
                P.emit("tensor", lambda e, pcr=pcr, xt=xt, fb=fb: e.transpose(
                    pcr, xt[:, fb * 128:(fb + 1) * 128], ident_f[:, :]),
                    reads=(kx, "ident_f", kpc), writes=(kpc,))
                COPY(cbs, pcr, (kpc, kcb), (kcb,))
                SCOPY(xnstage[:, fb * 128:(fb + 1) * 128], cbs, (kcb, kxn), (kxn,))
                MM(psA[0:8, 0:128], gwT_f[:, fb * 8:(fb + 1) * 8], cbs,
                   ("gwT_f", kcb), ("psA",), start=(fb == 0), stop=(fb == 7))
                P.emit("tensor", lambda e, pt2=pt2, sq=sq, fb=fb: e.transpose(
                    pt2[:, :], sq[:, fb * 128:(fb + 1) * 128], ident_f[:, :]),
                    reads=(ks, "ident_f", kpt), writes=(kpt,))
                P.emit("vector", lambda e, fb=fb, pt2=pt2: e.tensor_scalar(
                    out=x2stage[:, fb * 128:(fb + 1) * 128], in0=pt2[:, :],
                    scalar1=consts[:, 40 + fb:41 + fb], scalar2=consts[:, 48 + fb:49 + fb],
                    op0=AL.mult, op1=AL.add),
                    reads=(kpt, "consts", kx2), writes=(kx2,))
            ST(bass.AP(ag1x, it * 128, [[TPC, 128], [128 * TPC, 8], [1, 128]]),
               xnstage[:, :], "ag1x", tuple(f"xns{fb}" for fb in range(8)))
            ST(bass.AP(ag1z, it * 128, [[TPC, 128], [128 * TPC, 8], [1, 128]]),
               x2stage[:, :], "ag1z", tuple(f"x2s{fb}" for fb in range(8)))
            TS(lg_fm[:, it * 128:(it + 1) * 128], psA[0:8, 0:128], gb_s[:, 0:1], None,
               AL.add, AL.add, ("psA", "gb_s"), ("lg_fm",))

        for it in range(NT):
            Ccol = lambda j, it=it: rcols[:, it * 10 + j: it * 10 + j + 1]
            ohs = oh_tm[:, it * 8:(it + 1) * 8]
            lgs = lg_tm[:, it * 8:(it + 1) * 8]
            P.emit("tensor", lambda e, it=it: e.transpose(
                psT[:, 0:8], lg_fm[:, it * 128:(it + 1) * 128], ident_f[0:8, 0:8]),
                reads=("lg_fm", "ident_f"), writes=("psT",))
            COPY(lgs, psT[:, 0:8], ("psT",), ("lg_tm",))
            RED(Ccol(0), lgs, AL.max, ("lg_tm",), (f"r{it}",), negate=True)
            ACT(ohs, lgs, AF.Exp, ("lg_tm", f"r{it}"), ("oh_tm",), bias=Ccol(0))
            RED(Ccol(1), ohs, AL.add, ("oh_tm",), (f"r{it}",))
            P.emit("vector", lambda e, it=it: e.reciprocal(
                rcols[:, it * 10 + 2: it * 10 + 3], rcols[:, it * 10 + 1: it * 10 + 2]),
                reads=(f"r{it}",), writes=(f"r{it}",))
            TS(Ccol(3), Ccol(0), -1.0, None, AL.mult, AL.add, (f"r{it}",), (f"r{it}",))
            TS(ohs, lgs, Ccol(3), None, AL.is_equal, AL.add,
               ("lg_tm", f"r{it}"), ("oh_tm",))
            TT(pos_tm[:, it * 8:(it + 1) * 8], ohs, iota8[:, :], AL.mult,
               ("oh_tm", "iota8"), ("pos_tm",))
            RED(Ccol(4), pos_tm[:, it * 8:(it + 1) * 8], AL.add, ("pos_tm",), (f"r{it}",))
            MM(psT2[0:8, it:it + 1], ohs, onesf[:, 0:1], ("oh_tm", "onesf"), ("psT2",),
               start=True, stop=True)
        cnts_c = moepk[0:8, 3968:3984]
        cnts16 = moepk[0:16, 3840:3848]
        mb16 = moepk[0:16, 3848:3856]
        mrow16 = moepk[0:1, 3856:3984]
        COPY(cnts_c, psT2[0:8, 0:16], ("psT2", "cnts16"), ("cnts16",))
        P.emit("tensor", lambda e: e.transpose(
            psT[0:16, 0:8], cnts_c, ident_f[0:8, 0:8]),
            reads=("cnts16", "ident_f"), writes=("psT",))
        COPY(cnts16, psT[0:16, 0:8], ("psT", "cnts16"), ("cnts16",))
        MM(ps8[0:1, 0:8], onesf[0:16, 0:1], cnts16, ("onesf", "cnts16"), ("ps8",),
           start=True, stop=True)
        COPY(rtmp8[0:1, 0:8], ps8[0:1, 0:8], ("ps8",), ("cntcol",))
        ST(cntin.ap(), rtmp8[0:1, 0:8], "cntin", ("cntcol",))

        CC("AllGather", cntin.ap().opt(), cntout.ap().opt(), ("cntin",), ("cntout",))
        LD(cnt_s[:, :], cntout.ap(), "cnt_s", reads=("cntout",))
        MM(ps8[0:1, 0:8], prefm_s[:, :], cnt_s[:, :], ("prefm_s", "cnt_s"), ("ps8",),
           start=True, stop=True)
        COPY(rtmp8[0:1, 0:8], ps8[0:1, 0:8], ("ps8",), ("baserow",))
        MM(ps8[0:128, 0:8], ones1f[0:1, 0:128], rtmp8[0:1, 0:8], ("ones1f", "baserow"),
           ("ps8",), start=True, stop=True)
        COPY(base8[:, :], ps8[0:128, 0:8], ("ps8",), ("base8",))
        MM(psT2[0:16, 0:8], tri_f[0:16, 0:16], cnts16, ("tri_f", "cnts16"), ("psT2",),
           start=True, stop=True)
        TT(mb16, psT2[0:16, 0:8], cnts16, AL.subtract, ("psT2", "cnts16", "mb16"),
           ("mb16",))
        TT(mb16, mb16, base8[0:16, 0:8], AL.add, ("mb16", "base8"), ("mb16",))
        for it in range(NT):
            LD(mrow16[0:1, it * 8:(it + 1) * 8], mb16[it:it + 1, 0:8], "mrow16",
               reads=("mb16", "mrow16"), eng="sync")

        for it in range(NT):
            Ccol = lambda j, it=it: rcols[:, it * 10 + j: it * 10 + j + 1]
            ohs = oh_tm[:, it * 8:(it + 1) * 8]
            MM(psT2[:, 0:8], tri_f[:, :], ohs, ("tri_f", "oh_tm"), ("psT2",),
               start=True, stop=True)
            COPY(pos_tm[:, it * 8:(it + 1) * 8], psT2[:, 0:8], ("psT2",), ("pos_tm",))
            MM(ps8[0:128, 0:8], ones1f[0:1, 0:128], mrow16[0:1, it * 8:(it + 1) * 8],
               ("ones1f", "mrow16"), ("ps8",), start=True, stop=True)
            TT(rtmp8[:, :], pos_tm[:, it * 8:(it + 1) * 8], ps8[0:128, 0:8], AL.add,
               ("pos_tm", "ps8"), ("rtmp8",))
            TS(rtmp8[:, :], rtmp8[:, :], 1.0, None, AL.subtract, AL.add,
               ("rtmp8",), ("rtmp8",))
            TT(rtmp8[:, :], rtmp8[:, :], ohs, AL.mult, ("rtmp8", "oh_tm"), ("rtmp8",))
            RED(Ccol(9), rtmp8[:, :], AL.add, ("rtmp8",), (f"r{it}",))
            TS(Ccol(6), Ccol(9), float(CAP), None, AL.is_lt, AL.add, (f"r{it}",), (f"r{it}",))
            TT(Ccol(5), Ccol(9), Ccol(6), AL.mult, (f"r{it}",), (f"r{it}",))
            TT(Ccol(7), Ccol(2), Ccol(6), AL.mult, (f"r{it}",), (f"r{it}",))
            P.emit("tensor", lambda e, it=it: e.transpose(
                psT[0:4, 0:128], rcols[:, it * 10 + 4: it * 10 + 8], ident_f[:, :]),
                reads=(f"r{it}", "ident_f"), writes=("psT",))
            mst = moepk[0:4, 3584 + (it % 2) * 128: 3712 + (it % 2) * 128]
            SCOPY(mst, psT[0:4, 0:128], ("psT", f"mst{it % 2}"), (f"mst{it % 2}",))
            ST(metain.ap()[0:4, it * 128:(it + 1) * 128], mst, "metain", (f"mst{it % 2}",))
        CC("AllGather", ag1z.ap().opt(), ag1zout.ap().opt(), ("ag1z",), ("ag1zout",))
        CC("AllGather", metain.ap().opt(), metaout.ap().opt(), ("metain",), ("metaout",))
        CC("AllGather", ag1x.ap().opt(), ag1xout.ap().opt(), ("ag1x",), ("ag1xout",))
        for half in range(2):
            dyn_ld(myx2.ap()[half * EH:(half + 1) * EH, :], ag1zout,
                   half, 0, [[TPC, EH], [1, TPC]], "myx2", reads=("ag1zout",))

        PRI = []
        BULK = []
        _ps = [0]

        def take_ps():
            i = _ps[0] % 2
            _ps[0] += 1
            return (psA, "psA") if i == 0 else (psC, "psC")

        def pop_fill():
            if BULK:
                BULK.pop(0)()

        def inproj_pieces(ck):
            half = ck // 4
            tcl = ck % 4
            ps_list = []

            def p_x2sl(ck=ck, half=half, tcl=tcl):
                for k in range(8):
                    LD(x2sl_v[:, k * 512:(k + 1) * 512],
                       myx2.ap()[half * EH + k * 128: half * EH + (k + 1) * 128,
                                 tcl * 512:(tcl + 1) * 512],
                       "x2sl", reads=("myx2", "x2sl"))
            ps_list.append(p_x2sl)
            for m in range(16):
                def p_m(ck=ck, m=m):
                    par = m % 2
                    LD(wpar_v[:, par * 1024:(par + 1) * 1024],
                       winsp.ap()[m * 128:(m + 1) * 128, :], f"wpar{par}",
                       reads=("winsp", f"wpar{par}"))
                    ps, pk = take_ps()
                    MMC(ps[:, :],
                        [(wpar_v[:, par * 1024 + k * 128: par * 1024 + (k + 1) * 128],
                          x2sl_v[:, k * 512:(k + 1) * 512]) for k in range(8)],
                        (f"wpar{par}", "x2sl"), (pk,))
                    ev = uev_v[:, (m % 8) * 512:(m % 8 + 1) * 512]
                    evk = f"uev{m % 8}"
                    ACT(ev, ps[:, :], AF.Silu, (pk, "consts", evk), (evk,),
                        bias=consts[:, m:m + 1])
                    if m == 7:
                        ST(bass.AP(u_sp, ck * 512, [[TB, 128], [128 * TB, 8], [1, 512]]),
                           uev_v[:, 0:4096], "u_sp", tuple(f"uev{j}" for j in range(8)))
                        MMC(ps96[:, :],
                            [(wot[:, j * 96:(j + 1) * 96], uev_v[:, j * 512:(j + 1) * 512])
                             for j in range(8)],
                            ("wot",) + tuple(f"uev{j}" for j in range(8)), ("ps96",))
                        ev96 = rbcev_v[:, (ck % 2) * 512:(ck % 2 + 1) * 512]
                        COPY(ev96, ps96[:, :], ("ps96", f"rbcev{ck % 2}"),
                             (f"rbcev{ck % 2}",))
                        ST(ag3in[ck * 96:(ck + 1) * 96, :], ev96, "ag3in",
                           (f"rbcev{ck % 2}",))
                    if m == 15:
                        ST(bass.AP(gate_sp, ck * 512, [[TB, 128], [128 * TB, 8], [1, 512]]),
                           uev_v[:, 0:4096], "gate_sp", tuple(f"uev{j}" for j in range(8)))
                ps_list.append(p_m)
            return ps_list

        def ag3cc(ck):
            CC("AllGather", ag3in.ap()[ck * 96:(ck + 1) * 96, :].opt(),
               ag3out.ap()[ck * 768:(ck + 1) * 768, :].opt(),
               ("ag3in",), (f"ag3o{ck}",))

        def rbc(ck):
            dyn_ld(AR2[0:96, 1024:2048], ag3out, 2, ck * 8 * 96 * 512,
                   [[512, 96], [96 * 512, 2], [1, 512]], "art0",
                   reads=(f"ag3o{ck}", "art0", "art1"))
            TT(art[0][0:96, :], art[0][0:96, :], art[1][0:96, :], AL.add,
               ("art0",), ("art0", "art1"))
            TS(art[0][0:96, :], art[0][0:96, :], rbcb_s[:, :], None, AL.add, AL.add,
               ("art0", "rbcb_s"), ("art0",))
            COPY(rbf2_v, art[0][0:64, :], ("art0", "rbf2"), ("rbf2",))
            COPY(bmcm_b[:, ck * 512:(ck + 1) * 512], art[0][64:96, :],
                 ("art0", "bmcm"), ("bmcm",))
            TS(bmcm_b[0:16, ck * 512:(ck + 1) * 512],
               bmcm_b[0:16, ck * 512:(ck + 1) * 512], -1.0, None, AL.mult, AL.add,
               ("bmcm",), ("bmcm",))

        def delta(ck):
            groups = [(0, 3), (3, 3), (6, 2)]
            for g0, gn in groups:
                for j in range(gn):
                    blk = g0 + j
                    ps, pk = take_ps()
                    MM(ps[:, :], wsc[0:64, blk * 128:(blk + 1) * 128], rbf2_v,
                       ("wsc", "rbf2"), (pk,), start=True, stop=True)
                    ACT(art[j], ps[:, :], AF.Sigmoid,
                        (pk, "consts", "cvals", f"art{j}"), (f"art{j}",),
                        bias=consts[:, 16 + blk:17 + blk], scale=cvals[:, 2:3])
                for j in range(gn):
                    blk = g0 + j
                    db = dsgb[j % 2]
                    kdb = f"dsgb{j % 2}"
                    ACT(db, art[j], AF.Ln, (f"art{j}", kdb), (kdb,))
                    ST(lns_sp[blk * 128:(blk + 1) * 128, ck * 512:(ck + 1) * 512], db,
                       "lns_sp", (kdb,), eng="sync")

        def gather_pieces():
            out = []

            def p_f():
                FENCE(("slabx",), ("llistbc", "xsf"))
                FENCE(("xsfo",), ())
            out.append(p_f)
            for fb in range(8):
                def p_fb(fb=fb):
                    for r8 in range(NCORE):
                        LD(slabx_v[:, r8 * TPC:(r8 + 1) * TPC],
                           ag1xout.ap()[r8 * D + fb * 128: r8 * D + (fb + 1) * 128, :],
                           f"slx{r8}", reads=("ag1xout", f"slx{r8}"))
                    for part in range(3):
                        w = 1024 if part < 2 else 512
                        P.emit("gpsimd", lambda e, part=part, w=w: e.indirect_copy(
                            xsf_v[:, part * 1024: part * 1024 + w],
                            slabx_v[:, 0:NCORE * TPC],
                            cidx[:, part * 64: part * 64 + w // 16], True),
                            reads=tuple(f"slx{r}" for r in range(NCORE)) + ("cidx", "xsfo"),
                            writes=("xsfo",))
                    ST(xs_sp[fb * 128:(fb + 1) * 128, :], xsf_v, "xs_sp", ("xsfo",))
                out.append(p_fb)
            return out

        ag4state = [False, False]

        def ffn_pieces():
            out = []

            def p_fence():
                FENCE(("xssl", "h1"), ("xsfo", "slabx"))
            out.append(p_fence)
            for tc5 in range(5):
                def p_xl(tc5=tc5):
                    for k in range(8):
                        LD(xssl_v[:, k * 512:(k + 1) * 512],
                           xs_sp.ap()[k * 128:(k + 1) * 128, tc5 * 512:(tc5 + 1) * 512],
                           "xssl", reads=("xs_sp", "xssl"))
                out.append(p_xl)
                for f in range(32):
                    def p_f(tc5=tc5, f=f):
                        par = f % 2
                        LD(w1t[:, par * 1024:(par + 1) * 1024],
                           w1bf.ap()[f * 128:(f + 1) * 128, :], f"w1t{par}",
                           reads=("w1bf", f"w1t{par}"))
                        ps, pk = take_ps()
                        MMC(ps[:, :],
                            [(w1t[:, par * 1024 + k * 128: par * 1024 + (k + 1) * 128],
                              xssl_v[:, k * 512:(k + 1) * 512]) for k in range(8)],
                            (f"w1t{par}", "xssl"), (pk,))
                        ACT(h1_v[:, f * 512:(f + 1) * 512], ps[:, :], AF.Relu,
                            (pk, "b1_s", "h1"), ("h1",), bias=b1_s[:, f:f + 1])
                    out.append(p_f)
                for d8 in range(8):
                    def p_w2ld(d8=d8):
                        LD(w2t_v[:, :], w2bf.ap()[d8 * 128:(d8 + 1) * 128, :], "w2t",
                           reads=("w2bf", "w2t"))
                    out.append(p_w2ld)
                    psd = (psB, "psB")
                    for g in range(4):
                        def p_sub(tc5=tc5, d8=d8, g=g, psd=psd):
                            ps, pk = psd
                            MMC(ps[:, :],
                                [(w2t_v[:, (g * 8 + f) * 128:(g * 8 + f + 1) * 128],
                                  h1_v[:, (g * 8 + f) * 512:(g * 8 + f + 1) * 512])
                                 for f in range(8)],
                                ("w2t", "h1"), (pk,), first=(g == 0), last=(g == 3))
                        out.append(p_sub)

                    def p_ev(tc5=tc5, d8=d8, psd=psd):
                        ps, pk = psd
                        fe = ffev_v[:, 0:512]
                        TS(fe, ps[:, :], b2_s[:, d8:d8 + 1], None, AL.add, AL.add,
                           (pk, "b2_s", "ffev0"), ("ffev0",))
                        ST(ag4in[d8 * 128:(d8 + 1) * 128, tc5 * 512:(tc5 + 1) * 512], fe,
                           "ag4in", ("ffev0",))
                    out.append(p_ev)
            return out

        def scan_q(q, mid=None):
            while PRI:
                PRI.pop(0)()
            if q == 0:
                FENCE(("bbc", "cbc"), ("lbctab",))
                FENCE(("a0", "a1", "ym0", "ym1", "h0", "h1p", "yac"), ())
                FENCE(("ygt0", "ygt1"), ())
            for n in range(N):
                MM(psC[:, :], selmat[:, n * 128:(n + 1) * 128],
                   bmcm_b[:, q * QT:(q + 1) * QT],
                   ("bmcm", "selmat"), ("psC",), start=True, stop=True)
                SCOPY(bbc_v[:, n * QT:(n + 1) * QT], psC[:, :], ("psC", "bbc"), ("bbc",))
                MM(psY[:, :], selmat[:, (16 + n) * 128:(17 + n) * 128],
                   bmcm_b[:, q * QT:(q + 1) * QT],
                   ("bmcm", "selmat"), ("psYb",), start=True, stop=True)
                SCOPY(cbc_v[:, n * QT:(n + 1) * QT], psY[:, :], ("psYb", "cbc"), ("cbc",))
            for blk in range(NBLK):
                if blk == 2 and mid is not None:
                    mid()
                if blk in (3, 6):
                    npop = (len(PRI) + 1) // 2 if blk == 3 else len(PRI)
                    for _ in range(npop):
                        if PRI:
                            PRI.pop(0)()
                pb = blk % 2
                lk, uk, gk, ygk = f"lns{pb}", f"ut{pb}", f"gt{pb}", f"ygt{pb}"
                lns = lns_t[:, pb * QT:(pb + 1) * QT]
                ut = u_t[:, pb * QT:(pb + 1) * QT]
                gt = gate_t[:, pb * QT:(pb + 1) * QT]
                ygs = ygt[:, pb * QT:(pb + 1) * QT]
                LD(lns, lns_sp.ap()[blk * 128:(blk + 1) * 128, q * QT:(q + 1) * QT],
                   lk, reads=("lns_sp", lk))
                LD(ut, u_sp.ap()[blk * 128:(blk + 1) * 128, q * QT:(q + 1) * QT],
                   uk, reads=("u_sp", uk))
                LD(gt, gate_sp.ap()[blk * 128:(blk + 1) * 128, q * QT:(q + 1) * QT],
                   gk, reads=("gate_sp", gk))
                TT(ndu[:, :], lns, ut, AL.mult, (lk, uk, "ndu"), ("ndu",))
                for n in range(N):
                    pn = n % 2
                    ak, hk, mk = f"a{pn}", (f"h{pn}" if pn == 0 else "h1p"), f"ym{pn}"
                    av = a_v[:, pn * 512:(pn + 1) * 512]
                    hv = h_v[:, pn * 512:(pn + 1) * 512]
                    mv = ymul_v[:, pn * 512:(pn + 1) * 512]
                    ACT(av, lns, AF.Exp, (lk, "nega", ak), (ak,),
                        scale=nega[:, blk * N + n: blk * N + n + 1])
                    TT(b_t[:, :], ndu[:, :], bbc_v[:, n * QT:(n + 1) * QT], AL.mult,
                       ("ndu", "bbc", "b_t"), ("b_t",))
                    P.emit("vector", lambda e, q=q, blk=blk, n=n, hv=hv, av=av:
                           e.tensor_tensor_scan(
                        out=hv, data0=av, data1=b_t[:, :],
                        initial=(0.0 if q == 0 else carryc[:, blk * N + n: blk * N + n + 1]),
                        op0=AL.mult, op1=AL.add),
                        reads=(ak, "b_t", hk) + (() if q == 0 else ("carryc",)),
                        writes=(hk,))
                    COPY(carryc[:, blk * N + n: blk * N + n + 1], hv[:, QT - 1:QT],
                         (hk, "carryc"), ("carryc",))
                    TT(mv, hv, cbc_v[:, n * QT:(n + 1) * QT], AL.mult,
                       (hk, "cbc", mk), (mk,))
                    if n == 0:
                        COPY(yacc_v, mv, (mk, "yac"), ("yac",))
                    else:
                        TT(yacc_v, yacc_v, mv, AL.add, ("yac", mk), ("yac",))
                    if n % 2 == 1:
                        pop_fill()
                STT(yacc_v, ut, consts[:, 32 + blk:33 + blk], yacc_v,
                    AL.mult, AL.add, (uk, "consts", "yac"), ("yac",))
                TT(ygs, yacc_v, gt, AL.mult, ("yac", gk, ygk), (ygk,))
                ST(ag5in[q * EH + blk * 128: q * EH + (blk + 1) * 128, :], ygs,
                   "ag5in", (ygk,), eng="sync")

        def ag5cc(q):
            CC("AllGather", ag5in.ap()[q * EH:(q + 1) * EH, :].opt(),
               ag5out.ap()[q * 8 * EH:(q + 1) * 8 * EH, :].opt(),
               ("ag5in",), (f"ag5o{q}",))

        FENCE(("x2sl",), ())
        FENCE(("psC",), ("psC0", "psC1", "psC2", "psC3"))
        for k in range(8):
            LD(castB[:, 0:96], wdbc_my[k * 128:(k + 1) * 128, :], "castB")
            COPY(wot[:, k * 96:(k + 1) * 96],
                 castB[:, 0:96], ("castB", "wot"), ("wot",))
        LD(castA[0:64, :], wdt_my.ap(), "castA")
        COPY(wsc[0:64, 0:1024], castA[0:64, :], ("castA", "wsc"), ("wsc",))
        FENCE(("rbcev0", "rbcev1"), ("castB",))
        FENCE(("spb0", "spb1"), ("castA",))
        FENCE(("spo0", "spo1"), ())

        _spn = [0]

        def spill_tile(src, rr, cc, dst, dstw, key):
            j = _spn[0] % 2
            _spn[0] += 1
            bsl = castA[:, j * 512:(j + 1) * 512]
            osl = (stile if j == 0 else resid).bitcast(BF16)[:, 512:1024]
            out_ap = bass.AP(dst, cc * 4 * 128 * dstw + rr * 128,
                             [[dstw, 128], [128 * dstw, 4], [1, 128]])
            kb, ko = f"spb{j}", f"spo{j}"
            LD(bsl, src[rr * 128:(rr + 1) * 128, cc * 512:(cc + 1) * 512], kb,
               reads=(kb,), eng="scalar")
            SCOPY(osl, bsl, (kb, ko), (ko,))
            ST(out_ap, osl, key, (ko,), eng="scalar")

        for rr in range(8):
            for cc in range(4):
                spill_tile(win_my.ap(), rr, cc, winsp, D, "winsp")

        def spill_pieces():
            out = []
            for rr in range(8):
                for cc in range(8):
                    out.append(lambda rr=rr, cc=cc: spill_tile(
                        w1e.ap(), rr, cc, w1bf, D, "w1bf"))
            for rr in range(32):
                for cc in range(2):
                    out.append(lambda rr=rr, cc=cc: spill_tile(
                        w2e.ap(), rr, cc, w2bf, FF, "w2bf"))
            for rr in range(16):
                for cc in range(2):
                    out.append(lambda rr=rr, cc=cc: spill_tile(
                        wout_i.ap(), rr, cc, woutbf, E, "woutbf"))
            return out

        FENCE(("diagD",), ())
        COPY(identb_v[:, 0:128], ident_f[:, :], ("ident_f", "diagD"), ("diagD",))
        for blk in range(NBLK):
            TS(diagD_v[:, blk * 128:(blk + 1) * 128], identb_v[:, 0:128],
               consts[:, 32 + blk:33 + blk], None, AL.mult, AL.add,
               ("diagD", "consts"), ("diagD",))

        for go in inproj_pieces(0):
            go()
        ag3cc(0)
        rbc(0)
        delta(0)
        for go in inproj_pieces(1):
            go()
        ag3cc(1)
        FENCE(("lbctab",), ("xtA", "xtB", "sq0", "sq1", "lg_fm"))
        FENCE(("castA",), ("spb0", "spb1"))
        FENCE(("xsf",), ("llistbc",))
        for r8 in range(NCORE):
            LD(castA[:, r8 * 16:(r8 + 1) * 16],
               bass.AP(metaout, (r8 * 4 + 0) * TPC, [[1, 128], [128, 16]]),
               "castA", reads=("metaout",))
            LD(castA[:, 128 + r8 * 16: 128 + (r8 + 1) * 16],
               bass.AP(metaout, (r8 * 4 + 2) * TPC, [[1, 128], [128, 16]]),
               "castA", reads=("metaout",))
        TS(castB[:, 0:128], castA[:, 0:128], myexp_s[:, 0:1], None, AL.is_equal, AL.add,
           ("castA", "myexp_s", "castB"), ("castB",))
        TT(castA[:, 256:384], castB[:, 0:128], castA[:, 128:256], AL.mult,
           ("castB", "castA"), ("castA",))
        MM(psT[:, :], tri_f[:, :], castA[:, 256:384], ("tri_f", "castA"), ("psT",),
           start=True, stop=True)
        COPY(castA[:, 384:512], psT[:, :], ("psT", "castA"), ("castA",))
        LD(castB[0:1, 128:256], castA[127:128, 384:512], "castB", reads=("castA",),
           eng="sync")
        P.emit("tensor", lambda e: e.transpose(
            psT2[:, 0:1], castB[0:1, 128:256], ident_f[0:1, 0:1]),
            reads=("castB", "ident_f"), writes=("psT2",))
        COPY(castB[:, 256:257], psT2[:, 0:1], ("psT2", "castB"), ("castB",))
        MM(psT2[:, 0:1], tri_f[:, :], castB[:, 256:257], ("tri_f", "castB"), ("psT2",),
           start=True, stop=True)
        COPY(castB[:, 257:258], psT2[:, 0:1], ("psT2", "castB"), ("castB",))
        TT(castB[:, 258:259], castB[:, 257:258], castB[:, 256:257], AL.subtract,
           ("castB",), ("castB",))
        P.emit("tensor", lambda e: e.transpose(
            psT2[0:1, 0:128], castB[:, 258:259], ident_f[:, :]),
            reads=("castB", "ident_f"), writes=("psT2",))
        COPY(castB[0:1, 384:512], psT2[0:1, 0:128], ("psT2", "castB"), ("castB",))
        for pp in range(16):
            LD(moepk[pp:pp + 1, 0:128], castB[0:1, 384:512], "moepk",
               reads=("castB",), eng="sync")
        acc_i = moepk[:, 1024:1184]
        acc_p = moepk[:, 1184:1344]
        cmp_t = moepk[:, 1344:1504]
        tmp_t = moepk[:, 1504:1664]
        MEMSET(acc_i, 0.0, ("moepk",))
        MEMSET(acc_p, 0.0, ("moepk",))
        for i in range(128):
            TS(cmp_t, tokid[:, 0:160], moepk[:, i:i + 1], None, AL.is_ge, AL.add,
               ("tokid", "moepk"), ("moepk",))
            TT(acc_i, acc_i, cmp_t, AL.add, ("moepk",), ("moepk",))
            TS(tmp_t, cmp_t, moepk[:, i:i + 1], None, AL.mult, AL.add,
               ("moepk",), ("moepk",))
            TT(acc_p, acc_p, tmp_t, AL.max, ("moepk",), ("moepk",))
        TS(acc_i, acc_i, 1.0, None, AL.subtract, AL.add, ("moepk",), ("moepk",))
        TT(cmp_t, tokid[:, 0:160], acc_p, AL.subtract, ("tokid", "moepk"), ("moepk",))
        TS(tmp_t, acc_i, 128.0, None, AL.mult, AL.add, ("moepk",), ("moepk",))
        TT(tmp_t, tmp_t, cmp_t, AL.add, ("moepk",), ("moepk",))
        P.emit("vector", lambda e: e.tensor_scalar_min(tmp_t, tmp_t, 16383.0),
               reads=("moepk",), writes=("moepk",))
        COPY(comp16[:, 0:160], tmp_t, ("moepk", "comp16"), ("comp16",))
        for g in range(8):
            P.emit("sync", lambda e, g=g: e.dma_start(
                out=cidx[g * 16:(g + 1) * 16, :], in_=comp16[:, :]),
                reads=("comp16",), writes=("cidx",), dma=True)
        FENCE(("llistbc",), ("h1",))
        for i in range(128):
            TS(ARB2[:, i * 128:(i + 1) * 128], iotar[:, :], castA[:, 384 + i:385 + i],
               None, AL.is_ge, AL.add, ("iotar", "castA", "llistbc"), ("llistbc",))
        for ch in range(32):
            MM(psC[0:1, 0:512], onescb[:, 0:1], ARB2[:, ch * 512:(ch + 1) * 512],
               ("onescb", "llistbc"), ("psC",), start=True, stop=True)
            COPY(castB.bitcast(BF16)[0:1, 1024:1536], psC[0:1, 0:512],
                 ("psC", "castB"), ("castB",))
            MM(psA[:, :], ones1b[0:1, :], castB.bitcast(BF16)[0:1, 1024:1536],
               ("ones1b", "castB"), ("psA",), start=True, stop=True)
            COPY(ARBIG.bitcast(BF16)[:, ch * 512:(ch + 1) * 512], psA[:, :],
                 ("psA", "lbctab"), ("lbctab",))
        mr0 = moepk[:, 128:288]
        mr1 = moepk[:, 288:448]
        idr = moepk[:, 448:608]
        for r2 in range(2):
            mr = mr0 if r2 == 0 else mr1
            TS(mr, tmp_t, float(8192 * r2), None, AL.is_ge, AL.add, ("moepk",), ("moepk",))
            if r2 == 0:
                TS(idr, tmp_t, 8192.0, None, AL.is_lt, AL.add, ("moepk",), ("moepk",))
                TT(mr, mr, idr, AL.mult, ("moepk",), ("moepk",))
            TS(idr, tmp_t, float(8192 * r2), None, AL.subtract, AL.add,
               ("moepk",), ("moepk",))
            TT(idr, idr, mr, AL.mult, ("moepk",), ("moepk",))
            COPY(comp16[:, 0:160], idr, ("moepk", "comp16"), ("comp16",))
            for g in range(8):
                P.emit("sync", lambda e, g=g: e.dma_start(
                    out=cidx[g * 16:(g + 1) * 16, :], in_=comp16[:, :]),
                    reads=("comp16",), writes=("cidx",), dma=True)
            for part in range(3):
                w = 1024 if part < 2 else 512
                P.emit("gpsimd", lambda e, part=part, w=w, r2=r2: e.indirect_copy(
                    ARB2[:, r2 * 2560 + part * 1024:
                         r2 * 2560 + part * 1024 + w],
                    ARBIG.bitcast(BF16)[:, 8192 * r2: 8192 * (r2 + 1)],
                    cidx[:, part * 64: part * 64 + w // 16], True),
                    reads=("lbctab", "cidx", "xsf"), writes=("xsf",))
            ST(lv_hbm.ap()[0:1, :], ARB2[0:1, r2 * 2560: r2 * 2560 + 2560],
               f"lv_hbm{r2}", ("xsf",), eng="gpsimd")
            LD(moepk.bitcast(BF16)[:, 7600 + r2 * 160: 7760 + r2 * 160],
               bass.AP(lv_hbm, 0, [[1, 16], [16, 160]]), f"lvint{r2}",
               reads=(f"lv_hbm{r2}", "moepk"))
        COPY(cmp_t, moepk.bitcast(BF16)[:, 7600:7760], ("moepk",), ("moepk",))
        TT(cmp_t, cmp_t, mr0, AL.mult, ("moepk",), ("moepk",))
        COPY(idr, moepk.bitcast(BF16)[:, 7760:7920], ("moepk",), ("moepk",))
        TT(idr, idr, mr1, AL.mult, ("moepk",), ("moepk",))
        TT(cmp_t, cmp_t, idr, AL.add, ("moepk",), ("moepk",))
        TS(tmp_t, acc_i, 128.0, None, AL.mult, AL.add, ("moepk",), ("moepk",))
        TT(tmp_t, tmp_t, cmp_t, AL.add, ("moepk",), ("moepk",))
        P.emit("vector", lambda e: e.tensor_scalar_min(tmp_t, tmp_t, 16383.0),
               reads=("moepk",), writes=("moepk",))
        COPY(comp16[:, 0:160], tmp_t, ("moepk", "comp16"), ("comp16",))
        for g in range(8):
            P.emit("sync", lambda e, g=g: e.dma_start(
                out=cidx[g * 16:(g + 1) * 16, :], in_=comp16[:, :]),
                reads=("comp16",), writes=("cidx",), dma=True)

        LD(midx_v[:, 0:128], bass.AP(metain, 0, [[1, 16], [16, 128]]), "moepk",
           reads=("metain",))
        LD(mkeep_v[:, 0:128], bass.AP(metain, TPC, [[1, 16], [16, 128]]), "moepk",
           reads=("metain",))
        STT(mwork_v[:, 0:128], midx_v[:, 0:128], cvals[0:16, 4:5], mkeep_v[:, 0:128],
            AL.mult, AL.add, ("moepk", "cvals"), ("moepk",))
        for r3 in range(3):
            mr = moepk[:, 608 + r3 * 128: 736 + r3 * 128]
            TS(mr, mwork_v[:, 0:128], float(8192 * r3), None, AL.is_ge, AL.add,
               ("moepk",), ("moepk",))
            if r3 < 2:
                TS(moepk[:, 992:1120] if False else midx_v[:, 128:256],
                   mwork_v[:, 0:128], float(8192 * (r3 + 1)), None, AL.is_lt, AL.add,
                   ("moepk",), ("moepk",))
                TT(mr, mr, midx_v[:, 128:256], AL.mult, ("moepk",), ("moepk",))
            TS(midx_v[:, 128:256], mwork_v[:, 0:128], float(8192 * r3), None,
               AL.subtract, AL.add, ("moepk",), ("moepk",))
            TT(midx_v[:, 128:256], midx_v[:, 128:256], mr, AL.mult,
               ("moepk",), ("moepk",))
            COPY(gidx16[0:16, r3 * 128:(r3 + 1) * 128], midx_v[:, 128:256],
                 ("moepk", "gidx16"), ("gidx16",))
            for g in range(1, 8):
                P.emit("sync", lambda e, g=g, r3=r3: e.dma_start(
                    out=gidx16[g * 16:(g + 1) * 16, r3 * 128:(r3 + 1) * 128],
                    in_=gidx16[0:16, r3 * 128:(r3 + 1) * 128]),
                    reads=("gidx16",), writes=("gidx16",), dma=True)
        gxr = rstat[:, 0:16]
        FENCE(("gxr",), tuple(f"st{it}" for it in range(NT)))
        for r3 in range(3):
            mr = moepk[:, 608 + r3 * 128: 736 + r3 * 128]
            P.emit("tensor", lambda e, mr=mr: e.transpose(
                psT[0:128, 0:16], mr, ident_f[0:16, 0:16]),
                reads=("moepk", "ident_f"), writes=("psT",))
            SCOPY(gxr, psT[0:128, 0:16], ("psT", "gxr"), ("gxr",))
            ST(bass.AP(mrow_hbm, r3 * TPC, [[16, 128], [1, 16]]), gxr,
               f"mrowh{r3}", ("gxr",), eng="gpsimd")

        FENCE(("spb0", "spb1"), ("castA",))
        BULK.extend(gather_pieces())
        BULK.extend(spill_pieces())
        BULK.extend(ffn_pieces())
        for ck in range(NQ):
            if ck + 2 < NQ:
                PRI.extend(inproj_pieces(ck + 2))

            def mid(ck=ck):
                if ck + 1 < NQ:
                    rbc(ck + 1)
                    delta(ck + 1)
            scan_q(ck, mid=mid)
            while PRI:
                PRI.pop(0)()
            if ck + 2 < NQ:
                ag3cc(ck + 2)
            if ck >= NQ - 3 and not BULK and not ag4state[0]:
                CC("AllGather", ag4in.ap().opt(), ag4out.ap().opt(),
                   ("ag4in",), ("ag4out",))
                ag4state[0] = True
            ag5cc(ck)
        while BULK:
            BULK.pop(0)()
        if not ag4state[0]:
            CC("AllGather", ag4in.ap().opt(), ag4out.ap().opt(), ("ag4in",), ("ag4out",))

        FENCE(("gvmbc",), ("a0", "a1", "ym0", "ym1", "h0", "h1p", "yac",
                           "art0", "art1", "art2"))
        FENCE(("gvbc", "graw"), ("xssl", "xsfo"))
        FENCE(("castA", "castB"), ("spb0", "spb1", "rbcev0", "rbcev1"))
        FENCE(("gvrow",), ("moepk",))
        for r3 in range(3):
            for hh in range(2):
                LD(castA[0:1, :], mrow_hbm.ap()[r3:r3 + 1, hh * 1024:(hh + 1) * 1024],
                   "castA", reads=(f"mrowh{r3}", "castA"))
                LD(castB[0:1, :], metain.ap()[3:4, hh * 1024:(hh + 1) * 1024],
                   "castB", reads=("metain", "castB"))
                TT(castA[0:1, :], castA[0:1, :], castB[0:1, :], AL.mult,
                   ("castA", "castB"), ("castA",))
                COPY(gvrow[:, hh * 1024:(hh + 1) * 1024], castA[0:1, :],
                     ("castA", "gvrow"), ("gvrow",))
            for tc in range(4):
                MM(psC[:, :], ones1b[0:1, :], gvrow[0:1, tc * 512:(tc + 1) * 512],
                   ("ones1b", "gvrow"), ("psC",), start=True, stop=True)
                COPY(gvm_bc[:, r3 * 2048 + tc * 512: r3 * 2048 + (tc + 1) * 512],
                     psC[:, :], ("psC", "gvmbc"), ("gvmbc",))
        FENCE(("gfb",), ("h1", "slabx"))
        FENCE(("gacck",), ("yac", "art2"))
        FENCE(tuple(f"sl{r8}" for r8 in range(NCORE)),
              ("bbc", "cbc", "wpar0", "wpar1", "diagD"))
        gacc2 = AR2.bitcast(BF16)[:, 6144:8192]
        for fb in range(8):
            for r8 in range(NCORE):
                LD(slab_v[:, r8 * CAP:(r8 + 1) * CAP],
                   ag4out.ap()[r8 * D + fb * 128: r8 * D + (fb + 1) * 128, :],
                   f"sl{r8}", reads=("ag4out", f"sl{r8}"))
            t2 = gfb_v[:, fb * TPC:(fb + 1) * TPC]

            for r3 in range(3):
                dst = (gacc_v, gtmp_v, t2)[r3]

                def icpair(e, dst=dst, r3=r3):
                    last = None
                    for part in range(2):
                        last = e.indirect_copy(
                            dst[:, part * 1024:(part + 1) * 1024],
                            slab_v[:, 8192 * r3: min(8192 * (r3 + 1), NCORE * CAP)],
                            gidx16[:, r3 * 128 + part * 64: r3 * 128 + (part + 1) * 64],
                            True)
                    return last
                P.emit("gpsimd", icpair,
                       reads=tuple(f"sl{r8}" for r8 in range(NCORE)) +
                             ("gidx16", f"gr{r3}") + (("gfb",) if r3 == 2 else ()),
                       writes=(f"gr{r3}",))
            TT(gacc2, gacc_v, gvm_bc[:, 0:2048], AL.mult,
               ("gr0", "gvmbc", "gacck"), ("gacck",))
            TT(gtmp_v, gtmp_v, gvm_bc[:, 2048:4096], AL.mult,
               ("gr1", "gvmbc"), ("gr1",))
            TT(gacc2, gacc2, gtmp_v, AL.add, ("gacck", "gr1"), ("gacck",))
            TT(t2, t2, gvm_bc[:, 4096:6144], AL.mult, ("gr2", "gvmbc"), ("gr2",))
            TT(gacc2, gacc2, t2, AL.add, ("gacck", "gr2"), ("gacck",))
            ACT(gfb_v[:, fb * TPC:(fb + 1) * TPC], gacc2, AF.Silu,
                ("gacck", "gfb", "gr2"), ("gfb",))

        FENCE(("ysl",), ("gvmbc", "gacck"))
        FENCE(("ottm",), ("x2sl", "w2t"))
        FENCE(("wot0", "wot1"), ("w1t0", "w1t1", "wot", "wsc"))
        FENCE(("stile", "resid", "sgt"), ("dsgb0", "dsgb1", "spo0", "spo1", "ffev0"))
        FENCE(("xrow_v",), ("gvbc", "graw"))
        for tc in range(4):
            for half in range(2):
                dyn_ld(ysl_v[:, half * 4096:(half + 1) * 4096], ag5out, 4 + half,
                       tc * 8 * EH * 512,
                       [[512, 128], [128 * 512, 8], [1, 512]], "ysl",
                       reads=(f"ag5o{tc}", f"ag5o{4 + tc}", "ysl"))
            for d8 in range(8):
                par = d8 % 2
                wbuf = wot if par == 0 else w1t
                LD(wbuf[:, 0:2048], woutbf.ap()[d8 * 128:(d8 + 1) * 128, :],
                   f"wot{par}", reads=("woutbf", f"wot{par}"))
                ps, pk = take_ps()
                MMC(ps[:, :],
                    [(wbuf[:, eb * 128:(eb + 1) * 128],
                      ysl_v[:, eb * 512:(eb + 1) * 512]) for eb in range(16)],
                    (f"wot{par}", "ysl"), (pk,))
                TS(stile[:, :], ps[:, :], consts[:, 24 + d8:25 + d8], None, AL.add, AL.add,
                   (pk, "consts", "stile"), ("stile",))
                LD(resid.bitcast(BF16)[:, 512:1024],
                   ag1x.ap()[d8 * 128:(d8 + 1) * 128, tc * 512:(tc + 1) * 512],
                   "resid", reads=("ag1x", "resid"))
                TT(stile[:, :], stile[:, :], resid.bitcast(BF16)[:, 512:1024], AL.add,
                   ("stile", "resid"), ("stile",))
                TT(sgt[:, :], stile[:, :],
                   gfb_v[:, d8 * TPC + tc * 512: d8 * TPC + (tc + 1) * 512],
                   AL.mult, ("stile", "gfb", "sgt"), ("sgt",))
                for sub in range(4):
                    pst = psT if sub % 2 == 0 else psT2
                    pstk = "psT" if sub % 2 == 0 else "psT2"
                    TRANS(pst[:, :], sgt[:, sub * 128:(sub + 1) * 128], ("sgt",), (pstk,))
                    SCOPY(ottm_v[:, sub * 1024 + d8 * 128: sub * 1024 + (d8 + 1) * 128],
                          pst[:, :], (pstk, "ottm"), ("ottm",))
            for sub in range(4):
                it = tc * 4 + sub
                LD(xrow_v, xrows[it * 128:(it + 1) * 128, :], "xrow_v",
                   reads=("xrow_v",))
                TT(ottm_v[:, sub * 1024:(sub + 1) * 1024],
                   ottm_v[:, sub * 1024:(sub + 1) * 1024], xrow_v, AL.add,
                   ("xrow_v", "ottm"), ("ottm",))
                ST(out_rows[it * 128:(it + 1) * 128, :], ottm_v[:, sub * 1024:(sub + 1) * 1024],
                   "out", ("ottm",))

        if limit is None:
            P.finalize_outputs()
        else:
            P.limit = limit

        @block.tensor
        def _(e):
            P.replay("tensor", e, sems)

        @block.vector
        def _(e):
            P.replay("vector", e, sems)

        @block.scalar
        def _(e):
            P.replay("scalar", e, sems)

        @block.gpsimd
        def _(e):
            P.replay("gpsimd", e, sems)

        @block.sync
        def _(e):
            P.replay("sync", e, sems)

    return nc


def _in_maps(inputs):
    f32 = lambda a: np.ascontiguousarray(a, dtype=np.float32)
    x = f32(inputs["L0_attn_x"]).reshape(NCORE * TPC, D)
    W_in = f32(inputs["L0_attn_W_in"])
    b_in = f32(inputs["L0_attn_b_in"])
    W_delta = f32(inputs["L0_attn_W_delta"])
    b_delta = f32(inputs["L0_attn_b_delta"])
    W_dt = f32(inputs["L0_attn_W_dt"])
    b_dt = f32(inputs["L0_attn_b_dt"])
    W_B = f32(inputs["L0_attn_W_B"])
    b_B = f32(inputs["L0_attn_b_B"])
    W_C = f32(inputs["L0_attn_W_C"])
    b_C = f32(inputs["L0_attn_b_C"])
    A_log = f32(inputs["L0_attn_A_log"])
    Dv = f32(inputs["L0_attn_D"])
    W_out = f32(inputs["L0_attn_W_out"])
    b_out = f32(inputs["L0_attn_b_out"])
    ln_g = f32(inputs["L0_attn_ln_g"])
    ln_b = f32(inputs["L0_attn_ln_b"])
    gate_w = f32(inputs["L0_ffn_gate_w"])
    gate_b = f32(inputs["L0_ffn_gate_b"])
    w1 = f32(inputs["L0_ffn_w1"])
    b1 = f32(inputs["L0_ffn_b1"])
    w2 = f32(inputs["L0_ffn_w2"])
    b2 = f32(inputs["L0_ffn_b2"])

    maps = []
    for c in range(NCORE):
        b, h = c // 2, c % 2
        sl = slice(h * EH, (h + 1) * EH)
        win = np.concatenate([W_in[:, h * EH:(h + 1) * EH],
                              W_in[:, E + h * EH: E + (h + 1) * EH]], axis=1)
        binv = np.concatenate([b_in[h * EH:(h + 1) * EH],
                               b_in[E + h * EH: E + (h + 1) * EH]]).reshape(16, 128)
        wdbc = np.concatenate([W_delta[sl], W_B[sl], W_C[sl]], axis=1)
        rbcb = np.concatenate([b_delta, b_B, b_C]).reshape(96, 1)
        bases = np.array([[
            (2 * b) * D * TPC,
            (2 * b + 1) * D * TPC,
            (2 * b) * 96 * 512,
            (2 * b + 1) * 96 * 512,
            4 * h * 8 * EH * 512 + (2 * b) * EH * 512,
            4 * h * 8 * EH * 512 + (2 * b + 1) * EH * 512,
            0, 0]], dtype=np.int32)
        maps.append({
            "xrows": x[c * TPC:(c + 1) * TPC],
            "win_my": win,
            "bin_my": binv,
            "bdt_my": b_dt[sl].reshape(NBLK, 128),
            "wdbc_my": wdbc,
            "rbc_bias": rbcb,
            "wdt_my": np.ascontiguousarray(W_dt[:, sl]),
            "alog_my": A_log[sl],
            "d_my": Dv[sl].reshape(NBLK, 128),
            "wout_i": W_out,
            "bout_i": b_out.reshape(NBLK, 128),
            "lng_i": ln_g.reshape(NBLK, 128),
            "lnb_i": ln_b.reshape(NBLK, 128),
            "gwT": np.ascontiguousarray(gate_w.T),
            "gb_i": gate_b.reshape(NE, 1),
            "w1e": w1[c],
            "w2e": w2[c],
            "b1e": b1[c].reshape(32, 128),
            "b2e": b2[c].reshape(NBLK, 128),
            "prefmask": (np.arange(NE) < c).astype(np.float32).reshape(NE, 1),
            "myexp_i": np.full((128, 1), c, np.float32),
            "bases_i": bases,
        })
    return maps


_NC_CACHE = {}


def kernel(**inputs):
    if "nc" not in _NC_CACHE:
        _NC_CACHE["nc"] = build_nc()
    nc = _NC_CACHE["nc"]
    maps = _in_maps(inputs)
    res = run_bass_kernel_spmd(nc, maps, core_ids=list(range(NCORE)))
    out = np.concatenate([r["out_rows"] for r in res.results], axis=0)
    return out.reshape(4, 4096, 1024)


# version 23 (final)
# speedup vs baseline: 1.0289x; 1.0001x over previous
import numpy as np
import contextlib
import concourse.bass as bass
import concourse.mybir as mybir
from concourse.bass_utils import run_bass_kernel_spmd

F32 = mybir.dt.float32
BF16 = mybir.dt.bfloat16
U16 = mybir.dt.uint16
U32 = mybir.dt.uint32
I32 = mybir.dt.int32
AL = mybir.AluOpType
AF = mybir.ActivationFunctionType
AX = mybir.AxisListType.X

NCORE = 8
TPC = 2048
TB = 4096
D = 1024
E = 2048
EH = 1024
NBLK = 8
N = 16
R = 64
FF = 4096
CAP = 2560
NE = 8
QT = 512
NQ = TB // QT
NT = TPC // 128
EPS = 1e-5


class Prog:

    ENGINES = ("tensor", "vector", "scalar", "gpsimd", "sync")
    QPOOL = 10

    def __init__(self):
        self.stream = {e: [] for e in self.ENGINES}
        self.cnt = {}
        self.events = {}
        self.readers = {}
        self.seen = {e: {} for e in self.ENGINES}
        self.groups = {}
        self.rr = {}
        self.semcum = {}
        self.dma_sems = set()

    def _qsem(self, q, i):
        name = f"{q}_q{i}"
        self.dma_sems.add(name)
        return name

    def _open(self, q):
        g = self.groups.get(q)
        if g is not None and not g["closed"]:
            return g
        i = self.rr.get(q, 0)
        self.rr[q] = i + 1
        sem = self._qsem(q, i % self.QPOOL)
        base = self.semcum.get(sem, 0)
        if base > 0:
            self.gseq = getattr(self, "gseq", 0) + 1
            self.stream[q].append((None, [(sem, base)], None, self.gseq))
            self.seen[q][sem] = max(self.seen[q].get(sem, 0), base)
        g = {"sem": sem, "base": base, "n": 0, "closed": False}
        self.groups[q] = g
        return g

    def _close(self, g):
        if not g["closed"]:
            g["closed"] = True
            g["total"] = g["base"] + 16 * g["n"]
            self.semcum[g["sem"]] = g["total"]

    def emit(self, eng, fn, reads=(), writes=(), dma=False):
        import sys as _sys
        f = _sys._getframe(1)
        tag = f"{f.f_lineno}"
        if f.f_code.co_name in ("LD", "ST", "MM", "MMC", "SCOPY", "ACT", "COPY", "TT",
                                "TS", "STT", "RED", "MEMSET", "CC", "FENCE", "TRANS", "VOP"):
            f2 = f.f_back
            tag = f"{f.f_code.co_name}@{f2.f_lineno}"
        self.tags = getattr(self, "tags", [])
        waits = {}

        def need(ev):
            if ev is None:
                return
            if isinstance(ev, dict):
                self._close(ev)
                s, v = ev["sem"], ev["total"]
            else:
                s, v = ev
            if self.seen[eng].get(s, 0) >= v:
                return
            if waits.get(s, 0) < v:
                waits[s] = v

        for k in reads:
            need(self.events.get(k))
        for k in writes:
            need(self.events.get(k))
            for ev in self.readers.get(k, ()):
                need(ev)
        if dma:
            g = self._open(eng)
            g["n"] += 1
            ev = g
            inc = (g["sem"], 16)
        else:
            sem = eng
            self.cnt[sem] = self.cnt.get(sem, 0) + 1
            ev = (sem, self.cnt[sem])
            inc = (sem, 1)
        for k in reads:
            self.readers.setdefault(k, []).append(ev)
        for k in writes:
            self.events[k] = ev
            self.readers[k] = []
        for s, v in waits.items():
            self.seen[eng][s] = max(self.seen[eng].get(s, 0), v)
        self.gseq = getattr(self, "gseq", 0) + 1
        self.tags.append((self.gseq, eng, tag))
        self.stream[eng].append((fn, sorted(waits.items()), inc, self.gseq))

    def finalize_outputs(self):
        for q in ("gpsimd", "sync", "scalar"):
            g = self.groups.get(q)
            if g is not None:
                self._close(g)
            for i in range(self.QPOOL):
                sem = f"{q}_q{i}"
                tot = self.semcum.get(sem, 0)
                if tot > 0 and self.seen[q].get(sem, 0) < tot:
                    self.gseq = getattr(self, "gseq", 0) + 1
                    self.stream[q].append((None, [(sem, tot)], None, self.gseq))
                    self.seen[q][sem] = tot

    limit = None

    def replay(self, eng_name, e, sems):
        for fn, waits, inc, seq in self.stream[eng_name]:
            if self.limit is not None and seq > self.limit:
                continue
            for s, v in waits:
                e.wait_ge(sems[s], v)
            if fn is None:
                continue
            instr = fn(e)
            if inc is not None and instr is not None:
                instr.then_inc(sems[inc[0]], inc[1])


def build_nc(limit=None):
    nc = bass.Bass("TRN2", target_bir_lowering=False)

    def ein(name, shape, dt=F32):
        return nc.dram_tensor(name, shape, dt, kind="ExternalInput")

    xrows = ein("xrows", [TPC, D])
    win_my = ein("win_my", [D, 2048])
    bin_my = ein("bin_my", [16, 128])
    bdt_my = ein("bdt_my", [NBLK, 128])
    wdbc_my = ein("wdbc_my", [EH, 96])
    rbc_bias = ein("rbc_bias", [96, 1])
    wdt_my = ein("wdt_my", [R, EH])
    alog_my = ein("alog_my", [EH, N])
    d_my = ein("d_my", [NBLK, 128])
    wout_i = ein("wout_i", [E, D])
    bout_i = ein("bout_i", [NBLK, 128])
    lng_i = ein("lng_i", [NBLK, 128])
    lnb_i = ein("lnb_i", [NBLK, 128])
    gwT = ein("gwT", [D, NE])
    gb_i = ein("gb_i", [NE, 1])
    w1e = ein("w1e", [D, FF])
    w2e = ein("w2e", [FF, D])
    b1e = ein("b1e", [32, 128])
    b2e = ein("b2e", [NBLK, 128])
    prefmask = ein("prefmask", [NE, 1])
    myexp_i = ein("myexp_i", [128, 1])
    bases_i = ein("bases_i", [1, 8], I32)
    out_rows = nc.dram_tensor("out_rows", [TPC, D], F32, kind="ExternalOutput")

    dr = lambda name, shape, dt: nc.dram_tensor(name, shape, dt)
    ag1z = dr("ag1z", [D, TPC], BF16)
    ag1zout = dr("ag1zout", [NCORE * D, TPC], BF16)
    ag1x = dr("ag1x", [D, TPC], BF16)
    ag1xout = dr("ag1xout", [NCORE * D, TPC], BF16)
    cntin = dr("cntin", [1, NE], F32)
    cntout = dr("cntout", [NCORE, NE], F32)
    metain = dr("metain", [4, TPC], F32)
    metaout = dr("metaout", [NCORE * 4, TPC], F32)
    ag3in = dr("ag3in", [NQ * 96, QT], F32)
    ag3out = dr("ag3out", [NQ * NCORE * 96, QT], F32)
    ag4in = dr("ag4in", [D, CAP], BF16)
    ag4out = dr("ag4out", [NCORE * D, CAP], BF16)
    ag5in = dr("ag5in", [NQ * EH, QT], BF16)
    ag5out = dr("ag5out", [NQ * NCORE * EH, QT], BF16)
    u_sp = dr("u_sp", [EH, TB], BF16)
    gate_sp = dr("gate_sp", [EH, TB], BF16)
    lns_sp = dr("lns_sp", [EH, TB], BF16)
    winsp = dr("winsp", [E, D], BF16)
    w1bf = dr("w1bf", [FF, D], BF16)
    w2bf = dr("w2bf", [D, FF], BF16)
    woutbf = dr("woutbf", [D, E], BF16)
    xs_sp = dr("xs_sp", [D, CAP], BF16)
    myx2 = dr("myx2", [2 * EH, TPC], BF16)
    lv_hbm = dr("lv_hbm", [1, CAP], BF16)
    mrow_hbm = dr("mrow_hbm", [3, TPC], F32)

    P = Prog()

    import contextlib
    with contextlib.ExitStack() as _stack:
        _E = _stack.enter_context
        _stack.enter_context(nc.allow_non_contiguous_dma(
            reason="small transposed parameter loads / column stores"))
        onesf = _E(nc.sbuf_tensor("onesf", [128, 8], F32))
        ident_f = _E(nc.sbuf_tensor("ident_f", [128, 128], F32))
        tri_f = _E(nc.sbuf_tensor("tri_f", [128, 128], F32))
        ones1b = _E(nc.sbuf_tensor("ones1b", [1, 128], BF16))
        nones1b = _E(nc.sbuf_tensor("nones1b", [1, 128], BF16))
        ones1f = _E(nc.sbuf_tensor("ones1f", [1, 128], F32))
        iota8 = _E(nc.sbuf_tensor("iota8", [128, 8], F32))
        consts = _E(nc.sbuf_tensor("consts", [128, 64], F32))
        b1_s = _E(nc.sbuf_tensor("b1_s", [128, 32], F32))
        b2_s = _E(nc.sbuf_tensor("b2_s", [128, 8], F32))
        gwT_f = _E(nc.sbuf_tensor("gwT_f", [128, 64], F32))
        gb_s = _E(nc.sbuf_tensor("gb_s", [8, 1], F32))
        rbcb_s = _E(nc.sbuf_tensor("rbcb_s", [96, 1], F32))
        prefm_s = _E(nc.sbuf_tensor("prefm_s", [8, 1], F32))
        myexp_s = _E(nc.sbuf_tensor("myexp_s", [128, 1], F32))
        iotar = _E(nc.sbuf_tensor("iotar", [128, 128], F32))
        onescb = _E(nc.sbuf_tensor("onescb", [128, 2], BF16))
        bases_s = _E(nc.sbuf_tensor("bases_s", [1, 8], I32))
        nega = _E(nc.sbuf_tensor("nega", [128, NBLK * N], F32))
        fencebuf = _E(nc.sbuf_tensor("fencebuf", [1, 4], F32))
        cvals = _E(nc.sbuf_tensor("cvals", [128, 8], F32))
        ARBIG = _E(nc.sbuf_tensor("ARBIG", [128, 10240], F32))
        ARB2 = _E(nc.sbuf_tensor("ARB2", [128, 16384], BF16))
        AR2 = _E(nc.sbuf_tensor("AR2", [128, 4096], F32))
        ARX = _E(nc.sbuf_tensor("ARX", [128, 4096], BF16))
        ARW = _E(nc.sbuf_tensor("ARW", [128, 8192], BF16))
        ARE = _E(nc.sbuf_tensor("ARE", [128, 2048], F32))
        bmcm_b = _E(nc.sbuf_tensor("bmcm_b", [32, TB], BF16))
        castA = _E(nc.sbuf_tensor("castA", [128, 1024], F32))
        castB = _E(nc.sbuf_tensor("castB", [128, 1024], F32))
        moepk = _E(nc.sbuf_tensor("moepk", [16, 4096], F32))
        tokid = _E(nc.sbuf_tensor("tokid", [16, 1024], F32))
        comp16 = _E(nc.sbuf_tensor("comp16", [16, CAP // 16], U16))
        wsc = _E(nc.sbuf_tensor("wsc", [64, 2048], BF16))
        cidx = _E(nc.sbuf_tensor("cidx", [128, CAP // 16], U16))
        gidx16 = _E(nc.sbuf_tensor("gidx16", [128, 384], U16))
        selmat = _E(nc.sbuf_tensor("selmat", [32, 4096], BF16))
        w1t = _E(nc.sbuf_tensor("w1t", [128, 2048], BF16))
        wot = _E(nc.sbuf_tensor("wot", [128, 2048], BF16))
        lns_t = _E(nc.sbuf_tensor("lns_t", [128, 2 * QT], BF16))
        u_t = _E(nc.sbuf_tensor("u_t", [128, 2 * QT], BF16))
        gate_t = _E(nc.sbuf_tensor("gate_t", [128, 2 * QT], BF16))
        ndu = _E(nc.sbuf_tensor("ndu", [128, QT], BF16))
        b_t = _E(nc.sbuf_tensor("b_t", [128, QT], BF16))
        ygt = _E(nc.sbuf_tensor("ygt", [128, 2 * QT], BF16))
        rstat = _E(nc.sbuf_tensor("rstat", [128, 128], F32))
        carryc = _E(nc.sbuf_tensor("carryc", [128, NBLK * N], F32))
        lg_tm = _E(nc.sbuf_tensor("lg_tm", [128, NT * 8], F32))
        oh_tm = _E(nc.sbuf_tensor("oh_tm", [128, NT * 8], F32))
        pos_tm = _E(nc.sbuf_tensor("pos_tm", [128, NT * 8], F32))
        rcols = _E(nc.sbuf_tensor("rcols", [128, NT * 10], F32))
        base8 = _E(nc.sbuf_tensor("base8", [128, 8], F32))
        cnt_s = _E(nc.sbuf_tensor("cnt_s", [8, 8], F32))
        rtmp8 = _E(nc.sbuf_tensor("rtmp8", [128, 8], F32))
        stile = _E(nc.sbuf_tensor("stile", [128, 512], F32))
        resid = _E(nc.sbuf_tensor("resid", [128, 512], F32))
        sgt = _E(nc.sbuf_tensor("sgt", [128, 512], F32))
        x2stage = _E(nc.sbuf_tensor("x2stage", [128, 1024], BF16))
        xnstage = _E(nc.sbuf_tensor("xnstage", [128, 1024], BF16))
        psA = _E(nc.psum_tensor("psA", [128, 512], F32))
        psB = _E(nc.psum_tensor("psB", [128, 512], F32))
        psC = _E(nc.psum_tensor("psC", [128, 512], F32))
        ps96 = _E(nc.psum_tensor("ps96", [96, 512], F32))
        psT = _E(nc.psum_tensor("psT", [128, 128], F32))
        psT2 = _E(nc.psum_tensor("psT2", [128, 128], F32))
        psY = _E(nc.psum_tensor("psY", [128, 512], F32))
        ps8 = _E(nc.psum_tensor("ps8", [128, 8], F32))
        s_tensor = _E(nc.semaphore("s_tensor"))
        s_vector = _E(nc.semaphore("s_vector"))
        s_scalar = _E(nc.semaphore("s_scalar"))
        s_gpsimd = _E(nc.semaphore("s_gpsimd"))
        s_sync = _E(nc.semaphore("s_sync"))
        qsems = {}
        for q in ("sync", "gpsimd", "scalar"):
            for i in range(Prog.QPOOL):
                qsems[f"{q}_q{i}"] = _E(nc.semaphore(f"{q}_q{i}"))
        block = _E(nc.Block())
        sems = {
            "tensor": s_tensor, "vector": s_vector, "scalar": s_scalar,
            "gpsimd": s_gpsimd, "sync": s_sync,
        }
        sems.update(qsems)

        xtA = ARBIG[:, 0:1024]
        xtB = ARBIG[:, 1024:2048]
        sqA = ARBIG[:, 2048:3072]
        sqB = ARBIG[:, 5120:6144]
        lg_fm = ARBIG[0:8, 3072:5120]
        rbct_v = ARBIG[0:96, 0:4096]
        slab_v = ARBIG[:, 0:10240].bitcast(BF16)
        bbc_v = ARBIG[:, 0:4096].bitcast(BF16)
        cbc_v = ARBIG[:, 4096:8192].bitcast(BF16)
        wpar_v = ARBIG.bitcast(BF16)[:, 16384:18432]
        diagD_v = ARBIG.bitcast(BF16)[:, 18432:19456]
        identb_v = ARBIG.bitcast(BF16)[:, 19456:19584]
        slabx_v = ARB2
        h1_v = ARB2
        gfb_v = ARB2
        a_v = AR2[:, 0:1024]
        art = [AR2[:, 1024:1536], AR2[:, 1536:2048],
               AR2[:, 3584:4096]]
        yacc_v = AR2[:, 3072:3584]
        h_v = AR2.bitcast(BF16)[:, 4096:5120]
        ymul_v = AR2.bitcast(BF16)[:, 5120:6144]
        gvm_bc = AR2.bitcast(BF16)
        ysl_v = AR2[:, 0:4096].bitcast(BF16)
        xsf_v = ARX[:, 0:CAP]
        xssl_v = ARX
        gacc_v = ARX[:, 0:2048]
        gtmp_v = ARX[:, 2048:4096]
        xrow_v = ARX.bitcast(F32)[:, 0:1024]
        x2sl_v = ARW[:, 0:4096]
        w2t_v = ARW[:, 4096:8192]
        ottm_v = ARW.bitcast(F32)
        uev_v = ARE.bitcast(BF16)
        rbcev_v = castB[0:96, 0:1024]
        ffev_v = sgt.bitcast(BF16)[:, 0:512]
        rbf2_v = wsc[0:64, 1024:1536]
        gvrow = moepk.bitcast(BF16)[0:1, 4096:6144]
        dsgb = [stile.bitcast(BF16)[:, 0:512],
                resid.bitcast(BF16)[:, 0:512]]
        midx_v = moepk[:, 0:1024]
        mslot_v = moepk[:, 1024:2048]
        mkeep_v = moepk[:, 2048:3072]
        mwork_v = moepk[:, 3072:4096]

        def LD(dst, src, key, reads=(), eng="sync"):
            P.emit(eng, lambda e, d=dst, s=src: e.dma_start(out=d, in_=s),
                   reads=tuple(reads), writes=(key,), dma=True)

        def ST(dst, src, key, reads, eng="gpsimd"):
            P.emit(eng, lambda e, d=dst, s=src: e.dma_start(out=d, in_=s),
                   reads=tuple(reads), writes=(key,), dma=True)

        def MM(outp, lhsT, rhs, reads, writes, start, stop):
            P.emit("tensor",
                   lambda e, o=outp, l=lhsT, r=rhs, s1=start, s2=stop:
                   e.matmul(o, l, r, start=s1, stop=s2),
                   reads=tuple(reads), writes=tuple(writes))

        def MMC(outp, pairs, reads, writes, first=True, last=True):
            def fn(e, o=outp, ps=tuple(pairs), first=first, last=last):
                n = len(ps)
                for i, (l, r) in enumerate(ps):
                    mm = e.matmul(o, l, r, start=(first and i == 0),
                                  stop=(last and i == n - 1))
                return mm
            P.emit("tensor", fn, reads=tuple(reads), writes=tuple(writes))

        def SCOPY(outp, inp, reads, writes):
            P.emit("scalar", lambda e, o=outp, i=inp: e.copy(out=o, in_=i),
                   reads=tuple(reads), writes=tuple(writes))

        def TRANS(outp, inp, reads, writes):
            P.emit("tensor",
                   lambda e, o=outp, i=inp: e.transpose(o, i, ident_f[:, :]),
                   reads=tuple(reads) + ("ident_f",), writes=tuple(writes))

        def ACT(outp, inp, func, reads, writes, bias=0.0, scale=1.0):
            P.emit("scalar",
                   lambda e, o=outp, i=inp, f=func, b=bias, s=scale:
                   e.activation(out=o, in_=i, func=f, bias=b, scale=s),
                   reads=tuple(reads), writes=tuple(writes))

        def COPY(outp, inp, reads, writes, eng="vector"):
            P.emit(eng, lambda e, o=outp, i=inp: e.tensor_copy(out=o, in_=i),
                   reads=tuple(reads), writes=tuple(writes))

        def TT(outp, in0, in1, op, reads, writes, eng="vector"):
            P.emit(eng, lambda e, o=outp, a=in0, b=in1, p=op:
                   e.tensor_tensor(out=o, in0=a, in1=b, op=p),
                   reads=tuple(reads), writes=tuple(writes))

        def TS(outp, in0, s1, s2, op0, op1, reads, writes, eng="vector"):
            if s2 is None:
                op1 = AL.bypass
            P.emit(eng, lambda e, o=outp, a=in0, x=s1, y=s2, p=op0, q=op1:
                   e.tensor_scalar(out=o, in0=a, scalar1=x, scalar2=y, op0=p, op1=q),
                   reads=tuple(reads), writes=tuple(writes))

        def STT(outp, in0, scal, in1, op0, op1, reads, writes, eng="vector"):
            P.emit(eng, lambda e, o=outp, a=in0, s=scal, b=in1, p=op0, q=op1:
                   e.scalar_tensor_tensor(out=o, in0=a, scalar=s, in1=b, op0=p, op1=q),
                   reads=tuple(reads), writes=tuple(writes))

        def RED(outp, inp, op, reads, writes, negate=False):
            P.emit("vector", lambda e, o=outp, i=inp, p=op, n=negate:
                   e.tensor_reduce(o, i, axis=AX, op=p, negate=n),
                   reads=tuple(reads), writes=tuple(writes))

        def MEMSET(ap, val, writes, eng="vector"):
            P.emit(eng, lambda e, a=ap, v=val: e.memset(a, v), reads=(), writes=tuple(writes))

        def CC(kind, ins_ap, outs_ap, reads, writes):
            P.emit("gpsimd", lambda e, i=ins_ap, u=outs_ap, k=kind:
                   e.collective_compute(k, AL.bypass, replica_groups=[list(range(NCORE))],
                                        ins=[i], outs=[u]),
                   reads=tuple(reads), writes=tuple(writes))
            v = P.cnt["gpsimd"]
            P.gseq = getattr(P, "gseq", 0) + 1
            P.stream["gpsimd"].append((None, [("gpsimd", v)], None, P.gseq))
            P.seen["gpsimd"]["gpsimd"] = max(P.seen["gpsimd"].get("gpsimd", 0), v)

        def FENCE(new_keys, old_keys):
            if isinstance(new_keys, str):
                new_keys = (new_keys,)
            P.emit("vector", lambda e: e.memset(fencebuf[:, :], 0.0),
                   reads=(), writes=tuple(new_keys) + tuple(old_keys) + ("fencebuf",))

        _reg_cache = {}

        def dyn_ld(dst, handle, base_idx, const_off, apdims, key, reads=()):
            def fn(e, d=dst, h=handle, bi=base_idx, co=const_off, ap=apdims):
                rk = id(e)
                if rk not in _reg_cache:
                    _reg_cache[rk] = e.alloc_register("dynoff")
                r = _reg_cache[rk]
                e.reg_load(r, bases_s[0:1, bi:bi + 1])
                e.reg_add(r, r, co)
                try:
                    return e.dma_start(out=d, in_=bass.AP(h, r, ap))
                except Exception as ex:
                    print("DYN FAIL bi=", bi, "co=", co, ap, d.offset, ex)
                    raise
            P.emit("sync", fn, reads=tuple(reads) + ("bases_s",), writes=(key,), dma=True)

        MEMSET(onesf[:, :], 1.0, ("onesf",))
        MEMSET(cvals[:, 0:1], EPS, ("cvals",))
        MEMSET(cvals[:, 1:2], -0.5, ("cvals",))
        MEMSET(cvals[:, 2:3], -1.0, ("cvals",))
        MEMSET(cvals[:, 3:4], 1.0 / D, ("cvals",))
        MEMSET(cvals[:, 4:5], float(CAP), ("cvals",))
        MEMSET(ones1b[:, :], 1.0, ("ones1b",))
        MEMSET(nones1b[:, :], -1.0, ("nones1b",))
        MEMSET(ones1f[:, :], 1.0, ("ones1f",))
        MEMSET(carryc[:, :], 0.0, ("carryc",))
        P.emit("gpsimd", lambda e: e.affine_select(
            ident_f[:, :], onesf[:, 0:1].broadcast_to((128, 128)), pattern=[[1, 128]],
            base=0, channel_multiplier=-1, compare_op=AL.is_equal, fill=0.0),
            reads=("onesf",), writes=("ident_f",))
        P.emit("gpsimd", lambda e: e.affine_select(
            tri_f[:, :], onesf[:, 0:1].broadcast_to((128, 128)), pattern=[[1, 128]],
            base=0, channel_multiplier=-1, compare_op=AL.is_ge, fill=0.0),
            reads=("onesf",), writes=("tri_f",))
        P.emit("gpsimd", lambda e: e.iota(
            iota8[:, :], pattern=[[1, 8]], base=0, channel_multiplier=0,
            allow_small_or_imprecise_dtypes=True), reads=(), writes=("iota8",))
        P.emit("gpsimd", lambda e: e.iota(
            tokid[:, :], pattern=[[16, 1024]], base=0, channel_multiplier=1,
            allow_small_or_imprecise_dtypes=True), reads=(), writes=("tokid",))
        MEMSET(ARB2[0:32, 0:4096], 1.0, ("h1",))
        MEMSET(onescb[:, :], 1.0, ("onescb",))
        P.emit("gpsimd", lambda e: e.iota(
            iotar[:, :], pattern=[[1, 128]], base=0, channel_multiplier=0,
            allow_small_or_imprecise_dtypes=True), reads=(), writes=("iotar",))
        P.emit("gpsimd", lambda e: e.affine_select(
            selmat[:, :], ARB2[0:32, 0:4096], pattern=[[1, 32], [0, 128]], base=0,
            channel_multiplier=-1, compare_op=AL.is_equal, fill=0.0),
            reads=("h1",), writes=("selmat",))

        LD(consts[:, 0:16], bin_my.ap().transpose([1, 0]), "consts", eng="gpsimd")
        LD(consts[:, 56:64], bdt_my.ap().transpose([1, 0]), "consts", eng="gpsimd")
        LD(consts[:, 24:32], bout_i.ap().transpose([1, 0]), "consts", eng="gpsimd")
        LD(consts[:, 32:40], d_my.ap().transpose([1, 0]), "consts", eng="gpsimd")
        LD(consts[:, 40:48], lng_i.ap().transpose([1, 0]), "consts", eng="gpsimd")
        LD(consts[:, 48:56], lnb_i.ap().transpose([1, 0]), "consts", eng="gpsimd")
        TS(consts[:, 16:24], consts[:, 56:64], -1.0, None, AL.mult, AL.add,
           ("consts",), ("consts",))
        LD(b1_s[:, :], b1e.ap().transpose([1, 0]), "b1_s", eng="gpsimd")
        LD(b2_s[:, :], b2e.ap().transpose([1, 0]), "b2_s", eng="gpsimd")
        LD(gb_s[:, :], gb_i.ap(), "gb_s")
        LD(rbcb_s[:, :], rbc_bias.ap(), "rbcb_s")
        LD(prefm_s[:, :], prefmask.ap(), "prefm_s")
        LD(myexp_s[:, :], myexp_i.ap(), "myexp_s")
        LD(bases_s[:, :], bases_i.ap(), "bases_s")
        for k in range(8):
            LD(gwT_f[:, k * 8:(k + 1) * 8], gwT[k * 128:(k + 1) * 128, :], "gwT_f")
        for i in range(NBLK):
            LD(castA[:, i * N:(i + 1) * N], alog_my[i * 128:(i + 1) * 128, :], "castA")
        ACT(nega[:, :], castA[:, 0:NBLK * N], AF.Exp, ("castA",), ("nega",))

        for it in range(NT):
            xt = xtA if it % 2 == 0 else xtB
            kx = "xtA" if it % 2 == 0 else "xtB"
            sq = sqA if it % 2 == 0 else sqB
            ks = f"sq{it % 2}"
            st = lambda j: rstat[:, it * 8 + j: it * 8 + j + 1]
            kt = f"st{it}"
            LD(xt, xrows[it * 128:(it + 1) * 128, :], kx)
            for ln2 in range(2):
                RED(st(0), xt, AL.add, (kx,), (kt,))
                TT(sq, xt, xt, AL.mult, (kx, ks), (ks,), eng="gpsimd")
                RED(st(1), sq, AL.add, (ks,), (kt,))
                TS(st(2), st(0), 1.0 / D, None, AL.mult, AL.add, (kt,), (kt,))
                TT(st(3), st(2), st(2), AL.mult, (kt,), (kt,))
                STT(st(4), st(1), cvals[:, 3:4], st(3), AL.mult,
                    AL.subtract, (kt, "cvals"), (kt,))
                ACT(st(5), st(4), AF.Ln, (kt, "cvals"), (kt,), bias=cvals[:, 0:1])
                ACT(st(6), st(5), AF.Exp, (kt, "cvals"), (kt,), scale=cvals[:, 1:2])
                if ln2 == 0:
                    TS(xt, xt, st(2), st(6), AL.subtract, AL.mult, (kx, kt), (kx,))
                else:
                    TS(sq, xt, st(2), st(6), AL.subtract, AL.mult, (kx, kt), (ks,))
            for fb in range(8):
                cbs = castB[:, fb * 128:(fb + 1) * 128]
                kcb, kxn, kx2 = f"cB{fb}", f"xns{fb}", f"x2s{fb}"
                pcr = psC[:, (fb % 4) * 128:(fb % 4 + 1) * 128]
                kpc = f"psC{fb % 4}"
                pt2 = psT if fb % 2 == 0 else psT2
                kpt = "psT" if fb % 2 == 0 else "psT2"
                P.emit("tensor", lambda e, pcr=pcr, xt=xt, fb=fb: e.transpose(
                    pcr, xt[:, fb * 128:(fb + 1) * 128], ident_f[:, :]),
                    reads=(kx, "ident_f", kpc), writes=(kpc,))
                COPY(cbs, pcr, (kpc, kcb), (kcb,))
                SCOPY(xnstage[:, fb * 128:(fb + 1) * 128], cbs, (kcb, kxn), (kxn,))
                MM(psA[0:8, 0:128], gwT_f[:, fb * 8:(fb + 1) * 8], cbs,
                   ("gwT_f", kcb), ("psA",), start=(fb == 0), stop=(fb == 7))
                P.emit("tensor", lambda e, pt2=pt2, sq=sq, fb=fb: e.transpose(
                    pt2[:, :], sq[:, fb * 128:(fb + 1) * 128], ident_f[:, :]),
                    reads=(ks, "ident_f", kpt), writes=(kpt,))
                P.emit("vector", lambda e, fb=fb, pt2=pt2: e.tensor_scalar(
                    out=x2stage[:, fb * 128:(fb + 1) * 128], in0=pt2[:, :],
                    scalar1=consts[:, 40 + fb:41 + fb], scalar2=consts[:, 48 + fb:49 + fb],
                    op0=AL.mult, op1=AL.add),
                    reads=(kpt, "consts", kx2), writes=(kx2,))
            ST(bass.AP(ag1x, it * 128, [[TPC, 128], [128 * TPC, 8], [1, 128]]),
               xnstage[:, :], "ag1x", tuple(f"xns{fb}" for fb in range(8)))
            ST(bass.AP(ag1z, it * 128, [[TPC, 128], [128 * TPC, 8], [1, 128]]),
               x2stage[:, :], "ag1z", tuple(f"x2s{fb}" for fb in range(8)))
            TS(lg_fm[:, it * 128:(it + 1) * 128], psA[0:8, 0:128], gb_s[:, 0:1], None,
               AL.add, AL.add, ("psA", "gb_s"), ("lg_fm",))

        for it in range(NT):
            Ccol = lambda j, it=it: rcols[:, it * 10 + j: it * 10 + j + 1]
            ohs = oh_tm[:, it * 8:(it + 1) * 8]
            lgs = lg_tm[:, it * 8:(it + 1) * 8]
            P.emit("tensor", lambda e, it=it: e.transpose(
                psT[:, 0:8], lg_fm[:, it * 128:(it + 1) * 128], ident_f[0:8, 0:8]),
                reads=("lg_fm", "ident_f"), writes=("psT",))
            COPY(lgs, psT[:, 0:8], ("psT",), ("lg_tm",))
            RED(Ccol(0), lgs, AL.max, ("lg_tm",), (f"r{it}",), negate=True)
            ACT(ohs, lgs, AF.Exp, ("lg_tm", f"r{it}"), ("oh_tm",), bias=Ccol(0))
            RED(Ccol(1), ohs, AL.add, ("oh_tm",), (f"r{it}",))
            P.emit("vector", lambda e, it=it: e.reciprocal(
                rcols[:, it * 10 + 2: it * 10 + 3], rcols[:, it * 10 + 1: it * 10 + 2]),
                reads=(f"r{it}",), writes=(f"r{it}",))
            TS(Ccol(3), Ccol(0), -1.0, None, AL.mult, AL.add, (f"r{it}",), (f"r{it}",))
            TS(ohs, lgs, Ccol(3), None, AL.is_equal, AL.add,
               ("lg_tm", f"r{it}"), ("oh_tm",))
            TT(pos_tm[:, it * 8:(it + 1) * 8], ohs, iota8[:, :], AL.mult,
               ("oh_tm", "iota8"), ("pos_tm",))
            RED(Ccol(4), pos_tm[:, it * 8:(it + 1) * 8], AL.add, ("pos_tm",), (f"r{it}",))
            MM(psT2[0:8, it:it + 1], ohs, onesf[:, 0:1], ("oh_tm", "onesf"), ("psT2",),
               start=True, stop=True)
        cnts_c = moepk[0:8, 3968:3984]
        cnts16 = moepk[0:16, 3840:3848]
        mb16 = moepk[0:16, 3848:3856]
        mrow16 = moepk[0:1, 3856:3984]
        COPY(cnts_c, psT2[0:8, 0:16], ("psT2", "cnts16"), ("cnts16",))
        P.emit("tensor", lambda e: e.transpose(
            psT[0:16, 0:8], cnts_c, ident_f[0:8, 0:8]),
            reads=("cnts16", "ident_f"), writes=("psT",))
        COPY(cnts16, psT[0:16, 0:8], ("psT", "cnts16"), ("cnts16",))
        MM(ps8[0:1, 0:8], onesf[0:16, 0:1], cnts16, ("onesf", "cnts16"), ("ps8",),
           start=True, stop=True)
        COPY(rtmp8[0:1, 0:8], ps8[0:1, 0:8], ("ps8",), ("cntcol",))
        ST(cntin.ap(), rtmp8[0:1, 0:8], "cntin", ("cntcol",))

        CC("AllGather", cntin.ap().opt(), cntout.ap().opt(), ("cntin",), ("cntout",))
        LD(cnt_s[:, :], cntout.ap(), "cnt_s", reads=("cntout",))
        MM(ps8[0:1, 0:8], prefm_s[:, :], cnt_s[:, :], ("prefm_s", "cnt_s"), ("ps8",),
           start=True, stop=True)
        COPY(rtmp8[0:1, 0:8], ps8[0:1, 0:8], ("ps8",), ("baserow",))
        MM(ps8[0:128, 0:8], ones1f[0:1, 0:128], rtmp8[0:1, 0:8], ("ones1f", "baserow"),
           ("ps8",), start=True, stop=True)
        COPY(base8[:, :], ps8[0:128, 0:8], ("ps8",), ("base8",))
        MM(psT2[0:16, 0:8], tri_f[0:16, 0:16], cnts16, ("tri_f", "cnts16"), ("psT2",),
           start=True, stop=True)
        TT(mb16, psT2[0:16, 0:8], cnts16, AL.subtract, ("psT2", "cnts16", "mb16"),
           ("mb16",))
        TT(mb16, mb16, base8[0:16, 0:8], AL.add, ("mb16", "base8"), ("mb16",))
        for it in range(NT):
            LD(mrow16[0:1, it * 8:(it + 1) * 8], mb16[it:it + 1, 0:8], "mrow16",
               reads=("mb16", "mrow16"), eng="sync")

        for it in range(NT):
            Ccol = lambda j, it=it: rcols[:, it * 10 + j: it * 10 + j + 1]
            ohs = oh_tm[:, it * 8:(it + 1) * 8]
            MM(psT2[:, 0:8], tri_f[:, :], ohs, ("tri_f", "oh_tm"), ("psT2",),
               start=True, stop=True)
            COPY(pos_tm[:, it * 8:(it + 1) * 8], psT2[:, 0:8], ("psT2",), ("pos_tm",))
            MM(ps8[0:128, 0:8], ones1f[0:1, 0:128], mrow16[0:1, it * 8:(it + 1) * 8],
               ("ones1f", "mrow16"), ("ps8",), start=True, stop=True)
            TT(rtmp8[:, :], pos_tm[:, it * 8:(it + 1) * 8], ps8[0:128, 0:8], AL.add,
               ("pos_tm", "ps8"), ("rtmp8",))
            TS(rtmp8[:, :], rtmp8[:, :], 1.0, None, AL.subtract, AL.add,
               ("rtmp8",), ("rtmp8",))
            TT(rtmp8[:, :], rtmp8[:, :], ohs, AL.mult, ("rtmp8", "oh_tm"), ("rtmp8",))
            RED(Ccol(9), rtmp8[:, :], AL.add, ("rtmp8",), (f"r{it}",))
            TS(Ccol(6), Ccol(9), float(CAP), None, AL.is_lt, AL.add, (f"r{it}",), (f"r{it}",))
            TT(Ccol(5), Ccol(9), Ccol(6), AL.mult, (f"r{it}",), (f"r{it}",))
            TT(Ccol(7), Ccol(2), Ccol(6), AL.mult, (f"r{it}",), (f"r{it}",))
            P.emit("tensor", lambda e, it=it: e.transpose(
                psT[0:4, 0:128], rcols[:, it * 10 + 4: it * 10 + 8], ident_f[:, :]),
                reads=(f"r{it}", "ident_f"), writes=("psT",))
            mst = moepk[0:4, 3584 + (it % 2) * 128: 3712 + (it % 2) * 128]
            SCOPY(mst, psT[0:4, 0:128], ("psT", f"mst{it % 2}"), (f"mst{it % 2}",))
            ST(metain.ap()[0:4, it * 128:(it + 1) * 128], mst, "metain", (f"mst{it % 2}",))
        CC("AllGather", ag1z.ap().opt(), ag1zout.ap().opt(), ("ag1z",), ("ag1zout",))
        CC("AllGather", metain.ap().opt(), metaout.ap().opt(), ("metain",), ("metaout",))
        CC("AllGather", ag1x.ap().opt(), ag1xout.ap().opt(), ("ag1x",), ("ag1xout",))
        for half in range(2):
            dyn_ld(myx2.ap()[half * EH:(half + 1) * EH, :], ag1zout,
                   half, 0, [[TPC, EH], [1, TPC]], "myx2", reads=("ag1zout",))

        PRI = []
        BULK = []
        _ps = [0]

        def take_ps():
            i = _ps[0] % 2
            _ps[0] += 1
            return (psA, "psA") if i == 0 else (psC, "psC")

        def pop_fill():
            if BULK:
                BULK.pop(0)()

        def inproj_pieces(ck):
            half = ck // 4
            tcl = ck % 4
            ps_list = []

            def p_x2sl(ck=ck, half=half, tcl=tcl):
                for k in range(8):
                    LD(x2sl_v[:, k * 512:(k + 1) * 512],
                       myx2.ap()[half * EH + k * 128: half * EH + (k + 1) * 128,
                                 tcl * 512:(tcl + 1) * 512],
                       "x2sl", reads=("myx2", "x2sl"))
            ps_list.append(p_x2sl)
            for m in range(16):
                def p_m(ck=ck, m=m):
                    par = m % 2
                    LD(wpar_v[:, par * 1024:(par + 1) * 1024],
                       winsp.ap()[m * 128:(m + 1) * 128, :], f"wpar{par}",
                       reads=("winsp", f"wpar{par}"))
                    ps, pk = take_ps()
                    MMC(ps[:, :],
                        [(wpar_v[:, par * 1024 + k * 128: par * 1024 + (k + 1) * 128],
                          x2sl_v[:, k * 512:(k + 1) * 512]) for k in range(8)],
                        (f"wpar{par}", "x2sl"), (pk,))
                    ev = uev_v[:, (m % 8) * 512:(m % 8 + 1) * 512]
                    evk = f"uev{m % 8}"
                    ACT(ev, ps[:, :], AF.Silu, (pk, "consts", evk), (evk,),
                        bias=consts[:, m:m + 1])
                    if m == 7:
                        ST(bass.AP(u_sp, ck * 512, [[TB, 128], [128 * TB, 8], [1, 512]]),
                           uev_v[:, 0:4096], "u_sp", tuple(f"uev{j}" for j in range(8)))
                        MMC(ps96[:, :],
                            [(wot[:, j * 96:(j + 1) * 96], uev_v[:, j * 512:(j + 1) * 512])
                             for j in range(8)],
                            ("wot",) + tuple(f"uev{j}" for j in range(8)), ("ps96",))
                        ev96 = rbcev_v[:, (ck % 2) * 512:(ck % 2 + 1) * 512]
                        COPY(ev96, ps96[:, :], ("ps96", f"rbcev{ck % 2}"),
                             (f"rbcev{ck % 2}",))
                        ST(ag3in[ck * 96:(ck + 1) * 96, :], ev96, "ag3in",
                           (f"rbcev{ck % 2}",))
                    if m == 15:
                        ST(bass.AP(gate_sp, ck * 512, [[TB, 128], [128 * TB, 8], [1, 512]]),
                           uev_v[:, 0:4096], "gate_sp", tuple(f"uev{j}" for j in range(8)))
                ps_list.append(p_m)
            return ps_list

        def ag3cc(ck):
            CC("AllGather", ag3in.ap()[ck * 96:(ck + 1) * 96, :].opt(),
               ag3out.ap()[ck * 768:(ck + 1) * 768, :].opt(),
               ("ag3in",), (f"ag3o{ck}",))

        def rbc(ck):
            dyn_ld(AR2[0:96, 1024:2048], ag3out, 2, ck * 8 * 96 * 512,
                   [[512, 96], [96 * 512, 2], [1, 512]], "art0",
                   reads=(f"ag3o{ck}", "art0", "art1"))
            TT(art[0][0:96, :], art[0][0:96, :], art[1][0:96, :], AL.add,
               ("art0",), ("art0", "art1"))
            TS(art[0][0:96, :], art[0][0:96, :], rbcb_s[:, :], None, AL.add, AL.add,
               ("art0", "rbcb_s"), ("art0",))
            COPY(rbf2_v, art[0][0:64, :], ("art0", "rbf2"), ("rbf2",))
            COPY(bmcm_b[:, ck * 512:(ck + 1) * 512], art[0][64:96, :],
                 ("art0", "bmcm"), ("bmcm",))
            TS(bmcm_b[0:16, ck * 512:(ck + 1) * 512],
               bmcm_b[0:16, ck * 512:(ck + 1) * 512], -1.0, None, AL.mult, AL.add,
               ("bmcm",), ("bmcm",))

        def delta(ck):
            groups = [(0, 3), (3, 3), (6, 2)]
            for g0, gn in groups:
                for j in range(gn):
                    blk = g0 + j
                    ps, pk = take_ps()
                    MM(ps[:, :], wsc[0:64, blk * 128:(blk + 1) * 128], rbf2_v,
                       ("wsc", "rbf2"), (pk,), start=True, stop=True)
                    ACT(art[j], ps[:, :], AF.Sigmoid,
                        (pk, "consts", "cvals", f"art{j}"), (f"art{j}",),
                        bias=consts[:, 16 + blk:17 + blk], scale=cvals[:, 2:3])
                for j in range(gn):
                    blk = g0 + j
                    db = dsgb[j % 2]
                    kdb = f"dsgb{j % 2}"
                    ACT(db, art[j], AF.Ln, (f"art{j}", kdb), (kdb,))
                    ST(lns_sp[blk * 128:(blk + 1) * 128, ck * 512:(ck + 1) * 512], db,
                       "lns_sp", (kdb,), eng="sync")

        def gather_pieces():
            out = []

            def p_f():
                FENCE(("slabx",), ("llistbc", "xsf"))
                FENCE(("xsfo",), ())
            out.append(p_f)
            for fb in range(8):
                def p_fb(fb=fb):
                    for r8 in range(NCORE):
                        LD(slabx_v[:, r8 * TPC:(r8 + 1) * TPC],
                           ag1xout.ap()[r8 * D + fb * 128: r8 * D + (fb + 1) * 128, :],
                           f"slx{r8}", reads=("ag1xout", f"slx{r8}"))
                    for part in range(3):
                        w = 1024 if part < 2 else 512
                        P.emit("gpsimd", lambda e, part=part, w=w: e.indirect_copy(
                            xsf_v[:, part * 1024: part * 1024 + w],
                            slabx_v[:, 0:NCORE * TPC],
                            cidx[:, part * 64: part * 64 + w // 16], True),
                            reads=tuple(f"slx{r}" for r in range(NCORE)) + ("cidx", "xsfo"),
                            writes=("xsfo",))
                    ST(xs_sp[fb * 128:(fb + 1) * 128, :], xsf_v, "xs_sp", ("xsfo",))
                out.append(p_fb)
            return out

        ag4state = [False, False]

        def ffn_pieces():
            out = []

            def p_fence():
                FENCE(("xssl", "h1"), ("xsfo", "slabx"))
            out.append(p_fence)
            for tc5 in range(5):
                def p_xl(tc5=tc5):
                    for k in range(8):
                        LD(xssl_v[:, k * 512:(k + 1) * 512],
                           xs_sp.ap()[k * 128:(k + 1) * 128, tc5 * 512:(tc5 + 1) * 512],
                           "xssl", reads=("xs_sp", "xssl"))
                out.append(p_xl)
                for f in range(32):
                    def p_f(tc5=tc5, f=f):
                        par = f % 2
                        LD(w1t[:, par * 1024:(par + 1) * 1024],
                           w1bf.ap()[f * 128:(f + 1) * 128, :], f"w1t{par}",
                           reads=("w1bf", f"w1t{par}"))
                        ps, pk = take_ps()
                        MMC(ps[:, :],
                            [(w1t[:, par * 1024 + k * 128: par * 1024 + (k + 1) * 128],
                              xssl_v[:, k * 512:(k + 1) * 512]) for k in range(8)],
                            (f"w1t{par}", "xssl"), (pk,))
                        ACT(h1_v[:, f * 512:(f + 1) * 512], ps[:, :], AF.Relu,
                            (pk, "b1_s", "h1"), ("h1",), bias=b1_s[:, f:f + 1])
                    out.append(p_f)
                for d8 in range(8):
                    def p_w2ld(d8=d8):
                        LD(w2t_v[:, :], w2bf.ap()[d8 * 128:(d8 + 1) * 128, :], "w2t",
                           reads=("w2bf", "w2t"))
                    out.append(p_w2ld)
                    psd = (psB, "psB")
                    for g in range(4):
                        def p_sub(tc5=tc5, d8=d8, g=g, psd=psd):
                            ps, pk = psd
                            MMC(ps[:, :],
                                [(w2t_v[:, (g * 8 + f) * 128:(g * 8 + f + 1) * 128],
                                  h1_v[:, (g * 8 + f) * 512:(g * 8 + f + 1) * 512])
                                 for f in range(8)],
                                ("w2t", "h1"), (pk,), first=(g == 0), last=(g == 3))
                        out.append(p_sub)

                    def p_ev(tc5=tc5, d8=d8, psd=psd):
                        ps, pk = psd
                        fe = ffev_v[:, 0:512]
                        TS(fe, ps[:, :], b2_s[:, d8:d8 + 1], None, AL.add, AL.add,
                           (pk, "b2_s", "ffev0"), ("ffev0",))
                        ST(ag4in[d8 * 128:(d8 + 1) * 128, tc5 * 512:(tc5 + 1) * 512], fe,
                           "ag4in", ("ffev0",))
                    out.append(p_ev)
            return out

        def scan_q(q, mid=None):
            while PRI:
                PRI.pop(0)()
            if q == 0:
                FENCE(("bbc", "cbc"), ("lbctab",))
                FENCE(("a0", "a1", "ym0", "ym1", "h0", "h1p", "yac"), ())
                FENCE(("ygt0", "ygt1"), ())
            for n in range(N):
                MM(psC[:, :], selmat[:, n * 128:(n + 1) * 128],
                   bmcm_b[:, q * QT:(q + 1) * QT],
                   ("bmcm", "selmat"), ("psC",), start=True, stop=True)
                SCOPY(bbc_v[:, n * QT:(n + 1) * QT], psC[:, :], ("psC", "bbc"), ("bbc",))
                MM(psY[:, :], selmat[:, (16 + n) * 128:(17 + n) * 128],
                   bmcm_b[:, q * QT:(q + 1) * QT],
                   ("bmcm", "selmat"), ("psYb",), start=True, stop=True)
                SCOPY(cbc_v[:, n * QT:(n + 1) * QT], psY[:, :], ("psYb", "cbc"), ("cbc",))
            for blk in range(NBLK):
                if blk == 2 and mid is not None:
                    mid()
                if blk in (2, 4, 6):
                    npop = (len(PRI) + 2) // 3 if blk == 2 else (
                        (len(PRI) + 1) // 2 if blk == 4 else len(PRI))
                    for _ in range(npop):
                        if PRI:
                            PRI.pop(0)()
                pb = blk % 2
                lk, uk, gk, ygk = f"lns{pb}", f"ut{pb}", f"gt{pb}", f"ygt{pb}"
                lns = lns_t[:, pb * QT:(pb + 1) * QT]
                ut = u_t[:, pb * QT:(pb + 1) * QT]
                gt = gate_t[:, pb * QT:(pb + 1) * QT]
                ygs = ygt[:, pb * QT:(pb + 1) * QT]
                LD(lns, lns_sp.ap()[blk * 128:(blk + 1) * 128, q * QT:(q + 1) * QT],
                   lk, reads=("lns_sp", lk))
                LD(ut, u_sp.ap()[blk * 128:(blk + 1) * 128, q * QT:(q + 1) * QT],
                   uk, reads=("u_sp", uk))
                LD(gt, gate_sp.ap()[blk * 128:(blk + 1) * 128, q * QT:(q + 1) * QT],
                   gk, reads=("gate_sp", gk))
                TT(ndu[:, :], lns, ut, AL.mult, (lk, uk, "ndu"), ("ndu",))
                for n in range(N):
                    pn = n % 2
                    ak, hk, mk = f"a{pn}", (f"h{pn}" if pn == 0 else "h1p"), f"ym{pn}"
                    av = a_v[:, pn * 512:(pn + 1) * 512]
                    hv = h_v[:, pn * 512:(pn + 1) * 512]
                    mv = ymul_v[:, pn * 512:(pn + 1) * 512]
                    ACT(av, lns, AF.Exp, (lk, "nega", ak), (ak,),
                        scale=nega[:, blk * N + n: blk * N + n + 1])
                    TT(b_t[:, :], ndu[:, :], bbc_v[:, n * QT:(n + 1) * QT], AL.mult,
                       ("ndu", "bbc", "b_t"), ("b_t",))
                    P.emit("vector", lambda e, q=q, blk=blk, n=n, hv=hv, av=av:
                           e.tensor_tensor_scan(
                        out=hv, data0=av, data1=b_t[:, :],
                        initial=(0.0 if q == 0 else carryc[:, blk * N + n: blk * N + n + 1]),
                        op0=AL.mult, op1=AL.add),
                        reads=(ak, "b_t", hk) + (() if q == 0 else ("carryc",)),
                        writes=(hk,))
                    COPY(carryc[:, blk * N + n: blk * N + n + 1], hv[:, QT - 1:QT],
                         (hk, "carryc"), ("carryc",))
                    TT(mv, hv, cbc_v[:, n * QT:(n + 1) * QT], AL.mult,
                       (hk, "cbc", mk), (mk,))
                    if n == 0:
                        COPY(yacc_v, mv, (mk, "yac"), ("yac",))
                    else:
                        TT(yacc_v, yacc_v, mv, AL.add, ("yac", mk), ("yac",))
                    if n % 2 == 1:
                        pop_fill()
                STT(yacc_v, ut, consts[:, 32 + blk:33 + blk], yacc_v,
                    AL.mult, AL.add, (uk, "consts", "yac"), ("yac",))
                TT(ygs, yacc_v, gt, AL.mult, ("yac", gk, ygk), (ygk,))
                ST(ag5in[q * EH + blk * 128: q * EH + (blk + 1) * 128, :], ygs,
                   "ag5in", (ygk,), eng="sync")

        def ag5cc(q):
            CC("AllGather", ag5in.ap()[q * EH:(q + 1) * EH, :].opt(),
               ag5out.ap()[q * 8 * EH:(q + 1) * 8 * EH, :].opt(),
               ("ag5in",), (f"ag5o{q}",))

        FENCE(("x2sl",), ())
        FENCE(("psC",), ("psC0", "psC1", "psC2", "psC3"))
        for k in range(8):
            LD(castB[:, 0:96], wdbc_my[k * 128:(k + 1) * 128, :], "castB")
            COPY(wot[:, k * 96:(k + 1) * 96],
                 castB[:, 0:96], ("castB", "wot"), ("wot",))
        LD(castA[0:64, :], wdt_my.ap(), "castA")
        COPY(wsc[0:64, 0:1024], castA[0:64, :], ("castA", "wsc"), ("wsc",))
        FENCE(("rbcev0", "rbcev1"), ("castB",))
        FENCE(("spb0", "spb1"), ("castA",))
        FENCE(("spo0", "spo1"), ())

        _spn = [0]

        def spill_tile(src, rr, cc, dst, dstw, key):
            j = _spn[0] % 2
            _spn[0] += 1
            bsl = castA[:, j * 512:(j + 1) * 512]
            osl = (stile if j == 0 else resid).bitcast(BF16)[:, 512:1024]
            out_ap = bass.AP(dst, cc * 4 * 128 * dstw + rr * 128,
                             [[dstw, 128], [128 * dstw, 4], [1, 128]])
            kb, ko = f"spb{j}", f"spo{j}"
            LD(bsl, src[rr * 128:(rr + 1) * 128, cc * 512:(cc + 1) * 512], kb,
               reads=(kb,), eng="scalar")
            SCOPY(osl, bsl, (kb, ko), (ko,))
            ST(out_ap, osl, key, (ko,), eng="scalar")

        for rr in range(8):
            for cc in range(4):
                spill_tile(win_my.ap(), rr, cc, winsp, D, "winsp")

        def spill_pieces():
            out = []
            for rr in range(8):
                for cc in range(8):
                    out.append(lambda rr=rr, cc=cc: spill_tile(
                        w1e.ap(), rr, cc, w1bf, D, "w1bf"))
            for rr in range(32):
                for cc in range(2):
                    out.append(lambda rr=rr, cc=cc: spill_tile(
                        w2e.ap(), rr, cc, w2bf, FF, "w2bf"))
            for rr in range(16):
                for cc in range(2):
                    out.append(lambda rr=rr, cc=cc: spill_tile(
                        wout_i.ap(), rr, cc, woutbf, E, "woutbf"))
            return out

        FENCE(("diagD",), ())
        COPY(identb_v[:, 0:128], ident_f[:, :], ("ident_f", "diagD"), ("diagD",))
        for blk in range(NBLK):
            TS(diagD_v[:, blk * 128:(blk + 1) * 128], identb_v[:, 0:128],
               consts[:, 32 + blk:33 + blk], None, AL.mult, AL.add,
               ("diagD", "consts"), ("diagD",))

        for go in inproj_pieces(0):
            go()
        ag3cc(0)
        rbc(0)
        delta(0)
        for go in inproj_pieces(1):
            go()
        ag3cc(1)
        FENCE(("lbctab",), ("xtA", "xtB", "sq0", "sq1", "lg_fm"))
        FENCE(("castA",), ("spb0", "spb1"))
        FENCE(("xsf",), ("llistbc",))
        for r8 in range(NCORE):
            LD(castA[:, r8 * 16:(r8 + 1) * 16],
               bass.AP(metaout, (r8 * 4 + 0) * TPC, [[1, 128], [128, 16]]),
               "castA", reads=("metaout",))
            LD(castA[:, 128 + r8 * 16: 128 + (r8 + 1) * 16],
               bass.AP(metaout, (r8 * 4 + 2) * TPC, [[1, 128], [128, 16]]),
               "castA", reads=("metaout",))
        TS(castB[:, 0:128], castA[:, 0:128], myexp_s[:, 0:1], None, AL.is_equal, AL.add,
           ("castA", "myexp_s", "castB"), ("castB",))
        TT(castA[:, 256:384], castB[:, 0:128], castA[:, 128:256], AL.mult,
           ("castB", "castA"), ("castA",))
        MM(psT[:, :], tri_f[:, :], castA[:, 256:384], ("tri_f", "castA"), ("psT",),
           start=True, stop=True)
        COPY(castA[:, 384:512], psT[:, :], ("psT", "castA"), ("castA",))
        LD(castB[0:1, 128:256], castA[127:128, 384:512], "castB", reads=("castA",),
           eng="sync")
        P.emit("tensor", lambda e: e.transpose(
            psT2[:, 0:1], castB[0:1, 128:256], ident_f[0:1, 0:1]),
            reads=("castB", "ident_f"), writes=("psT2",))
        COPY(castB[:, 256:257], psT2[:, 0:1], ("psT2", "castB"), ("castB",))
        MM(psT2[:, 0:1], tri_f[:, :], castB[:, 256:257], ("tri_f", "castB"), ("psT2",),
           start=True, stop=True)
        COPY(castB[:, 257:258], psT2[:, 0:1], ("psT2", "castB"), ("castB",))
        TT(castB[:, 258:259], castB[:, 257:258], castB[:, 256:257], AL.subtract,
           ("castB",), ("castB",))
        P.emit("tensor", lambda e: e.transpose(
            psT2[0:1, 0:128], castB[:, 258:259], ident_f[:, :]),
            reads=("castB", "ident_f"), writes=("psT2",))
        COPY(castB[0:1, 384:512], psT2[0:1, 0:128], ("psT2", "castB"), ("castB",))
        for pp in range(16):
            LD(moepk[pp:pp + 1, 0:128], castB[0:1, 384:512], "moepk",
               reads=("castB",), eng="sync")
        acc_i = moepk[:, 1024:1184]
        acc_p = moepk[:, 1184:1344]
        cmp_t = moepk[:, 1344:1504]
        tmp_t = moepk[:, 1504:1664]
        MEMSET(acc_i, 0.0, ("moepk",))
        MEMSET(acc_p, 0.0, ("moepk",))
        for i in range(128):
            TS(cmp_t, tokid[:, 0:160], moepk[:, i:i + 1], None, AL.is_ge, AL.add,
               ("tokid", "moepk"), ("moepk",))
            TT(acc_i, acc_i, cmp_t, AL.add, ("moepk",), ("moepk",))
            TS(tmp_t, cmp_t, moepk[:, i:i + 1], None, AL.mult, AL.add,
               ("moepk",), ("moepk",))
            TT(acc_p, acc_p, tmp_t, AL.max, ("moepk",), ("moepk",))
        TS(acc_i, acc_i, 1.0, None, AL.subtract, AL.add, ("moepk",), ("moepk",))
        TT(cmp_t, tokid[:, 0:160], acc_p, AL.subtract, ("tokid", "moepk"), ("moepk",))
        TS(tmp_t, acc_i, 128.0, None, AL.mult, AL.add, ("moepk",), ("moepk",))
        TT(tmp_t, tmp_t, cmp_t, AL.add, ("moepk",), ("moepk",))
        P.emit("vector", lambda e: e.tensor_scalar_min(tmp_t, tmp_t, 16383.0),
               reads=("moepk",), writes=("moepk",))
        COPY(comp16[:, 0:160], tmp_t, ("moepk", "comp16"), ("comp16",))
        for g in range(8):
            P.emit("sync", lambda e, g=g: e.dma_start(
                out=cidx[g * 16:(g + 1) * 16, :], in_=comp16[:, :]),
                reads=("comp16",), writes=("cidx",), dma=True)
        FENCE(("llistbc",), ("h1",))
        for i in range(128):
            TS(ARB2[:, i * 128:(i + 1) * 128], iotar[:, :], castA[:, 384 + i:385 + i],
               None, AL.is_ge, AL.add, ("iotar", "castA", "llistbc"), ("llistbc",))
        for ch in range(32):
            MM(psC[0:1, 0:512], onescb[:, 0:1], ARB2[:, ch * 512:(ch + 1) * 512],
               ("onescb", "llistbc"), ("psC",), start=True, stop=True)
            COPY(castB.bitcast(BF16)[0:1, 1024:1536], psC[0:1, 0:512],
                 ("psC", "castB"), ("castB",))
            MM(psA[:, :], ones1b[0:1, :], castB.bitcast(BF16)[0:1, 1024:1536],
               ("ones1b", "castB"), ("psA",), start=True, stop=True)
            COPY(ARBIG.bitcast(BF16)[:, ch * 512:(ch + 1) * 512], psA[:, :],
                 ("psA", "lbctab"), ("lbctab",))
        mr0 = moepk[:, 128:288]
        mr1 = moepk[:, 288:448]
        idr = moepk[:, 448:608]
        for r2 in range(2):
            mr = mr0 if r2 == 0 else mr1
            TS(mr, tmp_t, float(8192 * r2), None, AL.is_ge, AL.add, ("moepk",), ("moepk",))
            if r2 == 0:
                TS(idr, tmp_t, 8192.0, None, AL.is_lt, AL.add, ("moepk",), ("moepk",))
                TT(mr, mr, idr, AL.mult, ("moepk",), ("moepk",))
            TS(idr, tmp_t, float(8192 * r2), None, AL.subtract, AL.add,
               ("moepk",), ("moepk",))
            TT(idr, idr, mr, AL.mult, ("moepk",), ("moepk",))
            COPY(comp16[:, 0:160], idr, ("moepk", "comp16"), ("comp16",))
            for g in range(8):
                P.emit("sync", lambda e, g=g: e.dma_start(
                    out=cidx[g * 16:(g + 1) * 16, :], in_=comp16[:, :]),
                    reads=("comp16",), writes=("cidx",), dma=True)
            for part in range(3):
                w = 1024 if part < 2 else 512
                P.emit("gpsimd", lambda e, part=part, w=w, r2=r2: e.indirect_copy(
                    ARB2[:, r2 * 2560 + part * 1024:
                         r2 * 2560 + part * 1024 + w],
                    ARBIG.bitcast(BF16)[:, 8192 * r2: 8192 * (r2 + 1)],
                    cidx[:, part * 64: part * 64 + w // 16], True),
                    reads=("lbctab", "cidx", "xsf"), writes=("xsf",))
            ST(lv_hbm.ap()[0:1, :], ARB2[0:1, r2 * 2560: r2 * 2560 + 2560],
               f"lv_hbm{r2}", ("xsf",), eng="gpsimd")
            LD(moepk.bitcast(BF16)[:, 7600 + r2 * 160: 7760 + r2 * 160],
               bass.AP(lv_hbm, 0, [[1, 16], [16, 160]]), f"lvint{r2}",
               reads=(f"lv_hbm{r2}", "moepk"))
        COPY(cmp_t, moepk.bitcast(BF16)[:, 7600:7760], ("moepk",), ("moepk",))
        TT(cmp_t, cmp_t, mr0, AL.mult, ("moepk",), ("moepk",))
        COPY(idr, moepk.bitcast(BF16)[:, 7760:7920], ("moepk",), ("moepk",))
        TT(idr, idr, mr1, AL.mult, ("moepk",), ("moepk",))
        TT(cmp_t, cmp_t, idr, AL.add, ("moepk",), ("moepk",))
        TS(tmp_t, acc_i, 128.0, None, AL.mult, AL.add, ("moepk",), ("moepk",))
        TT(tmp_t, tmp_t, cmp_t, AL.add, ("moepk",), ("moepk",))
        P.emit("vector", lambda e: e.tensor_scalar_min(tmp_t, tmp_t, 16383.0),
               reads=("moepk",), writes=("moepk",))
        COPY(comp16[:, 0:160], tmp_t, ("moepk", "comp16"), ("comp16",))
        for g in range(8):
            P.emit("sync", lambda e, g=g: e.dma_start(
                out=cidx[g * 16:(g + 1) * 16, :], in_=comp16[:, :]),
                reads=("comp16",), writes=("cidx",), dma=True)

        LD(midx_v[:, 0:128], bass.AP(metain, 0, [[1, 16], [16, 128]]), "moepk",
           reads=("metain",))
        LD(mkeep_v[:, 0:128], bass.AP(metain, TPC, [[1, 16], [16, 128]]), "moepk",
           reads=("metain",))
        STT(mwork_v[:, 0:128], midx_v[:, 0:128], cvals[0:16, 4:5], mkeep_v[:, 0:128],
            AL.mult, AL.add, ("moepk", "cvals"), ("moepk",))
        for r3 in range(3):
            mr = moepk[:, 608 + r3 * 128: 736 + r3 * 128]
            TS(mr, mwork_v[:, 0:128], float(8192 * r3), None, AL.is_ge, AL.add,
               ("moepk",), ("moepk",))
            if r3 < 2:
                TS(moepk[:, 992:1120] if False else midx_v[:, 128:256],
                   mwork_v[:, 0:128], float(8192 * (r3 + 1)), None, AL.is_lt, AL.add,
                   ("moepk",), ("moepk",))
                TT(mr, mr, midx_v[:, 128:256], AL.mult, ("moepk",), ("moepk",))
            TS(midx_v[:, 128:256], mwork_v[:, 0:128], float(8192 * r3), None,
               AL.subtract, AL.add, ("moepk",), ("moepk",))
            TT(midx_v[:, 128:256], midx_v[:, 128:256], mr, AL.mult,
               ("moepk",), ("moepk",))
            COPY(gidx16[0:16, r3 * 128:(r3 + 1) * 128], midx_v[:, 128:256],
                 ("moepk", "gidx16"), ("gidx16",))
            for g in range(1, 8):
                P.emit("sync", lambda e, g=g, r3=r3: e.dma_start(
                    out=gidx16[g * 16:(g + 1) * 16, r3 * 128:(r3 + 1) * 128],
                    in_=gidx16[0:16, r3 * 128:(r3 + 1) * 128]),
                    reads=("gidx16",), writes=("gidx16",), dma=True)
        gxr = rstat[:, 0:16]
        FENCE(("gxr",), tuple(f"st{it}" for it in range(NT)))
        for r3 in range(3):
            mr = moepk[:, 608 + r3 * 128: 736 + r3 * 128]
            P.emit("tensor", lambda e, mr=mr: e.transpose(
                psT[0:128, 0:16], mr, ident_f[0:16, 0:16]),
                reads=("moepk", "ident_f"), writes=("psT",))
            SCOPY(gxr, psT[0:128, 0:16], ("psT", "gxr"), ("gxr",))
            ST(bass.AP(mrow_hbm, r3 * TPC, [[16, 128], [1, 16]]), gxr,
               f"mrowh{r3}", ("gxr",), eng="gpsimd")

        FENCE(("spb0", "spb1"), ("castA",))
        BULK.extend(gather_pieces())
        BULK.extend(spill_pieces())
        BULK.extend(ffn_pieces())
        for ck in range(NQ):
            if ck + 2 < NQ:
                PRI.extend(inproj_pieces(ck + 2))

            def mid(ck=ck):
                if ck + 1 < NQ:
                    rbc(ck + 1)
                    delta(ck + 1)
            scan_q(ck, mid=mid)
            while PRI:
                PRI.pop(0)()
            if ck + 2 < NQ:
                ag3cc(ck + 2)
            if ck >= NQ - 3 and not BULK and not ag4state[0]:
                CC("AllGather", ag4in.ap().opt(), ag4out.ap().opt(),
                   ("ag4in",), ("ag4out",))
                ag4state[0] = True
            ag5cc(ck)
        while BULK:
            BULK.pop(0)()
        if not ag4state[0]:
            CC("AllGather", ag4in.ap().opt(), ag4out.ap().opt(), ("ag4in",), ("ag4out",))

        FENCE(("gvmbc",), ("a0", "a1", "ym0", "ym1", "h0", "h1p", "yac",
                           "art0", "art1", "art2"))
        FENCE(("gvbc", "graw"), ("xssl", "xsfo"))
        FENCE(("castA", "castB"), ("spb0", "spb1", "rbcev0", "rbcev1"))
        FENCE(("gvrow",), ("moepk",))
        for r3 in range(3):
            for hh in range(2):
                LD(castA[0:1, :], mrow_hbm.ap()[r3:r3 + 1, hh * 1024:(hh + 1) * 1024],
                   "castA", reads=(f"mrowh{r3}", "castA"))
                LD(castB[0:1, :], metain.ap()[3:4, hh * 1024:(hh + 1) * 1024],
                   "castB", reads=("metain", "castB"))
                TT(castA[0:1, :], castA[0:1, :], castB[0:1, :], AL.mult,
                   ("castA", "castB"), ("castA",))
                COPY(gvrow[:, hh * 1024:(hh + 1) * 1024], castA[0:1, :],
                     ("castA", "gvrow"), ("gvrow",))
            for tc in range(4):
                MM(psC[:, :], ones1b[0:1, :], gvrow[0:1, tc * 512:(tc + 1) * 512],
                   ("ones1b", "gvrow"), ("psC",), start=True, stop=True)
                COPY(gvm_bc[:, r3 * 2048 + tc * 512: r3 * 2048 + (tc + 1) * 512],
                     psC[:, :], ("psC", "gvmbc"), ("gvmbc",))
        FENCE(("gfb",), ("h1", "slabx"))
        FENCE(("gacck",), ("yac", "art2"))
        FENCE(tuple(f"sl{r8}" for r8 in range(NCORE)),
              ("bbc", "cbc", "wpar0", "wpar1", "diagD"))
        gacc2 = AR2.bitcast(BF16)[:, 6144:8192]
        for fb in range(8):
            for r8 in range(NCORE):
                LD(slab_v[:, r8 * CAP:(r8 + 1) * CAP],
                   ag4out.ap()[r8 * D + fb * 128: r8 * D + (fb + 1) * 128, :],
                   f"sl{r8}", reads=("ag4out", f"sl{r8}"))
            t2 = gfb_v[:, fb * TPC:(fb + 1) * TPC]

            for r3 in range(3):
                dst = (gacc_v, gtmp_v, t2)[r3]

                def icpair(e, dst=dst, r3=r3):
                    last = None
                    for part in range(2):
                        last = e.indirect_copy(
                            dst[:, part * 1024:(part + 1) * 1024],
                            slab_v[:, 8192 * r3: min(8192 * (r3 + 1), NCORE * CAP)],
                            gidx16[:, r3 * 128 + part * 64: r3 * 128 + (part + 1) * 64],
                            True)
                    return last
                P.emit("gpsimd", icpair,
                       reads=tuple(f"sl{r8}" for r8 in range(NCORE)) +
                             ("gidx16", f"gr{r3}") + (("gfb",) if r3 == 2 else ()),
                       writes=(f"gr{r3}",))
            TT(gacc2, gacc_v, gvm_bc[:, 0:2048], AL.mult,
               ("gr0", "gvmbc", "gacck"), ("gacck",))
            TT(gtmp_v, gtmp_v, gvm_bc[:, 2048:4096], AL.mult,
               ("gr1", "gvmbc"), ("gr1",))
            TT(gacc2, gacc2, gtmp_v, AL.add, ("gacck", "gr1"), ("gacck",))
            TT(t2, t2, gvm_bc[:, 4096:6144], AL.mult, ("gr2", "gvmbc"), ("gr2",))
            TT(gacc2, gacc2, t2, AL.add, ("gacck", "gr2"), ("gacck",))
            ACT(gfb_v[:, fb * TPC:(fb + 1) * TPC], gacc2, AF.Silu,
                ("gacck", "gfb", "gr2"), ("gfb",))

        FENCE(("ysl",), ("gvmbc", "gacck"))
        FENCE(("ottm",), ("x2sl", "w2t"))
        FENCE(("wot0", "wot1"), ("w1t0", "w1t1", "wot", "wsc"))
        FENCE(("stile", "resid", "sgt"), ("dsgb0", "dsgb1", "spo0", "spo1", "ffev0"))
        FENCE(("xrow_v",), ("gvbc", "graw"))
        for tc in range(4):
            for half in range(2):
                dyn_ld(ysl_v[:, half * 4096:(half + 1) * 4096], ag5out, 4 + half,
                       tc * 8 * EH * 512,
                       [[512, 128], [128 * 512, 8], [1, 512]], "ysl",
                       reads=(f"ag5o{tc}", f"ag5o{4 + tc}", "ysl"))
            for d8 in range(8):
                par = d8 % 2
                wbuf = wot if par == 0 else w1t
                LD(wbuf[:, 0:2048], woutbf.ap()[d8 * 128:(d8 + 1) * 128, :],
                   f"wot{par}", reads=("woutbf", f"wot{par}"))
                ps, pk = take_ps()
                MMC(ps[:, :],
                    [(wbuf[:, eb * 128:(eb + 1) * 128],
                      ysl_v[:, eb * 512:(eb + 1) * 512]) for eb in range(16)],
                    (f"wot{par}", "ysl"), (pk,))
                TS(stile[:, :], ps[:, :], consts[:, 24 + d8:25 + d8], None, AL.add, AL.add,
                   (pk, "consts", "stile"), ("stile",))
                LD(resid.bitcast(BF16)[:, 512:1024],
                   ag1x.ap()[d8 * 128:(d8 + 1) * 128, tc * 512:(tc + 1) * 512],
                   "resid", reads=("ag1x", "resid"))
                TT(stile[:, :], stile[:, :], resid.bitcast(BF16)[:, 512:1024], AL.add,
                   ("stile", "resid"), ("stile",))
                TT(sgt[:, :], stile[:, :],
                   gfb_v[:, d8 * TPC + tc * 512: d8 * TPC + (tc + 1) * 512],
                   AL.mult, ("stile", "gfb", "sgt"), ("sgt",))
                for sub in range(4):
                    pst = psT if sub % 2 == 0 else psT2
                    pstk = "psT" if sub % 2 == 0 else "psT2"
                    TRANS(pst[:, :], sgt[:, sub * 128:(sub + 1) * 128], ("sgt",), (pstk,))
                    SCOPY(ottm_v[:, sub * 1024 + d8 * 128: sub * 1024 + (d8 + 1) * 128],
                          pst[:, :], (pstk, "ottm"), ("ottm",))
            for sub in range(4):
                it = tc * 4 + sub
                LD(xrow_v, xrows[it * 128:(it + 1) * 128, :], "xrow_v",
                   reads=("xrow_v",))
                TT(ottm_v[:, sub * 1024:(sub + 1) * 1024],
                   ottm_v[:, sub * 1024:(sub + 1) * 1024], xrow_v, AL.add,
                   ("xrow_v", "ottm"), ("ottm",))
                ST(out_rows[it * 128:(it + 1) * 128, :], ottm_v[:, sub * 1024:(sub + 1) * 1024],
                   "out", ("ottm",))

        if limit is None:
            P.finalize_outputs()
        else:
            P.limit = limit

        @block.tensor
        def _(e):
            P.replay("tensor", e, sems)

        @block.vector
        def _(e):
            P.replay("vector", e, sems)

        @block.scalar
        def _(e):
            P.replay("scalar", e, sems)

        @block.gpsimd
        def _(e):
            P.replay("gpsimd", e, sems)

        @block.sync
        def _(e):
            P.replay("sync", e, sems)

    return nc


def _in_maps(inputs):
    f32 = lambda a: np.ascontiguousarray(a, dtype=np.float32)
    x = f32(inputs["L0_attn_x"]).reshape(NCORE * TPC, D)
    W_in = f32(inputs["L0_attn_W_in"])
    b_in = f32(inputs["L0_attn_b_in"])
    W_delta = f32(inputs["L0_attn_W_delta"])
    b_delta = f32(inputs["L0_attn_b_delta"])
    W_dt = f32(inputs["L0_attn_W_dt"])
    b_dt = f32(inputs["L0_attn_b_dt"])
    W_B = f32(inputs["L0_attn_W_B"])
    b_B = f32(inputs["L0_attn_b_B"])
    W_C = f32(inputs["L0_attn_W_C"])
    b_C = f32(inputs["L0_attn_b_C"])
    A_log = f32(inputs["L0_attn_A_log"])
    Dv = f32(inputs["L0_attn_D"])
    W_out = f32(inputs["L0_attn_W_out"])
    b_out = f32(inputs["L0_attn_b_out"])
    ln_g = f32(inputs["L0_attn_ln_g"])
    ln_b = f32(inputs["L0_attn_ln_b"])
    gate_w = f32(inputs["L0_ffn_gate_w"])
    gate_b = f32(inputs["L0_ffn_gate_b"])
    w1 = f32(inputs["L0_ffn_w1"])
    b1 = f32(inputs["L0_ffn_b1"])
    w2 = f32(inputs["L0_ffn_w2"])
    b2 = f32(inputs["L0_ffn_b2"])

    maps = []
    for c in range(NCORE):
        b, h = c // 2, c % 2
        sl = slice(h * EH, (h + 1) * EH)
        win = np.concatenate([W_in[:, h * EH:(h + 1) * EH],
                              W_in[:, E + h * EH: E + (h + 1) * EH]], axis=1)
        binv = np.concatenate([b_in[h * EH:(h + 1) * EH],
                               b_in[E + h * EH: E + (h + 1) * EH]]).reshape(16, 128)
        wdbc = np.concatenate([W_delta[sl], W_B[sl], W_C[sl]], axis=1)
        rbcb = np.concatenate([b_delta, b_B, b_C]).reshape(96, 1)
        bases = np.array([[
            (2 * b) * D * TPC,
            (2 * b + 1) * D * TPC,
            (2 * b) * 96 * 512,
            (2 * b + 1) * 96 * 512,
            4 * h * 8 * EH * 512 + (2 * b) * EH * 512,
            4 * h * 8 * EH * 512 + (2 * b + 1) * EH * 512,
            0, 0]], dtype=np.int32)
        maps.append({
            "xrows": x[c * TPC:(c + 1) * TPC],
            "win_my": win,
            "bin_my": binv,
            "bdt_my": b_dt[sl].reshape(NBLK, 128),
            "wdbc_my": wdbc,
            "rbc_bias": rbcb,
            "wdt_my": np.ascontiguousarray(W_dt[:, sl]),
            "alog_my": A_log[sl],
            "d_my": Dv[sl].reshape(NBLK, 128),
            "wout_i": W_out,
            "bout_i": b_out.reshape(NBLK, 128),
            "lng_i": ln_g.reshape(NBLK, 128),
            "lnb_i": ln_b.reshape(NBLK, 128),
            "gwT": np.ascontiguousarray(gate_w.T),
            "gb_i": gate_b.reshape(NE, 1),
            "w1e": w1[c],
            "w2e": w2[c],
            "b1e": b1[c].reshape(32, 128),
            "b2e": b2[c].reshape(NBLK, 128),
            "prefmask": (np.arange(NE) < c).astype(np.float32).reshape(NE, 1),
            "myexp_i": np.full((128, 1), c, np.float32),
            "bases_i": bases,
        })
    return maps


_NC_CACHE = {}


def kernel(**inputs):
    if "nc" not in _NC_CACHE:
        _NC_CACHE["nc"] = build_nc()
    nc = _NC_CACHE["nc"]
    maps = _in_maps(inputs)
    res = run_bass_kernel_spmd(nc, maps, core_ids=list(range(NCORE)))
    out = np.concatenate([r["out_rows"] for r in res.results], axis=0)
    return out.reshape(4, 4096, 1024)
